# version 16
# baseline (speedup 1.0000x reference)
from contextlib import ExitStack

import numpy as np
import ml_dtypes
import concourse.bass as bass
import concourse.mybir as mybir
from concourse.bass_utils import run_bass_kernel_spmd

F32 = mybir.dt.float32
BF16 = mybir.dt.bfloat16
AF = mybir.ActivationFunctionType
ALU = mybir.AluOpType
AX = mybir.AxisListType

B, T, D, H, DH, C6 = 4, 4096, 1024, 16, 64, 6
NKT = 32
NSLOT = 16
NG = 4
HFF = 1408
NE = 8
NHC = 11
SCALE = float(6.0 ** -0.5)
PAIRS = [(0, 1), (0, 2), (0, 3), (1, 2), (1, 3), (2, 3)]
JSIGN = [1.0, -1.0, 1.0, 1.0, -1.0, 1.0]
TILES_A = [0, 3, 4, 7, 8, 11, 12, 15, 16, 19, 20, 23, 24, 27, 28, 31]
TILES_B = [1, 2, 5, 6, 9, 10, 13, 14, 17, 18, 21, 22, 25, 26, 29, 30]
ENGINES = ("sync", "tensor", "vector", "scalar", "gpsimd")
_PHASES = 3
_SKIP_Q = False


def kmax(g):
    return 8 * g + 8


class Prog:

    def __init__(self, cnt=None):
        self.ops = {e: [] for e in ENGINES}
        self.cnt = dict(cnt) if cnt else {}

    def val(self, sem):
        return self.cnt.get(sem, 0)

    def emit(self, eng, fn, waits=(), inc=None, inc_n=1):
        w = [(s, v) for (s, v) in waits if v > 0]
        self.ops[eng].append((fn, w, inc, inc_n))
        if inc is not None:
            self.cnt[inc] = self.cnt.get(inc, 0) + inc_n

    def replay(self, eng_name, handle, sems):
        for fn, waits, inc, inc_n in self.ops[eng_name]:
            for s, v in waits:
                handle.wait_ge(sems[s], v)
            ins = fn(handle)
            if inc is not None:
                if ins is None:
                    raise RuntimeError(f"op with inc {inc} returned no instruction")
                ins.then_inc(sems[inc], inc_n)

    def run_block(self, nc, sems):
        with nc.Block() as blk:
            blk.sync(lambda e: self.replay("sync", e, sems))
            blk.tensor(lambda e: self.replay("tensor", e, sems))
            blk.vector(lambda e: self.replay("vector", e, sems))
            blk.scalar(lambda e: self.replay("scalar", e, sems))
            blk.gpsimd(lambda e: self.replay("gpsimd", e, sems))


def build_program(debug=0, phases=3):
    nc = bass.Bass("TRN2", target_bir_lowering=False)

    x_kv = nc.dram_tensor("x_kv", [T, D], F32, kind="ExternalInput").ap()
    xq_h = nc.dram_tensor("xq", [NSLOT * 128, D], F32, kind="ExternalInput").ap()
    xqp_h = nc.dram_tensor("xq_prev", [NSLOT * 128, D], F32, kind="ExternalInput").ap()
    masks_h = nc.dram_tensor("masks", [128, NSLOT, 2, 128], BF16, kind="ExternalInput").ap()
    ident_h = nc.dram_tensor("ident_hbm", [128, 128], F32, kind="ExternalInput").ap()
    w1q_h = nc.dram_tensor("W1_q", [2 * D, 64], F32, kind="ExternalInput").ap()
    w2q_h = nc.dram_tensor("W2_q", [2 * D, 64], F32, kind="ExternalInput").ap()
    w1k_h = nc.dram_tensor("W1_k", [D, 64], F32, kind="ExternalInput").ap()
    w2k_h = nc.dram_tensor("W2_k", [D, 64], F32, kind="ExternalInput").ap()
    wv_h = nc.dram_tensor("W_v", [D, D], F32, kind="ExternalInput").ap()
    bvb_h = nc.dram_tensor("b_v_b", [128, D], F32, kind="ExternalInput").ap()
    wo_h = nc.dram_tensor("W_o", [D, D], F32, kind="ExternalInput").ap()
    bob_h = nc.dram_tensor("b_o_b", [128, D], F32, kind="ExternalInput").ap()
    wr_h = nc.dram_tensor("w_router", [D, NE], F32, kind="ExternalInput").ap()
    wg_h = nc.dram_tensor("w_gate", [NE, D, HFF], F32, kind="ExternalInput").ap()
    wu_h = nc.dram_tensor("w_up", [NE, D, HFF], F32, kind="ExternalInput").ap()
    wd_h = nc.dram_tensor("w_down", [NE, HFF, D], F32, kind="ExternalInput").ap()
    out_h = nc.dram_tensor("out_q", [NSLOT * 128, D], F32, kind="ExternalOutput").ap()

    dbg = {}
    if debug >= 1:
        dbg["vs"] = nc.dram_tensor("dbg_vs", [128, NKT, H, 65], F32, kind="ExternalOutput").ap()
        dbg["jlkt"] = nc.dram_tensor("dbg_jlkt", [128, 4, NKT * 128], F32, kind="ExternalOutput").ap()
        dbg["lqt"] = nc.dram_tensor("dbg_lqt", [128, 4, NSLOT * 128], F32, kind="ExternalOutput").ap()
    if debug >= 3:
        dbg["kk"] = nc.dram_tensor("dbg_kk", [128, 128], F32, kind="ExternalOutput").ap()
        dbg["lraw"] = nc.dram_tensor("dbg_lraw", [128, H, C6], F32, kind="ExternalOutput").ap()
        dbg["ssb"] = nc.dram_tensor("dbg_ssb", [128, H], F32, kind="ExternalOutput").ap()
        dbg["rnrm"] = nc.dram_tensor("dbg_rnrm", [128, H], F32, kind="ExternalOutput").ap()
        dbg["lsc"] = nc.dram_tensor("dbg_lsc", [128, H, C6], F32, kind="ExternalOutput").ap()
    if debug >= 2:
        dbg["acc"] = nc.dram_tensor("dbg_acc", [128, NSLOT, D], F32, kind="ExternalOutput").ap()
        dbg["comb"] = nc.dram_tensor("dbg_comb", [128, NSLOT, NE], F32, kind="ExternalOutput").ap()
        dbg["attnT"] = nc.dram_tensor("dbg_attnT", [128, 8, 512], F32, kind="ExternalOutput").ap()

    sem_names = [
        "wA", "z0", "x0", "x1", "cast", "tp", "xt", "mmv", "ss", "rr", "lk", "tk", "jl",
        "bar1", "wB", "mk0", "mk1", "sc", "exp", "msk", "av", "sr", "rbm", "rbf", "rbr",
        "ate", "wo", "xqd", "y1a", "v1", "sd1", "xf", "tpf", "xft", "rt",
        "rmax", "rexp", "cmb", "bar2", "wg", "wu", "wd", "gmm", "umm", "sg",
        "hm", "dmm", "acc", "ln2", "sd2", "eo", "out", "dbgs",
    ]

    with ExitStack() as _st0:
        sb = lambda *a: _st0.enter_context(nc.sbuf_tensor(*a))
        ident = sb("ident", [128, 128], BF16)
        acc_t = sb("acc", [128, NSLOT, D], BF16)
        comb_t = sb("comb", [128, NSLOT, NE], F32)
        rtr_f = sb("rtr_f", [128, 8, NE], F32)
        rtr_hi = sb("rtr_hi", [128, 8, NE], BF16)
        rtr_lo = sb("rtr_lo", [128, 8, NE], BF16)
        epsb = sb("epsb", [128, 1], F32)
        sems_ctx = [nc.semaphore(n) for n in sem_names]
        sems = dict(zip(sem_names, (c.__enter__() for c in sems_ctx)))

        with ExitStack() as _st1:
            sb1 = lambda *a: _st1.enter_context(nc.sbuf_tensor(*a))
            Vs = sb1("Vs", [128, NKT, H, 65], BF16)
            JLKT = sb1("JLKT", [128, 4, NKT * 128], BF16)
            LQT = sb1("LQT", [128, 4, NSLOT * 128], BF16)
            with ExitStack() as _stA:
                sbA = lambda *a: _stA.enter_context(nc.sbuf_tensor(*a))
                psA = lambda *a: _stA.enter_context(nc.psum_tensor(*a))
                wvks = sbA("wvks", [128, 8, 1152], BF16)
                wq_sb = sbA("wq_sb", [128, 16, 128], BF16)
                bvb = sbA("bvb", [128, D], F32)
                xa0 = sbA("xa0", [128, D], F32)
                xa1 = sbA("xa1", [128, D], F32)
                xab0 = sbA("xab0", [128, D], BF16)
                xab1 = sbA("xab1", [128, D], BF16)
                xT0 = sbA("xT0", [128, 8, 128], BF16)
                xT1 = sbA("xT1", [128, 8, 128], BF16)
                xTP0 = sbA("xTP0", [128, 8, 128], BF16)
                xTP1 = sbA("xTP1", [128, 8, 128], BF16)
                Lraw0 = sbA("Lraw0", [128, H, C6], F32)
                Lraw1 = sbA("Lraw1", [128, H, C6], F32)
                Lsq = sbA("Lsq", [128, H, C6], F32)
                prodAll = sbA("prodAll", [128, 12, H], F32)
                sst = sbA("sst", [128, 3, H], F32)
                ssb = sbA("ssb", [128, H], F32)
                rnrm0 = sbA("rnrm0", [128, H], F32)
                rnrm1 = sbA("rnrm1", [128, H], F32)
                nrnrm = sbA("nrnrm", [128, H], F32)
                rln = sbA("rln", [128, H], F32)
                Lsc = sbA("Lsc", [128, H, C6], BF16)
                kk_sb = sbA("kk_sb", [128, 128], F32)
                zb = sbA("zb", [128, 128], BF16)
                pT = psA("pT", [128, 8, 128], BF16)
                pTk = psA("pTk", [128, 4, 128], BF16)
                pV0 = psA("pV0", [128, 1152], F32)
                pV1 = psA("pV1", [128, 1152], F32)
                P = Prog()
                xa, xab, xT, xTP = [xa0, xa1], [xab0, xab1], [xT0, xT1], [xTP0, xTP1]
                pV = [pV0, pV1]

                P.emit("gpsimd", lambda g: g.dma_start(out=ident[:], in_=ident_h), inc="wA", inc_n=16)
                P.emit("gpsimd", lambda g: g.dma_start(
                    out=wvks[:, :, 0:1024], in_=wv_h.rearrange("(dc p) n -> p dc n", p=128)), inc="wA", inc_n=16)
                P.emit("gpsimd", lambda g: g.dma_start(
                    out=wvks[:, :, 1024:1088], in_=w1k_h.rearrange("(dc p) n -> p dc n", p=128)), inc="wA", inc_n=16)
                P.emit("gpsimd", lambda g: g.dma_start(
                    out=wvks[:, :, 1088:1152], in_=w2k_h.rearrange("(dc p) n -> p dc n", p=128)), inc="wA", inc_n=16)
                P.emit("gpsimd", lambda g: g.dma_start(
                    out=wq_sb[:, :, 0:64], in_=w1q_h.rearrange("(kc p) n -> p kc n", p=128)), inc="wA", inc_n=16)
                P.emit("gpsimd", lambda g: g.dma_start(
                    out=wq_sb[:, :, 64:128], in_=w2q_h.rearrange("(kc p) n -> p kc n", p=128)), inc="wA", inc_n=16)
                P.emit("gpsimd", lambda g: g.dma_start(out=bvb[:], in_=bvb_h), inc="wA", inc_n=16)
                P.emit("gpsimd", lambda g: g.dma_start(
                    out=rtr_f[:], in_=wr_h.rearrange("(dc p) n -> p dc n", p=128)), inc="wA", inc_n=16)
                W_SETUP = P.val("wA")

                P.emit("vector", lambda v: v.memset(epsb[:], 1e-5))
                P.emit("vector", lambda v: v.drain())
                P.emit("vector", lambda v: v.memset(zb[:], 0.0))
                P.emit("vector", lambda v: v.drain(), inc="z0")
                def ztps(te):
                    ins = None
                    for m in range(4):
                        ins = te.transpose(pTk[:, m, :], zb[:], ident[:])
                    return ins
                P.emit("tensor", ztps, waits=[("z0", 1), ("wA", W_SETUP)], inc="tk")
                P.emit("vector", lambda v: v.tensor_copy(rtr_hi[:], rtr_f[:]), waits=[("wA", W_SETUP)])
                P.emit("vector", lambda v: v.drain())
                P.emit("vector", lambda v: v.tensor_tensor(rtr_lo[:], rtr_f[:], rtr_hi[:], ALU.subtract))
                P.emit("vector", lambda v: v.drain())


                xt_count = {}

                def tile_prep(idx, src_ap, row0, xT_list, xT_slot, xT_free):
                    b = idx % 2
                    xs = "x%d" % b
                    P.emit("sync", lambda sy, s=src_ap, r=row0, bb=b: sy.dma_start(
                        out=xa[bb][:], in_=s[r:r + 128, :]),
                        waits=[("cast", idx - 1)], inc=xs, inc_n=16)
                    P.emit("vector", lambda v, bb=b: v.tensor_copy(xab[bb][:], xa[bb][:]),
                           waits=[(xs, 16 * (idx // 2 + 1)), ("tp", idx - 1)])
                    P.emit("vector", lambda v: v.drain(), inc="cast")
                    def tps(te, bb=b):
                        ins = None
                        for dc in range(8):
                            ins = te.transpose(pT[:, dc, :], xab[bb][:, dc * 128:(dc + 1) * 128], ident[:])
                        return ins
                    P.emit("tensor", tps, waits=[("cast", idx + 1), ("xt", idx)], inc="tp")
                    P.emit("vector", lambda v, sl=xT_slot: v.tensor_copy(xT_list[sl][:], pT[:]),
                           waits=[("tp", idx + 1)] + list(xT_free))
                    P.emit("vector", lambda v: v.drain(), inc="xt")
                    xt_count[idx] = P.val("xt")

                Lraw = [Lraw0, Lraw1]
                rnrm = [rnrm0, rnrm1]
                ext_state = {}
                ext_serial = [0]

                def ext_front(pv_t, off0, jside, dest, col0):
                    se = ext_serial[0]
                    ext_serial[0] += 1
                    lb = se % 2
                    P.emit("vector", lambda v: v.tensor_copy(kk_sb[:], pv_t[:, off0:off0 + 128]),
                           waits=[("mmv", P.val("mmv"))])
                    P.emit("vector", lambda v: v.drain())
                    o1, o2 = 0, 64
                    for c, (i, j) in enumerate(PAIRS):
                        P.emit("vector", lambda v, i=i, j=j, c=c: v.tensor_mul(
                            prodAll[:, 2 * c, :], kk_sb[:, o1 + i: o1 + 64: 4], kk_sb[:, o2 + j: o2 + 64: 4]))
                        P.emit("vector", lambda v, i=i, j=j, c=c: v.tensor_mul(
                            prodAll[:, 2 * c + 1, :], kk_sb[:, o1 + j: o1 + 64: 4], kk_sb[:, o2 + i: o2 + 64: 4]))
                    P.emit("vector", lambda v: v.drain())
                    for c in range(C6):
                        P.emit("vector", lambda v, c=c, lb=lb: v.tensor_tensor(
                            Lraw[lb][:, :, c], prodAll[:, 2 * c, :], prodAll[:, 2 * c + 1, :], ALU.subtract))
                    P.emit("vector", lambda v: v.drain())
                    P.emit("vector", lambda v, lb=lb: v.tensor_tensor(Lsq[:], Lraw[lb][:], Lraw[lb][:], ALU.mult),
                           waits=[("rr", P.val("rr"))])
                    P.emit("vector", lambda v: v.drain())
                    for k in range(3):
                        P.emit("vector", lambda v, k=k: v.tensor_add(
                            sst[:, k, :], Lsq[:, :, 2 * k], Lsq[:, :, 2 * k + 1]))
                    P.emit("vector", lambda v: v.drain())
                    P.emit("vector", lambda v: v.tensor_add(sst[:, 0, :], sst[:, 0, :], sst[:, 1, :]))
                    P.emit("vector", lambda v: v.drain())
                    P.emit("vector", lambda v: v.tensor_add(sst[:, 1, :], sst[:, 0, :], sst[:, 2, :]))
                    P.emit("vector", lambda v: v.drain())
                    P.emit("vector", lambda v: v.tensor_scalar_max(ssb[:], sst[:, 1, :], 1e-24))
                    P.emit("vector", lambda v: v.drain(), inc="ss")
                    P.emit("scalar", lambda s: s.activation(rln[:], ssb[:], AF.Ln),
                           waits=[("ss", P.val("ss")), ("lk", P.val("lk"))])
                    P.emit("scalar", lambda s: s.drain())
                    P.emit("scalar", lambda s, lb=lb: s.activation(rnrm[lb][:], rln[:], AF.Exp, bias=0.0, scale=-0.5))
                    P.emit("scalar", lambda s: s.drain(), inc="rr")
                    ext_state[se] = (jside, dest, col0, lb)

                def ext_back(se):
                    jside, dest, col0, lb = ext_state.pop(se)
                    P.emit("vector", lambda v, lb=lb: v.tensor_scalar_mul(nrnrm[:], rnrm[lb][:], -1.0),
                           waits=[("rr", se + 1), ("tk", P.val("tk"))])
                    P.emit("vector", lambda v: v.drain())
                    for c in range(C6):
                        if jside:
                            oc = 5 - c
                            m = rnrm[lb] if JSIGN[oc] > 0 else nrnrm
                            P.emit("vector", lambda v, c=c, oc=oc, m=m, lb=lb: v.tensor_tensor(
                                Lsc[:, :, oc], Lraw[lb][:, :, c], m[:], ALU.mult))
                        else:
                            P.emit("vector", lambda v, c=c, lb=lb: v.tensor_tensor(
                                Lsc[:, :, c], Lraw[lb][:, :, c], rnrm[lb][:], ALU.mult))
                    P.emit("vector", lambda v: v.drain(), inc="lk")
                    head_transposes(dest, col0)

                def head_transposes(dest, col0):
                    def tps(te):
                        ins = None
                        for h in range(H):
                            g_, m = h % 4, h // 4
                            ins = te.transpose(
                                pTk[32 * g_:32 * g_ + 6, m, :], Lsc[:, h, :], ident[:],
                                tile_position=(0, 32 * g_))
                        return ins
                    P.emit("tensor", tps, waits=[("lk", P.val("lk")), ("jl", P.val("jl"))], inc="tk")
                    P.emit("vector", lambda v: v.tensor_copy(dest[:, :, col0:col0 + 128], pTk[:]),
                           waits=[("tk", P.val("tk"))])
                    P.emit("vector", lambda v: v.drain(), inc="jl")

                def emit_vmm(kt):
                    b = kt % 2
                    def vmm(te, b=b):
                        ins = None
                        for dc in range(8):
                            for (n0, n1) in ((0, 512), (512, 1024), (1024, 1152)):
                                ins = te.matmul(pV[b][:, n0:n1], xT[b][:, dc, :], wvks[:, dc, n0:n1],
                                                start=(dc == 0), stop=(dc == 7))
                        return ins
                    P.emit("tensor", vmm,
                           waits=[("xt", xt_count[kt]), ("ss", kt - 1), ("wA", W_SETUP)], inc="mmv")
                    P.emit("vector", lambda v, b=b, kt=kt: v.tensor_tensor(
                        Vs[:, kt, :, 0:64], pV[b][:, 0:1024], bvb[:], ALU.add),
                        waits=[("mmv", P.val("mmv"))])
                    P.emit("vector", lambda v: v.drain())

                tile_prep(0, x_kv, 0, xT, 0, [("mmv", -1)])
                for kt in range(NKT):
                    if kt + 1 < NKT:
                        tile_prep(kt + 1, x_kv, (kt + 1) * 128, xT, (kt + 1) % 2,
                                  [("mmv", kt)])
                    emit_vmm(kt)
                    ext_front(pV[kt % 2], 1024, True, JLKT, kt * 128)
                    if kt >= 1:
                        ext_back(kt - 1)
                P.emit("vector", lambda v: v.memset(Vs[:, :, :, 64:65], 1.0))
                P.emit("vector", lambda v: v.drain(), inc="jl")

                def emit_qmm(s_):
                    sb_ = s_ % 2
                    pvb = pV[s_ % 2]
                    def qmm(te, sb_=sb_, pvb=pvb):
                        ins = None
                        for kc in range(16):
                            lhs = xT[sb_][:, kc, :] if kc < 8 else xTP[sb_][:, kc - 8, :]
                            ins = te.matmul(pvb[:, 0:128], lhs, wq_sb[:, kc, :],
                                            start=(kc == 0), stop=(kc == 15))
                        return ins
                    P.emit("tensor", qmm,
                           waits=[("xt", xt_count[NKT + 2 * s_ + 1]), ("ss", NKT + s_ - 1), ("wA", W_SETUP)], inc="mmv")

                tile_prep(NKT, xq_h, 0, xT, 0, [("mmv", NKT - 1)])
                tile_prep(NKT + 1, xqp_h, 0, xTP, 0, [("mmv", NKT - 1)])
                for s_ in range(NSLOT):
                    if s_ + 1 < NSLOT:
                        i0 = NKT + 2 * (s_ + 1)
                        tile_prep(i0, xq_h, (s_ + 1) * 128, xT, (s_ + 1) % 2, [("mmv", NKT + s_)])
                        tile_prep(i0 + 1, xqp_h, (s_ + 1) * 128, xTP, (s_ + 1) % 2, [("mmv", NKT + s_)])
                    emit_qmm(s_)
                    ext_front(pV[s_ % 2], 0, False, LQT, s_ * 128)
                    ext_back(NKT + s_ - 1)
                ext_back(NKT + NSLOT - 1)

                if debug >= 3:
                    JL_D3 = P.val("jl")
                    P.emit("gpsimd", lambda g: g.dma_start(out=dbg["kk"], in_=kk_sb[:]),
                           waits=[("jl", JL_D3)], inc="dbgs", inc_n=16)
                    P.emit("gpsimd", lambda g: g.dma_start(out=dbg["lraw"], in_=Lraw[:]), inc="dbgs", inc_n=16)
                    P.emit("gpsimd", lambda g: g.dma_start(out=dbg["ssb"], in_=ssb[:]), inc="dbgs", inc_n=16)
                    P.emit("gpsimd", lambda g: g.dma_start(out=dbg["rnrm"], in_=rnrm[:]), inc="dbgs", inc_n=16)
                    P.emit("gpsimd", lambda g: g.dma_start(out=dbg["lsc"], in_=Lsc[:]), inc="dbgs", inc_n=16)
                if debug >= 1:
                    JL_D = P.val("jl")
                    P.emit("gpsimd", lambda g: g.dma_start(out=dbg["vs"], in_=Vs[:]),
                           waits=[("jl", JL_D)], inc="dbgs", inc_n=16)
                    P.emit("gpsimd", lambda g: g.dma_start(out=dbg["jlkt"], in_=JLKT[:]), inc="dbgs", inc_n=16)
                    P.emit("gpsimd", lambda g: g.dma_start(out=dbg["lqt"], in_=LQT[:]), inc="dbgs", inc_n=16)

                JL_ALL = P.val("jl")
                P.emit("sync", lambda sy: sy.sem_inc(sems["bar1"], 1), waits=[("x0", P.val("x0")), ("x1", P.val("x1"))])
                P.emit("tensor", lambda te: te.sem_inc(sems["bar1"], 1))
                P.emit("vector", lambda v: v.sem_inc(sems["bar1"], 1), waits=[("jl", JL_ALL)])
                P.emit("scalar", lambda s: s.sem_inc(sems["bar1"], 1), waits=[("rr", P.val("rr"))])
                P.emit("gpsimd", lambda g: g.sem_inc(sems["bar1"], 1),
                       waits=[("wA", W_SETUP), ("dbgs", P.val("dbgs"))])
                P.cnt["bar1"] = 5
                P.run_block(nc, sems)
                cnt1 = P.cnt

            if phases < 2:
                for c in sems_ctx:
                    c.__exit__(None, None, None)
                return nc
            with ExitStack() as _stB:
                sbB = lambda *a: _stB.enter_context(nc.sbuf_tensor(*a))
                psB = lambda *a: _stB.enter_context(nc.psum_tensor(*a))
                masks_g0 = sbB("masks_g0", [128, 4, 2, 128], BF16)
                attnT = sbB("attnT", [128, 8, 512], BF16)
                PT0 = sbB("PT0", [128, 512], BF16)
                PT1 = sbB("PT1", [128, 512], BF16)
                PT2 = sbB("PT2", [128, 512], BF16)
                PT3 = sbB("PT3", [128, 512], BF16)
                
                wo_sb = sbB("wo_sb", [128, 8, 1024], BF16)
                bob = sbB("bob", [128, D], F32)
                srow_b = sbB("srow_b", [1, 512], BF16)
                ones1b = sbB("ones1b", [1, 64], BF16)
                
                rbl_t = sbB("rbl", [64, 512], F32)
                rbr0 = sbB("rbr0", [64, 512], F32)
                rbr1 = sbB("rbr1", [64, 512], F32)
                xqr_t = sbB("xqr", [128, D], F32)
                y1a_t = sbB("y1a", [128, D], F32)
                y1b_t = sbB("y1b", [128, D], F32)
                xlo_t = sbB("xlo", [128, D], BF16)
                xfT_t = sbB("xfT_t", [128, 8, 128], BF16)
                xfTlo_t = sbB("xfTlo_t", [128, 8, 128], BF16)
                stq = sbB("stq", [128, 4], F32)
                rstd1 = sbB("rstd1", [128, 1], F32)
                rln1 = sbB("rln1", [128, 1], F32)
                e8_t = sbB("e8", [128, NE], F32)
                nmx_t = sbB("nmx", [128, 1], F32)
                s8_t = sbB("s8", [128, 1], F32)
                scr8_t = sbB("scr8", [128, NE], F32)
                st8_t = sbB("st8", [128, 8], F32)
                s2t_t = sbB("s2t", [128, NE], F32)
                mask8_t = sbB("mask8", [128, NE], F32)
                
                pS0 = psB("pS0", [128, 512], F32)
                pS1 = psB("pS1", [128, 512], F32)
                pS2 = psB("pS2", [128, 512], F32)
                pAV0 = psB("pAV0", [65, 512], F32)
                pAV1 = psB("pAV1", [65, 512], F32)
                
                pWO0 = psB("pWO0", [128, 1024], F32)
                
                pT2 = psB("pT2", [128, 8, 128], BF16)
                P = Prog(cnt1)
                PT = [PT0, PT1, PT2, PT3]
                pS, pAV = [pS0, pS1, pS2], [pAV0, pAV1]
                pWO = pWO0
                rbr = [rbr0, rbr1]
                masks_g = masks_g0

                P.emit("gpsimd", lambda g: g.dma_start(
                    out=wo_sb[:], in_=wo_h.rearrange("(dc p) n -> p dc n", p=128)), inc="wB", inc_n=16)
                P.emit("gpsimd", lambda g: g.dma_start(out=bob[:], in_=bob_h), inc="wB", inc_n=16)
                W_B = P.val("wB")
                P.emit("vector", lambda v: v.memset(ones1b[:], 1.0), waits=[("bar1", 5)])
                P.emit("vector", lambda v: v.drain(), inc="xt")
                ONES_RDY = P.val("xt")

                sbank_free = [None, None, None]
                ptslot_free = [None, None, None, None]
                avbank_free = [None, None]
                exp_serial = [0]
                wo_prev_g = 0

                def emit_score(g, h, kc):
                    gr, m = h % 4, h // 4
                    jlo = max(4 * g, kc // 2)
                    off = (jlo - 4 * g) * 128
                    sb = exp_serial[0] % 3
                    pslot = exp_serial[0] % 4
                    exp_serial[0] += 1
                    waits = []
                    if sbank_free[sb]:
                        waits.append(sbank_free[sb])
                    if h <= 1 and g == 0 and kc < 2:
                        waits.append(("jl", JL_ALL))
                    def scmm(te, sb=sb, off=off, gr=gr, m=m, kc=kc, g=g):
                        return te.matmul(
                            pS[sb][:, off:512],
                            JLKT[32 * gr:32 * gr + 6, m, kc * 128:(kc + 1) * 128],
                            LQT[32 * gr:32 * gr + 6, m, g * 512 + off:(g + 1) * 512],
                            start=True, stop=True, tile_position=(32 * gr, 0))
                    P.emit("tensor", scmm, waits=waits, inc="sc")
                    ncols = 512 - off
                    ew = [("sc", P.val("sc"))]
                    if ptslot_free[pslot]:
                        ew.append(ptslot_free[pslot])
                    P.emit("scalar", lambda s, sb=sb, off=off, pslot=pslot, ncols=ncols: s.activation(
                        PT[pslot][:, 0:ncols], pS[sb][:, off:512], AF.Exp, bias=0.0, scale=SCALE),
                        waits=ew, inc="exp")
                    sbank_free[sb] = ("exp", P.val("exp"))
                    masked = kc >= 8 * g
                    if masked:
                        jj, ii = kc // 2 - 4 * g, kc % 2
                        P.emit("vector", lambda v, pslot=pslot, jj=jj, ii=ii, g=g: v.tensor_mul(
                            PT[pslot][:, 0:128], PT[pslot][:, 0:128], masks_g[:, jj, ii, :]),
                            waits=[("exp", P.val("exp")), MSK_RDY[0]])
                        P.emit("vector", lambda v: v.drain(), inc="msk")
                    return (pslot, off, ncols, masked, P.val("exp"), P.val("msk"))

                def emit_av(g, h, kc, sinfo):
                    pslot, off, ncols, masked, expv, mskv = sinfo
                    hb = h % 2
                    K = kmax(g)
                    avw = [("msk", mskv)] if masked else [("exp", expv)]
                    if kc == 0 and avbank_free[hb]:
                        avw.append(avbank_free[hb])
                    def avmm(te, hb=hb, off=off, kc=kc, h=h, ncols=ncols, K=K, pslot=pslot):
                        return te.matmul(
                            pAV[hb][:, off:512], Vs[:, kc, h, :], PT[pslot][:, 0:ncols],
                            start=(kc == 0), stop=(kc == K - 1))
                    P.emit("tensor", avmm, waits=avw, inc="av")
                    ptslot_free[pslot] = ("av", P.val("av"))

                for g in range(NG):
                    av_counts, sr_vals, rbm_vals, rbr_vals = {}, {}, {}, {}
                    last_rbr = P.val("rbr")
                    P.emit("gpsimd", lambda gq, g=g: gq.dma_start(
                        out=masks_g[:], in_=masks_h[:, 4 * g:4 * g + 4, :, :]),
                        waits=[("bar1", 5), ("msk", P.cnt.get("_mskuse_%d" % (g - 1), 0))],
                        inc="mk0", inc_n=16)
                    MSK_RDY = [("mk0", P.val("mk0"))]
                    K = kmax(g)

                    def emit_chain_pre(hh):
                        hbb = hh % 2
                        srw = [("av", av_counts[hh])]
                        if rbm_vals:
                            srw.append(("rbm", max(rbm_vals.values())))
                        P.emit("vector", lambda v, hbb=hbb: v.tensor_copy(
                            srow_b[0:1, :], pAV[hbb][64:65, :]), waits=srw)
                        P.emit("vector", lambda v: v.drain(), inc="sr")
                        sr_vals[hh] = P.val("sr")
                        rw = [("sr", sr_vals[hh]), ("xt", ONES_RDY)]
                        prev_rbr = rbr_vals[hh - 1] if (hh - 1) in rbr_vals else last_rbr
                        if prev_rbr:
                            rw.append(("rbr", prev_rbr))
                        rw.append(("y1a", P.val("y1a")))
                        def rbmm(te):
                            return te.matmul(pWO[0:64, 512:1024], ones1b[:], srow_b[0:1, :], start=True, stop=True)
                        P.emit("tensor", rbmm, waits=rw, inc="rbm")
                        rbm_vals[hh] = P.val("rbm")

                    def emit_chain_act(hh):
                        hbb = hh % 2
                        P.emit("scalar", lambda s: s.activation(rbl_t[:], pWO[0:64, 512:1024], AF.Ln),
                               waits=[("rbm", rbm_vals[hh])])
                        P.emit("scalar", lambda s: s.drain())
                        P.emit("scalar", lambda s, hbb=hbb: s.activation(
                            rbr[hbb][:], rbl_t[:], AF.Exp, bias=0.0, scale=-1.0))
                        P.emit("scalar", lambda s: s.drain(), inc="rbr")
                        rbr_vals[hh] = P.val("rbr")

                    def emit_chain_post(hh):
                        hbb = hh % 2
                        cp, half = hh // 2, (hh % 2) * 64
                        P.emit("vector", lambda v, hbb=hbb, cp=cp, half=half: v.tensor_tensor(
                            attnT[half:half + 64, cp, :], pAV[hbb][0:64, :], rbr[hbb][:], ALU.mult),
                            waits=[("rbr", rbr_vals[hh]), ("wo", wo_prev_g)])
                        P.emit("vector", lambda v: v.drain(), inc="ate")
                        avbank_free[hbb] = ("ate", P.val("ate"))

                    def emit_chain(hh):
                        emit_chain_pre(hh)
                        emit_chain_act(hh)
                        emit_chain_post(hh)

                    for p in range(8):
                        a, b = 2 * p, 2 * p + 1
                        prev_a = prev_b = None
                        for kc in range(K):
                            sinfo_a = emit_score(g, a, kc)
                            sinfo_b = emit_score(g, b, kc)
                            if p >= 1:
                                if kc == 0:
                                    emit_chain_pre(2 * p - 2)
                                    emit_chain_act(2 * p - 2)
                                    emit_chain_pre(2 * p - 1)
                                    emit_chain_act(2 * p - 1)
                                elif kc == 1:
                                    emit_chain_post(2 * p - 2)
                                    emit_chain_post(2 * p - 1)
                            if kc >= 1:
                                emit_av(g, a, kc - 1, prev_a)
                                emit_av(g, b, kc - 1, prev_b)
                            prev_a, prev_b = sinfo_a, sinfo_b
                        emit_av(g, a, K - 1, prev_a)
                        emit_av(g, b, K - 1, prev_b)
                        av_counts[a] = P.val("av") - 1
                        av_counts[b] = P.val("av")
                    emit_chain(14)
                    emit_chain(14)
                    emit_chain(15)

                    ATE_G = P.val("ate")
                    for qq in range(4):
                        qt = 4 * g + qq
                        qb = qt % 2
                        P.emit("sync", lambda sy, qt=qt: sy.dma_start(
                            out=xqr_t[:], in_=xq_h[qt * 128:(qt + 1) * 128, :]),
                            waits=[("bar1", 5), ("y1a", P.val("y1a"))], inc="xqd", inc_n=16)
                        def womm(te, qq=qq):
                            ins = None
                            for cp in range(8):
                                for nb in range(2):
                                    ins = te.matmul(
                                        pWO[:, nb * 512:(nb + 1) * 512],
                                        attnT[:, cp, qq * 128:(qq + 1) * 128],
                                        wo_sb[:, cp, nb * 512:(nb + 1) * 512],
                                        start=(cp == 0), stop=(cp == 7))
                            return ins
                        wow = [("ate", ATE_G), ("wB", W_B), ("y1a", P.val("y1a"))]
                        P.emit("tensor", womm, waits=wow, inc="wo")
                        WO_T = P.val("wo")
                        P.emit("vector", lambda v: v.tensor_tensor(y1a_t[:], pWO[:], bob[:], ALU.add),
                               waits=[("wo", WO_T), ("xqd", P.val("xqd"))])
                        P.emit("vector", lambda v: v.drain())
                        P.emit("vector", lambda v: v.scalar_tensor_tensor(
                            y1b_t[:], y1a_t[:], 1.0, xqr_t[:], ALU.mult, ALU.add, accum_out=stq[:, 0:1]))
                        P.emit("vector", lambda v: v.drain(), inc="y1a")
                        Y1A = P.val("y1a")
                        P.emit("vector", lambda v: v.tensor_scalar(stq[:, 0:1], stq[:, 0:1], 1.0 / D, None, ALU.mult))
                        P.emit("vector", lambda v: v.drain())
                        P.emit("vector", lambda v: v.tensor_scalar(y1a_t[:], y1b_t[:], stq[:, 0:1], None, ALU.subtract))
                        P.emit("vector", lambda v: v.drain())
                        P.emit("vector", lambda v: v.scalar_tensor_tensor(
                            y1b_t[:], y1a_t[:], 1.0, y1a_t[:], ALU.mult, ALU.mult, accum_out=stq[:, 1:2]))
                        P.emit("vector", lambda v: v.drain(), inc="v1")
                        P.emit("scalar", lambda s: s.activation(rln1[:], stq[:, 1:2], AF.Ln,
                                                                bias=epsb[:, 0:1], scale=1.0 / D),
                               waits=[("v1", P.val("v1"))])
                        P.emit("scalar", lambda s: s.drain())
                        P.emit("scalar", lambda s: s.activation(rstd1[:], rln1[:], AF.Exp, bias=0.0, scale=-0.5))
                        P.emit("scalar", lambda s: s.drain(), inc="sd1")
                        P.emit("vector", lambda v: v.tensor_scalar(y1b_t[:], y1a_t[:], rstd1[:, 0:1], None, ALU.mult),
                               waits=[("sd1", P.val("sd1"))])
                        P.emit("vector", lambda v: v.drain())
                        P.emit("vector", lambda v, qt=qt: v.tensor_copy(acc_t[:, qt, :], y1b_t[:]))
                        P.emit("vector", lambda v: v.drain())
                        P.emit("vector", lambda v, qt=qt: v.tensor_tensor(xlo_t[:], y1b_t[:], acc_t[:, qt, :], ALU.subtract))
                        P.emit("vector", lambda v: v.drain(), inc="xf")
                        XF = P.val("xf")
                        def tph(te, qt=qt):
                            ins = None
                            for dc in range(8):
                                ins = te.transpose(pT2[:, dc, :], acc_t[:, qt, dc * 128:(dc + 1) * 128], ident[:])
                            return ins
                        P.emit("tensor", tph, waits=[("xf", XF), ("xft", P.val("xft"))], inc="tpf")
                        P.emit("vector", lambda v: v.tensor_copy(xfT_t[:], pT2[:]),
                               waits=[("tpf", P.val("tpf"))])
                        P.emit("vector", lambda v: v.drain(), inc="xft")
                        def tpl(te):
                            ins = None
                            for dc in range(8):
                                ins = te.transpose(pT2[:, dc, :], xlo_t[:, dc * 128:(dc + 1) * 128], ident[:])
                            return ins
                        P.emit("tensor", tpl, waits=[("xft", P.val("xft"))], inc="tpf")
                        P.emit("vector", lambda v: v.tensor_copy(xfTlo_t[:], pT2[:]),
                               waits=[("tpf", P.val("tpf"))])
                        P.emit("vector", lambda v: v.drain(), inc="xft")
                        XFT = P.val("xft")
                        rtw = [("xft", XFT)]
                        if sbank_free[qb]:
                            rtw.append(sbank_free[qb])
                        def rtmm(te, qb=qb):
                            ins = None
                            srcs = ((xfT_t, rtr_hi), (xfT_t, rtr_lo), (xfTlo_t, rtr_hi))
                            n = 0
                            for src, w in srcs:
                                for dc in range(8):
                                    ins = te.matmul(pS[qb][:, 0:8], src[:, dc, :], w[:, dc, :],
                                                    start=(n == 0), stop=(n == 23))
                                    n += 1
                            return ins
                        P.emit("tensor", rtmm, waits=rtw, inc="rt")
                        P.emit("vector", lambda v, qb=qb: v.reduce_max(
                            nmx_t[:], pS[qb][:, 0:8], axis=AX.X, negate=True),
                            waits=[("rt", P.val("rt"))])
                        P.emit("vector", lambda v: v.drain(), inc="rmax")
                        P.emit("scalar", lambda s, qb=qb: s.activation(
                            e8_t[:], pS[qb][:, 0:8], AF.Exp, bias=nmx_t[:, 0:1], scale=1.0),
                            waits=[("rmax", P.val("rmax"))])
                        P.emit("scalar", lambda s: s.drain(), inc="rexp")
                        sbank_free[qb] = ("rexp", P.val("rexp"))
                        P.emit("vector", lambda v: v.reduce_sum(s8_t[:], e8_t[:], axis=AX.X),
                               waits=[("rexp", P.val("rexp"))])
                        P.emit("vector", lambda v: v.drain())
                        P.emit("vector", lambda v: v.reciprocal(nmx_t[:], s8_t[:]))
                        P.emit("vector", lambda v: v.drain())
                        P.emit("vector", lambda v: v.tensor_scalar(scr8_t[:], e8_t[:], nmx_t[:, 0:1], None, ALU.mult))
                        P.emit("vector", lambda v: v.drain())
                        P.emit("vector", lambda v: v.max(st8_t[:], scr8_t[:]))
                        P.emit("vector", lambda v: v.drain())
                        P.emit("vector", lambda v: v.memset(st8_t[:, 2:8], -1.0e30))
                        P.emit("vector", lambda v: v.drain())
                        P.emit("vector", lambda v: v.match_replace(s2t_t[:], st8_t[:], scr8_t[:], -2.0e30))
                        P.emit("vector", lambda v: v.drain())
                        P.emit("vector", lambda v: v.tensor_single_scalar(mask8_t[:], s2t_t[:], -1.0e29, ALU.is_lt))
                        P.emit("vector", lambda v: v.drain())
                        P.emit("vector", lambda v: v.tensor_mul(mask8_t[:], mask8_t[:], scr8_t[:]))
                        P.emit("vector", lambda v: v.tensor_add(stq[:, 2:3], st8_t[:, 0:1], st8_t[:, 1:2]))
                        P.emit("vector", lambda v: v.drain())
                        P.emit("vector", lambda v: v.tensor_scalar(s2t_t[:, 0:1], stq[:, 2:3], 1.0, 1e-20, ALU.mult, ALU.add))
                        P.emit("vector", lambda v: v.drain())
                        P.emit("vector", lambda v: v.reciprocal(s8_t[:], s2t_t[:, 0:1]))
                        P.emit("vector", lambda v: v.drain())
                        P.emit("vector", lambda v, qt=qt: v.tensor_scalar(
                            comb_t[:, qt, :], mask8_t[:], s8_t[:, 0:1], None, ALU.mult))
                        P.emit("vector", lambda v: v.drain(), inc="cmb")

                    wo_prev_g = P.val("wo")
                    P.cnt["_mskuse_%d" % g] = P.val("msk")

                if debug >= 2:
                    P.emit("gpsimd", lambda g: g.dma_start(out=dbg["acc"], in_=acc_t[:]),
                           waits=[("xf", P.val("xf"))], inc="dbgs", inc_n=16)
                    P.emit("gpsimd", lambda g: g.dma_start(out=dbg["comb"], in_=comb_t[:]),
                           waits=[("cmb", P.val("cmb"))], inc="dbgs", inc_n=16)
                    P.emit("gpsimd", lambda g: g.dma_start(out=dbg["attnT"], in_=attnT[:]),
                           waits=[("ate", P.val("ate"))], inc="dbgs", inc_n=16)

                P.emit("sync", lambda sy: sy.sem_inc(sems["bar2"], 1), waits=[("xqd", P.val("xqd"))])
                P.emit("tensor", lambda te: te.sem_inc(sems["bar2"], 1))
                P.emit("vector", lambda v: v.sem_inc(sems["bar2"], 1), waits=[("cmb", P.val("cmb"))])
                P.emit("scalar", lambda s: s.sem_inc(sems["bar2"], 1))
                P.emit("gpsimd", lambda g: g.sem_inc(sems["bar2"], 1),
                       waits=[("wB", W_B), ("dbgs", P.val("dbgs"))])
                P.cnt["bar2"] = 5
                P.run_block(nc, sems)
                cnt2 = P.cnt

        if phases < 3:
            for c in sems_ctx:
                c.__exit__(None, None, None)
            return nc
        with ExitStack() as _stD:
            sbD = lambda *a: _stD.enter_context(nc.sbuf_tensor(*a))
            psD = lambda *a: _stD.enter_context(nc.psum_tensor(*a))
            xfT = sbD("xfT", [128, 8, NSLOT * 128], BF16)
            ht_sb = sbD("ht", [128, NHC, NSLOT * 128], BF16)
            wg_sb = sbD("wg_sb", [128, NHC * 8, 128], BF16)
            wu_sb = sbD("wu_sb", [128, NHC * 8, 128], BF16)
            wd_sb = sbD("wd_sb", [128, NHC, 1024], BF16)
            sG0 = sbD("sG0", [128, 512], BF16)
            sG1 = sbD("sG1", [128, 512], BF16)
            xc2 = sbD("xc2", [128, D], F32)
            sq2 = sbD("sq2", [128, D], F32)
            st2 = sbD("st2", [128, 2], F32)
            rstd2 = sbD("rstd2", [128, 1], F32)
            rln2 = sbD("rln2", [128, 1], F32)
            outst0 = sbD("outst0", [128, D], F32)
            outst1 = sbD("outst1", [128, D], F32)
            pG0 = psD("pG0", [128, 512], F32)
            pG1 = psD("pG1", [128, 512], F32)
            pU0 = psD("pU0", [128, 512], F32)
            pU1 = psD("pU1", [128, 512], F32)
            pD0 = psD("pD0", [128, 1024], F32)
            
            pT3 = psD("pT3", [128, 8, 128], BF16)
            P = Prog(cnt2)
            pG, pU = [pG0, pG1], [pU0, pU1]
            pD = pD0
            sG = [sG0, sG1]
            outst = [outst0, outst1]

            for t in range(NSLOT):
                def tpx(te, t=t):
                    ins = None
                    for dc in range(8):
                        ins = te.transpose(pT3[:, dc, :], acc_t[:, t, dc * 128:(dc + 1) * 128], ident[:])
                    return ins
                P.emit("tensor", tpx, waits=[("bar2", 5), ("xft", P.val("xft"))], inc="tpf")
                P.emit("vector", lambda v, t=t: v.tensor_copy(xfT[:, :, t * 128:(t + 1) * 128], pT3[:]),
                       waits=[("bar2", 5), ("tpf", P.val("tpf"))])
                P.emit("vector", lambda v: v.drain(), inc="xft")
            XFT_ALL = P.val("xft")

            for e in range(NE):
                P.emit("gpsimd", lambda g, e=e: g.dma_start(
                    out=wg_sb[:], in_=wg_h[e].rearrange("(dc p) (hc q) -> p hc dc q", p=128, q=128)),
                    waits=[("bar2", 5), ("gmm", 44 * e)], inc="wg", inc_n=16)
                P.emit("gpsimd", lambda g, e=e: g.dma_start(
                    out=wu_sb[:], in_=wu_h[e].rearrange("(dc p) (hc q) -> p hc dc q", p=128, q=128)),
                    waits=[("bar2", 5), ("umm", 44 * e)], inc="wu", inc_n=16)
                P.emit("gpsimd", lambda g, e=e: g.dma_start(
                    out=wd_sb[:], in_=wd_h[e].rearrange("(hc p) n -> p hc n", p=128)),
                    waits=[("bar2", 5), ("dmm", 16 * e)], inc="wd", inc_n=16)

            for e in range(NE):
                for tc in range(4):
                    for hc in range(NHC):
                        gi = (tc * NHC + hc) % 2
                        def gmm(te, hc=hc, tc=tc, gi=gi, e=e):
                            ins = None
                            for dc in range(8):
                                ins = te.matmul(pG[gi][:], wg_sb[:, hc * 8 + dc, :],
                                                xfT[:, dc, tc * 512:(tc + 1) * 512],
                                                start=(dc == 0), stop=(dc == 7))
                            return ins
                        P.emit("tensor", gmm,
                               waits=[("wg", 16 * (e + 1)), ("xft", XFT_ALL),
                                      ("sg", P.val("sg") - 1)], inc="gmm")
                        def umm(te, hc=hc, tc=tc, gi=gi, e=e):
                            ins = None
                            for dc in range(8):
                                ins = te.matmul(pU[gi][:], wu_sb[:, hc * 8 + dc, :],
                                                xfT[:, dc, tc * 512:(tc + 1) * 512],
                                                start=(dc == 0), stop=(dc == 7))
                            return ins
                        P.emit("tensor", umm,
                               waits=[("wu", 16 * (e + 1)), ("hm", P.val("hm") - 1)], inc="umm")
                        P.emit("scalar", lambda s, gi=gi: s.activation(sG[gi][:], pG[gi][:], AF.Silu),
                               waits=[("gmm", P.val("gmm")), ("hm", P.val("hm") - 1)])
                        P.emit("scalar", lambda s: s.drain(), inc="sg")
                        P.emit("vector", lambda v, gi=gi, hc=hc, tc=tc: v.tensor_tensor(
                            ht_sb[:, hc, tc * 512:(tc + 1) * 512], sG[gi][:], pU[gi][:], ALU.mult),
                            waits=[("sg", P.val("sg")), ("umm", P.val("umm"))])
                        P.emit("vector", lambda v: v.drain(), inc="hm")
                HM_E = P.val("hm")
                for t in range(NSLOT):
                    def dmm(te, t=t):
                        ins = None
                        for hc in range(NHC):
                            for nb in range(2):
                                ins = te.matmul(pD[:, nb * 512:(nb + 1) * 512],
                                                ht_sb[:, hc, t * 128:(t + 1) * 128],
                                                wd_sb[:, hc, nb * 512:(nb + 1) * 512],
                                                start=(hc == 0), stop=(hc == NHC - 1))
                        return ins
                    P.emit("tensor", dmm,
                           waits=[("wd", 16 * (e + 1)), ("hm", HM_E), ("acc", P.val("acc"))],
                           inc="dmm")
                    if e == NE - 1:
                        P.emit("vector", lambda v, e=e, t=t: v.scalar_tensor_tensor(
                            acc_t[:, t, :], pD[:], comb_t[:, t, e:e + 1], acc_t[:, t, :],
                            ALU.mult, ALU.add, accum_out=st2[:, 0:1]),
                            waits=[("dmm", P.val("dmm"))])
                    else:
                        P.emit("vector", lambda v, e=e, t=t: v.scalar_tensor_tensor(
                            acc_t[:, t, :], pD[:], comb_t[:, t, e:e + 1], acc_t[:, t, :],
                            ALU.mult, ALU.add),
                            waits=[("dmm", P.val("dmm"))])
                    P.emit("vector", lambda v: v.drain(), inc="acc")
                    if e == NE - 1:
                        ob = t % 2
                        P.emit("vector", lambda v: v.tensor_scalar(st2[:, 0:1], st2[:, 0:1], 1.0 / D, None, ALU.mult),
                               waits=[("acc", P.val("acc"))])
                        P.emit("vector", lambda v: v.drain())
                        P.emit("vector", lambda v, t=t: v.tensor_scalar(xc2[:], acc_t[:, t, :], st2[:, 0:1], None, ALU.subtract))
                        P.emit("vector", lambda v: v.drain())
                        P.emit("vector", lambda v: v.scalar_tensor_tensor(
                            sq2[:], xc2[:], 1.0, xc2[:], ALU.mult, ALU.mult, accum_out=st2[:, 1:2]))
                        P.emit("vector", lambda v: v.drain(), inc="ln2")
                        P.emit("scalar", lambda s: s.activation(rln2[:], st2[:, 1:2], AF.Ln,
                                                                bias=epsb[:, 0:1], scale=1.0 / D),
                               waits=[("ln2", P.val("ln2"))])
                        P.emit("scalar", lambda s: s.drain())
                        P.emit("scalar", lambda s: s.activation(rstd2[:], rln2[:], AF.Exp, bias=0.0, scale=-0.5))
                        P.emit("scalar", lambda s: s.drain(), inc="sd2")
                        P.emit("vector", lambda v, ob=ob: v.tensor_scalar(
                            outst[ob][:], xc2[:], rstd2[:, 0:1], None, ALU.mult),
                            waits=[("sd2", P.val("sd2")), ("out", 16 * t)])
                        P.emit("vector", lambda v: v.drain(), inc="eo")
                        P.emit("sync", lambda sy, t=t, ob=ob: sy.dma_start(
                            out=out_h[t * 128:(t + 1) * 128, :], in_=outst[ob][:]),
                            waits=[("eo", P.val("eo"))], inc="out", inc_n=16)

            P.emit("sync", lambda sy: sy.nop(), waits=[("out", 16 * NSLOT)])
            P.run_block(nc, sems)

        for c in sems_ctx:
            c.__exit__(None, None, None)
    return nc



def _make_masks(tiles):
    m = np.zeros((128, NSLOT, 2, 128), np.float32)
    tri = (np.arange(128)[:, None] <= np.arange(128)[None, :]).astype(np.float32)
    for j, t in enumerate(tiles):
        c = t + 1
        for i in range(2):
            kc = 2 * j + i
            if kc < c - 1:
                m[:, j, i, :] = 1.0
            elif kc == c - 1:
                m[:, j, i, :] = tri
    return np.ascontiguousarray(m.astype(ml_dtypes.bfloat16))


def kernel(L0_attn_x, L0_attn_W1_q, L0_attn_W2_q, L0_attn_W1_k, L0_attn_W2_k,
           L0_attn_W_v, L0_attn_b_v, L0_attn_W_o, L0_attn_b_o,
           L0_ffn_w_router, L0_ffn_w_gate, L0_ffn_w_up, L0_ffn_w_down,
           _debug=0):
    x = np.ascontiguousarray(np.asarray(L0_attn_x, dtype=np.float32))
    arrs = {
        "W1_q": L0_attn_W1_q, "W2_q": L0_attn_W2_q, "W1_k": L0_attn_W1_k,
        "W2_k": L0_attn_W2_k, "W_v": L0_attn_W_v, "W_o": L0_attn_W_o,
        "w_router": L0_ffn_w_router, "w_gate": L0_ffn_w_gate,
        "w_up": L0_ffn_w_up, "w_down": L0_ffn_w_down,
    }
    arrs = {k: np.ascontiguousarray(np.asarray(v, dtype=np.float32)) for k, v in arrs.items()}
    bv = np.asarray(L0_attn_b_v, dtype=np.float32)
    bo = np.asarray(L0_attn_b_o, dtype=np.float32)
    arrs["b_v_b"] = np.ascontiguousarray(np.broadcast_to(bv, (128, D)))
    arrs["b_o_b"] = np.ascontiguousarray(np.broadcast_to(bo, (128, D)))
    arrs["ident_hbm"] = np.eye(128, dtype=np.float32)
    masks = {0: _make_masks(TILES_A), 1: _make_masks(TILES_B)}

    in_maps = []
    for c in range(8):
        b, v_ = c // 2, c % 2
        tiles = TILES_A if v_ == 0 else TILES_B
        rows = np.concatenate([np.arange(t * 128, (t + 1) * 128) for t in tiles])
        xb = x[b]
        prev = rows - 1
        xqp = np.where((prev >= 0)[:, None], xb[np.maximum(prev, 0)], np.float32(0.0))
        m = dict(arrs)
        m["x_kv"] = xb
        m["xq"] = np.ascontiguousarray(xb[rows])
        m["xq_prev"] = np.ascontiguousarray(xqp.astype(np.float32))
        m["masks"] = masks[v_]
        in_maps.append(m)

    nc = build_program(debug=_debug, phases=_PHASES)
    res = run_bass_kernel_spmd(nc, in_maps, core_ids=list(range(8)))

    out = np.zeros((B, T, D), np.float32)
    for c in range(8):
        b, v_ = c // 2, c % 2
        tiles = TILES_A if v_ == 0 else TILES_B
        oq = np.asarray(res.results[c]["out_q"])
        for j, t in enumerate(tiles):
            out[b, t * 128:(t + 1) * 128, :] = oq[j * 128:(j + 1) * 128, :]
    if _debug:
        return out, res
    return out


# version 17
# speedup vs baseline: 1.0418x; 1.0418x over previous
from contextlib import ExitStack

import numpy as np
import ml_dtypes
import concourse.bass as bass
import concourse.mybir as mybir
from concourse.bass_utils import run_bass_kernel_spmd

F32 = mybir.dt.float32
BF16 = mybir.dt.bfloat16
AF = mybir.ActivationFunctionType
ALU = mybir.AluOpType
AX = mybir.AxisListType

B, T, D, H, DH, C6 = 4, 4096, 1024, 16, 64, 6
NKT = 32
NSLOT = 16
NG = 4
HFF = 1408
NE = 8
NHC = 11
SCALE = float(6.0 ** -0.5)
PAIRS = [(0, 1), (0, 2), (0, 3), (1, 2), (1, 3), (2, 3)]
JSIGN = [1.0, -1.0, 1.0, 1.0, -1.0, 1.0]
TILES_A = [0, 3, 4, 7, 8, 11, 12, 15, 16, 19, 20, 23, 24, 27, 28, 31]
TILES_B = [1, 2, 5, 6, 9, 10, 13, 14, 17, 18, 21, 22, 25, 26, 29, 30]
ENGINES = ("sync", "tensor", "vector", "scalar", "gpsimd")
_PHASES = 3
_SKIP_Q = False


def kmax(g):
    return 8 * g + 8


class Prog:

    def __init__(self, cnt=None):
        self.ops = {e: [] for e in ENGINES}
        self.cnt = dict(cnt) if cnt else {}

    def val(self, sem):
        return self.cnt.get(sem, 0)

    def emit(self, eng, fn, waits=(), inc=None, inc_n=1):
        w = [(s, v) for (s, v) in waits if v > 0]
        self.ops[eng].append((fn, w, inc, inc_n))
        if inc is not None:
            self.cnt[inc] = self.cnt.get(inc, 0) + inc_n

    def replay(self, eng_name, handle, sems):
        for fn, waits, inc, inc_n in self.ops[eng_name]:
            for s, v in waits:
                handle.wait_ge(sems[s], v)
            ins = fn(handle)
            if inc is not None:
                if ins is None:
                    raise RuntimeError(f"op with inc {inc} returned no instruction")
                ins.then_inc(sems[inc], inc_n)

    def run_block(self, nc, sems):
        with nc.Block() as blk:
            blk.sync(lambda e: self.replay("sync", e, sems))
            blk.tensor(lambda e: self.replay("tensor", e, sems))
            blk.vector(lambda e: self.replay("vector", e, sems))
            blk.scalar(lambda e: self.replay("scalar", e, sems))
            blk.gpsimd(lambda e: self.replay("gpsimd", e, sems))


def build_program(debug=0, phases=3):
    nc = bass.Bass("TRN2", target_bir_lowering=False)

    x_kv = nc.dram_tensor("x_kv", [T, D], F32, kind="ExternalInput").ap()
    xq_h = nc.dram_tensor("xq", [NSLOT * 128, D], F32, kind="ExternalInput").ap()
    xqp_h = nc.dram_tensor("xq_prev", [NSLOT * 128, D], F32, kind="ExternalInput").ap()
    masks_h = nc.dram_tensor("masks", [128, NSLOT, 2, 128], BF16, kind="ExternalInput").ap()
    ident_h = nc.dram_tensor("ident_hbm", [128, 128], F32, kind="ExternalInput").ap()
    w1q_h = nc.dram_tensor("W1_q", [2 * D, 64], F32, kind="ExternalInput").ap()
    w2q_h = nc.dram_tensor("W2_q", [2 * D, 64], F32, kind="ExternalInput").ap()
    w1k_h = nc.dram_tensor("W1_k", [D, 64], F32, kind="ExternalInput").ap()
    w2k_h = nc.dram_tensor("W2_k", [D, 64], F32, kind="ExternalInput").ap()
    wv_h = nc.dram_tensor("W_v", [D, D], F32, kind="ExternalInput").ap()
    bvb_h = nc.dram_tensor("b_v_b", [128, D], F32, kind="ExternalInput").ap()
    wo_h = nc.dram_tensor("W_o", [D, D], F32, kind="ExternalInput").ap()
    bob_h = nc.dram_tensor("b_o_b", [128, D], F32, kind="ExternalInput").ap()
    wr_h = nc.dram_tensor("w_router", [D, NE], F32, kind="ExternalInput").ap()
    wg_h = nc.dram_tensor("w_gate", [NE, D, HFF], F32, kind="ExternalInput").ap()
    wu_h = nc.dram_tensor("w_up", [NE, D, HFF], F32, kind="ExternalInput").ap()
    wd_h = nc.dram_tensor("w_down", [NE, HFF, D], F32, kind="ExternalInput").ap()
    out_h = nc.dram_tensor("out_q", [NSLOT * 128, D], F32, kind="ExternalOutput").ap()

    dbg = {}
    if debug >= 1:
        dbg["vs"] = nc.dram_tensor("dbg_vs", [128, NKT, H, 65], F32, kind="ExternalOutput").ap()
        dbg["jlkt"] = nc.dram_tensor("dbg_jlkt", [128, 4, NKT * 128], F32, kind="ExternalOutput").ap()
        dbg["lqt"] = nc.dram_tensor("dbg_lqt", [128, 4, NSLOT * 128], F32, kind="ExternalOutput").ap()
    if debug >= 3:
        dbg["kk"] = nc.dram_tensor("dbg_kk", [128, 128], F32, kind="ExternalOutput").ap()
        dbg["lraw"] = nc.dram_tensor("dbg_lraw", [128, H, C6], F32, kind="ExternalOutput").ap()
        dbg["ssb"] = nc.dram_tensor("dbg_ssb", [128, H], F32, kind="ExternalOutput").ap()
        dbg["rnrm"] = nc.dram_tensor("dbg_rnrm", [128, H], F32, kind="ExternalOutput").ap()
        dbg["lsc"] = nc.dram_tensor("dbg_lsc", [128, H, C6], F32, kind="ExternalOutput").ap()
    if debug >= 2:
        dbg["acc"] = nc.dram_tensor("dbg_acc", [128, NSLOT, D], F32, kind="ExternalOutput").ap()
        dbg["comb"] = nc.dram_tensor("dbg_comb", [128, NSLOT, NE], F32, kind="ExternalOutput").ap()
        dbg["attnT"] = nc.dram_tensor("dbg_attnT", [128, 8, 512], F32, kind="ExternalOutput").ap()

    sem_names = [
        "wA", "z0", "x0", "x1", "cast", "tp", "xt", "mmv", "ss", "rr", "lk", "tk", "jl",
        "bar1", "wB", "mk0", "mk1", "sc", "exp", "msk", "av", "sr", "rbm", "rbf", "rbr",
        "ate", "wo", "xqd", "y1a", "v1", "sd1", "xf", "tpf", "xft", "rt",
        "rmax", "rexp", "cmb", "bar2", "wg", "wu", "wd", "gmm", "umm", "sg",
        "hm", "dmm", "acc", "ln2", "sd2", "eo", "out", "dbgs",
    ]

    with ExitStack() as _st0:
        sb = lambda *a: _st0.enter_context(nc.sbuf_tensor(*a))
        ident = sb("ident", [128, 128], BF16)
        acc_t = sb("acc", [128, NSLOT, D], BF16)
        comb_t = sb("comb", [128, NSLOT, NE], F32)
        rtr_f = sb("rtr_f", [128, 8, NE], F32)
        rtr_hi = sb("rtr_hi", [128, 8, NE], BF16)
        rtr_lo = sb("rtr_lo", [128, 8, NE], BF16)
        epsb = sb("epsb", [128, 1], F32)
        sems_ctx = [nc.semaphore(n) for n in sem_names]
        sems = dict(zip(sem_names, (c.__enter__() for c in sems_ctx)))

        with ExitStack() as _st1:
            sb1 = lambda *a: _st1.enter_context(nc.sbuf_tensor(*a))
            Vs = sb1("Vs", [128, NKT, H, 65], BF16)
            JLKT = sb1("JLKT", [128, 4, NKT * 128], BF16)
            LQT = sb1("LQT", [128, 4, NSLOT * 128], BF16)
            with ExitStack() as _stA:
                sbA = lambda *a: _stA.enter_context(nc.sbuf_tensor(*a))
                psA = lambda *a: _stA.enter_context(nc.psum_tensor(*a))
                wvks = sbA("wvks", [128, 8, 1152], BF16)
                wq_sb = sbA("wq_sb", [128, 16, 128], BF16)
                bvb = sbA("bvb", [128, D], F32)
                xa0 = sbA("xa0", [128, D], F32)
                xa1 = sbA("xa1", [128, D], F32)
                xab0 = sbA("xab0", [128, D], BF16)
                xab1 = sbA("xab1", [128, D], BF16)
                xT0 = sbA("xT0", [128, 8, 128], BF16)
                xT1 = sbA("xT1", [128, 8, 128], BF16)
                xTP0 = sbA("xTP0", [128, 8, 128], BF16)
                xTP1 = sbA("xTP1", [128, 8, 128], BF16)
                Lraw0 = sbA("Lraw0", [128, H, C6], F32)
                Lraw1 = sbA("Lraw1", [128, H, C6], F32)
                Lsq = sbA("Lsq", [128, H, C6], F32)
                prodAll = sbA("prodAll", [128, 12, H], F32)
                sst = sbA("sst", [128, 3, H], F32)
                ssb = sbA("ssb", [128, H], F32)
                rnrm0 = sbA("rnrm0", [128, H], F32)
                rnrm1 = sbA("rnrm1", [128, H], F32)
                nrnrm = sbA("nrnrm", [128, H], F32)
                rln = sbA("rln", [128, H], F32)
                Lsc = sbA("Lsc", [128, H, C6], BF16)
                kk_sb = sbA("kk_sb", [128, 128], F32)
                zb = sbA("zb", [128, 128], BF16)
                pT = psA("pT", [128, 8, 128], BF16)
                pTk = psA("pTk", [128, 4, 128], BF16)
                pV0 = psA("pV0", [128, 1152], F32)
                pV1 = psA("pV1", [128, 1152], F32)
                P = Prog()
                xa, xab, xT, xTP = [xa0, xa1], [xab0, xab1], [xT0, xT1], [xTP0, xTP1]
                pV = [pV0, pV1]

                P.emit("gpsimd", lambda g: g.dma_start(out=ident[:], in_=ident_h), inc="wA", inc_n=16)
                P.emit("gpsimd", lambda g: g.dma_start(
                    out=wvks[:, :, 0:1024], in_=wv_h.rearrange("(dc p) n -> p dc n", p=128)), inc="wA", inc_n=16)
                P.emit("gpsimd", lambda g: g.dma_start(
                    out=wvks[:, :, 1024:1088], in_=w1k_h.rearrange("(dc p) n -> p dc n", p=128)), inc="wA", inc_n=16)
                P.emit("gpsimd", lambda g: g.dma_start(
                    out=wvks[:, :, 1088:1152], in_=w2k_h.rearrange("(dc p) n -> p dc n", p=128)), inc="wA", inc_n=16)
                P.emit("gpsimd", lambda g: g.dma_start(
                    out=wq_sb[:, :, 0:64], in_=w1q_h.rearrange("(kc p) n -> p kc n", p=128)), inc="wA", inc_n=16)
                P.emit("gpsimd", lambda g: g.dma_start(
                    out=wq_sb[:, :, 64:128], in_=w2q_h.rearrange("(kc p) n -> p kc n", p=128)), inc="wA", inc_n=16)
                P.emit("gpsimd", lambda g: g.dma_start(out=bvb[:], in_=bvb_h), inc="wA", inc_n=16)
                P.emit("gpsimd", lambda g: g.dma_start(
                    out=rtr_f[:], in_=wr_h.rearrange("(dc p) n -> p dc n", p=128)), inc="wA", inc_n=16)
                W_SETUP = P.val("wA")

                P.emit("vector", lambda v: v.memset(epsb[:], 1e-5))
                P.emit("vector", lambda v: v.drain())
                P.emit("vector", lambda v: v.memset(zb[:], 0.0))
                P.emit("vector", lambda v: v.drain(), inc="z0")
                def ztps(te):
                    ins = None
                    for m in range(4):
                        ins = te.transpose(pTk[:, m, :], zb[:], ident[:])
                    return ins
                P.emit("tensor", ztps, waits=[("z0", 1), ("wA", W_SETUP)], inc="tk")
                P.emit("vector", lambda v: v.tensor_copy(rtr_hi[:], rtr_f[:]), waits=[("wA", W_SETUP)])
                P.emit("vector", lambda v: v.drain())
                P.emit("vector", lambda v: v.tensor_tensor(rtr_lo[:], rtr_f[:], rtr_hi[:], ALU.subtract))
                P.emit("vector", lambda v: v.drain())


                xt_count = {}

                def tile_prep(idx, src_ap, row0, xT_list, xT_slot, xT_free):
                    b = idx % 2
                    xs = "x%d" % b
                    P.emit("sync", lambda sy, s=src_ap, r=row0, bb=b: sy.dma_start(
                        out=xa[bb][:], in_=s[r:r + 128, :]),
                        waits=[("cast", idx - 1)], inc=xs, inc_n=16)
                    P.emit("vector", lambda v, bb=b: v.tensor_copy(xab[bb][:], xa[bb][:]),
                           waits=[(xs, 16 * (idx // 2 + 1)), ("tp", idx - 1)])
                    P.emit("vector", lambda v: v.drain(), inc="cast")
                    def tps(te, bb=b):
                        ins = None
                        for dc in range(8):
                            ins = te.transpose(pT[:, dc, :], xab[bb][:, dc * 128:(dc + 1) * 128], ident[:])
                        return ins
                    P.emit("tensor", tps, waits=[("cast", idx + 1), ("xt", idx)], inc="tp")
                    P.emit("vector", lambda v, sl=xT_slot: v.tensor_copy(xT_list[sl][:], pT[:]),
                           waits=[("tp", idx + 1)] + list(xT_free))
                    P.emit("vector", lambda v: v.drain(), inc="xt")
                    xt_count[idx] = P.val("xt")

                Lraw = [Lraw0, Lraw1]
                rnrm = [rnrm0, rnrm1]
                ext_state = {}
                ext_serial = [0]

                def ext_front(pv_t, off0, jside, dest, col0):
                    se = ext_serial[0]
                    ext_serial[0] += 1
                    lb = se % 2
                    P.emit("vector", lambda v: v.tensor_copy(kk_sb[:], pv_t[:, off0:off0 + 128]),
                           waits=[("mmv", P.val("mmv"))])
                    P.emit("vector", lambda v: v.drain())
                    o1, o2 = 0, 64
                    for c, (i, j) in enumerate(PAIRS):
                        P.emit("vector", lambda v, i=i, j=j, c=c: v.tensor_mul(
                            prodAll[:, 2 * c, :], kk_sb[:, o1 + i: o1 + 64: 4], kk_sb[:, o2 + j: o2 + 64: 4]))
                        P.emit("vector", lambda v, i=i, j=j, c=c: v.tensor_mul(
                            prodAll[:, 2 * c + 1, :], kk_sb[:, o1 + j: o1 + 64: 4], kk_sb[:, o2 + i: o2 + 64: 4]))
                    P.emit("vector", lambda v: v.drain())
                    for c in range(C6):
                        P.emit("vector", lambda v, c=c, lb=lb: v.tensor_tensor(
                            Lraw[lb][:, :, c], prodAll[:, 2 * c, :], prodAll[:, 2 * c + 1, :], ALU.subtract))
                    P.emit("vector", lambda v: v.drain())
                    P.emit("vector", lambda v, lb=lb: v.tensor_tensor(Lsq[:], Lraw[lb][:], Lraw[lb][:], ALU.mult),
                           waits=[("rr", P.val("rr"))])
                    P.emit("vector", lambda v: v.drain())
                    for k in range(3):
                        P.emit("vector", lambda v, k=k: v.tensor_add(
                            sst[:, k, :], Lsq[:, :, 2 * k], Lsq[:, :, 2 * k + 1]))
                    P.emit("vector", lambda v: v.drain())
                    P.emit("vector", lambda v: v.tensor_add(sst[:, 0, :], sst[:, 0, :], sst[:, 1, :]))
                    P.emit("vector", lambda v: v.drain())
                    P.emit("vector", lambda v: v.tensor_add(sst[:, 1, :], sst[:, 0, :], sst[:, 2, :]))
                    P.emit("vector", lambda v: v.drain())
                    P.emit("vector", lambda v: v.tensor_scalar_max(ssb[:], sst[:, 1, :], 1e-24))
                    P.emit("vector", lambda v: v.drain(), inc="ss")
                    P.emit("scalar", lambda s: s.activation(rln[:], ssb[:], AF.Ln),
                           waits=[("ss", P.val("ss")), ("lk", P.val("lk"))])
                    P.emit("scalar", lambda s: s.drain())
                    P.emit("scalar", lambda s, lb=lb: s.activation(rnrm[lb][:], rln[:], AF.Exp, bias=0.0, scale=-0.5))
                    P.emit("scalar", lambda s: s.drain(), inc="rr")
                    ext_state[se] = (jside, dest, col0, lb)

                def ext_back(se):
                    jside, dest, col0, lb = ext_state.pop(se)
                    P.emit("vector", lambda v, lb=lb: v.tensor_scalar_mul(nrnrm[:], rnrm[lb][:], -1.0),
                           waits=[("rr", se + 1), ("tk", P.val("tk"))])
                    P.emit("vector", lambda v: v.drain())
                    for c in range(C6):
                        if jside:
                            oc = 5 - c
                            m = rnrm[lb] if JSIGN[oc] > 0 else nrnrm
                            P.emit("vector", lambda v, c=c, oc=oc, m=m, lb=lb: v.tensor_tensor(
                                Lsc[:, :, oc], Lraw[lb][:, :, c], m[:], ALU.mult))
                        else:
                            P.emit("vector", lambda v, c=c, lb=lb: v.tensor_tensor(
                                Lsc[:, :, c], Lraw[lb][:, :, c], rnrm[lb][:], ALU.mult))
                    P.emit("vector", lambda v: v.drain(), inc="lk")
                    head_transposes(dest, col0)

                def head_transposes(dest, col0):
                    def tps(te):
                        ins = None
                        for h in range(H):
                            g_, m = h % 4, h // 4
                            ins = te.transpose(
                                pTk[32 * g_:32 * g_ + 6, m, :], Lsc[:, h, :], ident[:],
                                tile_position=(0, 32 * g_))
                        return ins
                    P.emit("tensor", tps, waits=[("lk", P.val("lk")), ("jl", P.val("jl"))], inc="tk")
                    P.emit("vector", lambda v: v.tensor_copy(dest[:, :, col0:col0 + 128], pTk[:]),
                           waits=[("tk", P.val("tk"))])
                    P.emit("vector", lambda v: v.drain(), inc="jl")

                def emit_vmm(kt):
                    b = kt % 2
                    def vmm(te, b=b):
                        ins = None
                        for dc in range(8):
                            for (n0, n1) in ((0, 512), (512, 1024), (1024, 1152)):
                                ins = te.matmul(pV[b][:, n0:n1], xT[b][:, dc, :], wvks[:, dc, n0:n1],
                                                start=(dc == 0), stop=(dc == 7))
                        return ins
                    P.emit("tensor", vmm,
                           waits=[("xt", xt_count[kt]), ("ss", kt - 1), ("wA", W_SETUP)], inc="mmv")
                    P.emit("vector", lambda v, b=b, kt=kt: v.tensor_tensor(
                        Vs[:, kt, :, 0:64], pV[b][:, 0:1024], bvb[:], ALU.add),
                        waits=[("mmv", P.val("mmv"))])
                    P.emit("vector", lambda v: v.drain())

                tile_prep(0, x_kv, 0, xT, 0, [("mmv", -1)])
                for kt in range(NKT):
                    if kt + 1 < NKT:
                        tile_prep(kt + 1, x_kv, (kt + 1) * 128, xT, (kt + 1) % 2,
                                  [("mmv", kt)])
                    emit_vmm(kt)
                    ext_front(pV[kt % 2], 1024, True, JLKT, kt * 128)
                    if kt >= 1:
                        ext_back(kt - 1)
                P.emit("vector", lambda v: v.memset(Vs[:, :, :, 64:65], 1.0))
                P.emit("vector", lambda v: v.drain(), inc="jl")

                def emit_qmm(s_):
                    sb_ = s_ % 2
                    pvb = pV[s_ % 2]
                    def qmm(te, sb_=sb_, pvb=pvb):
                        ins = None
                        for kc in range(16):
                            lhs = xT[sb_][:, kc, :] if kc < 8 else xTP[sb_][:, kc - 8, :]
                            ins = te.matmul(pvb[:, 0:128], lhs, wq_sb[:, kc, :],
                                            start=(kc == 0), stop=(kc == 15))
                        return ins
                    P.emit("tensor", qmm,
                           waits=[("xt", xt_count[NKT + 2 * s_ + 1]), ("ss", NKT + s_ - 1), ("wA", W_SETUP)], inc="mmv")

                tile_prep(NKT, xq_h, 0, xT, 0, [("mmv", NKT - 1)])
                tile_prep(NKT + 1, xqp_h, 0, xTP, 0, [("mmv", NKT - 1)])
                for s_ in range(NSLOT):
                    if s_ + 1 < NSLOT:
                        i0 = NKT + 2 * (s_ + 1)
                        tile_prep(i0, xq_h, (s_ + 1) * 128, xT, (s_ + 1) % 2, [("mmv", NKT + s_)])
                        tile_prep(i0 + 1, xqp_h, (s_ + 1) * 128, xTP, (s_ + 1) % 2, [("mmv", NKT + s_)])
                    emit_qmm(s_)
                    ext_front(pV[s_ % 2], 0, False, LQT, s_ * 128)
                    ext_back(NKT + s_ - 1)
                ext_back(NKT + NSLOT - 1)

                if debug >= 3:
                    JL_D3 = P.val("jl")
                    P.emit("gpsimd", lambda g: g.dma_start(out=dbg["kk"], in_=kk_sb[:]),
                           waits=[("jl", JL_D3)], inc="dbgs", inc_n=16)
                    P.emit("gpsimd", lambda g: g.dma_start(out=dbg["lraw"], in_=Lraw[:]), inc="dbgs", inc_n=16)
                    P.emit("gpsimd", lambda g: g.dma_start(out=dbg["ssb"], in_=ssb[:]), inc="dbgs", inc_n=16)
                    P.emit("gpsimd", lambda g: g.dma_start(out=dbg["rnrm"], in_=rnrm[:]), inc="dbgs", inc_n=16)
                    P.emit("gpsimd", lambda g: g.dma_start(out=dbg["lsc"], in_=Lsc[:]), inc="dbgs", inc_n=16)
                if debug >= 1:
                    JL_D = P.val("jl")
                    P.emit("gpsimd", lambda g: g.dma_start(out=dbg["vs"], in_=Vs[:]),
                           waits=[("jl", JL_D)], inc="dbgs", inc_n=16)
                    P.emit("gpsimd", lambda g: g.dma_start(out=dbg["jlkt"], in_=JLKT[:]), inc="dbgs", inc_n=16)
                    P.emit("gpsimd", lambda g: g.dma_start(out=dbg["lqt"], in_=LQT[:]), inc="dbgs", inc_n=16)

                JL_ALL = P.val("jl")
                P.emit("sync", lambda sy: sy.sem_inc(sems["bar1"], 1), waits=[("x0", P.val("x0")), ("x1", P.val("x1"))])
                P.emit("tensor", lambda te: te.sem_inc(sems["bar1"], 1))
                P.emit("vector", lambda v: v.sem_inc(sems["bar1"], 1), waits=[("jl", JL_ALL)])
                P.emit("scalar", lambda s: s.sem_inc(sems["bar1"], 1), waits=[("rr", P.val("rr"))])
                P.emit("gpsimd", lambda g: g.sem_inc(sems["bar1"], 1),
                       waits=[("wA", W_SETUP), ("dbgs", P.val("dbgs"))])
                P.cnt["bar1"] = 5
                P.run_block(nc, sems)
                cnt1 = P.cnt

            if phases < 2:
                for c in sems_ctx:
                    c.__exit__(None, None, None)
                return nc
            with ExitStack() as _stB:
                sbB = lambda *a: _stB.enter_context(nc.sbuf_tensor(*a))
                psB = lambda *a: _stB.enter_context(nc.psum_tensor(*a))
                masks_g0 = sbB("masks_g0", [128, 4, 2, 128], BF16)
                attnT = sbB("attnT", [128, 8, 512], BF16)
                PT0 = sbB("PT0", [128, 512], BF16)
                PT1 = sbB("PT1", [128, 512], BF16)
                PT2 = sbB("PT2", [128, 512], BF16)
                PT3 = sbB("PT3", [128, 512], BF16)
                
                wo_sb = sbB("wo_sb", [128, 8, 1024], BF16)
                bob = sbB("bob", [128, D], F32)
                srow_b = sbB("srow_b", [1, 512], BF16)
                ones1b = sbB("ones1b", [1, 64], BF16)
                
                rbl_t = sbB("rbl", [64, 512], F32)
                rbr0 = sbB("rbr0", [64, 512], F32)
                rbr1 = sbB("rbr1", [64, 512], F32)
                xqr_t = sbB("xqr", [128, D], F32)
                y1a_t = sbB("y1a", [128, D], F32)
                y1b_t = sbB("y1b", [128, D], F32)
                xlo_t = sbB("xlo", [128, D], BF16)
                xfT_t = sbB("xfT_t", [128, 8, 128], BF16)
                xfTlo_t = sbB("xfTlo_t", [128, 8, 128], BF16)
                stq = sbB("stq", [128, 4], F32)
                rstd1 = sbB("rstd1", [128, 1], F32)
                rln1 = sbB("rln1", [128, 1], F32)
                e8_t = sbB("e8", [128, NE], F32)
                nmx_t = sbB("nmx", [128, 1], F32)
                s8_t = sbB("s8", [128, 1], F32)
                scr8_t = sbB("scr8", [128, NE], F32)
                st8_t = sbB("st8", [128, 8], F32)
                s2t_t = sbB("s2t", [128, NE], F32)
                mask8_t = sbB("mask8", [128, NE], F32)
                
                pS0 = psB("pS0", [128, 512], F32)
                pS1 = psB("pS1", [128, 512], F32)
                pS2 = psB("pS2", [128, 512], F32)
                pAV0 = psB("pAV0", [65, 512], F32)
                pAV1 = psB("pAV1", [65, 512], F32)
                
                pWO0 = psB("pWO0", [128, 1024], F32)
                
                pT2 = psB("pT2", [128, 8, 128], BF16)
                P = Prog(cnt1)
                PT = [PT0, PT1, PT2, PT3]
                pS, pAV = [pS0, pS1, pS2], [pAV0, pAV1]
                pWO = pWO0
                rbr = [rbr0, rbr1]
                masks_g = masks_g0

                P.emit("gpsimd", lambda g: g.dma_start(
                    out=wo_sb[:], in_=wo_h.rearrange("(dc p) n -> p dc n", p=128)), inc="wB", inc_n=16)
                P.emit("gpsimd", lambda g: g.dma_start(out=bob[:], in_=bob_h), inc="wB", inc_n=16)
                W_B = P.val("wB")
                P.emit("vector", lambda v: v.memset(ones1b[:], 1.0), waits=[("bar1", 5)])
                P.emit("vector", lambda v: v.drain(), inc="xt")
                ONES_RDY = P.val("xt")

                sbank_free = [None, None, None]
                ptslot_free = [None, None, None, None]
                avbank_free = [None, None]
                exp_serial = [0]
                wo_prev_g = 0

                def emit_score(g, h, kc):
                    gr, m = h % 4, h // 4
                    jlo = max(4 * g, kc // 2)
                    off = (jlo - 4 * g) * 128
                    sb = exp_serial[0] % 3
                    pslot = exp_serial[0] % 4
                    exp_serial[0] += 1
                    waits = []
                    if sbank_free[sb]:
                        waits.append(sbank_free[sb])
                    if h <= 1 and g == 0 and kc < 2:
                        waits.append(("jl", JL_ALL))
                    def scmm(te, sb=sb, off=off, gr=gr, m=m, kc=kc, g=g):
                        return te.matmul(
                            pS[sb][:, off:512],
                            JLKT[32 * gr:32 * gr + 6, m, kc * 128:(kc + 1) * 128],
                            LQT[32 * gr:32 * gr + 6, m, g * 512 + off:(g + 1) * 512],
                            start=True, stop=True, tile_position=(32 * gr, 0))
                    P.emit("tensor", scmm, waits=waits, inc="sc")
                    ncols = 512 - off
                    ew = [("sc", P.val("sc"))]
                    if ptslot_free[pslot]:
                        ew.append(ptslot_free[pslot])
                    P.emit("scalar", lambda s, sb=sb, off=off, pslot=pslot, ncols=ncols: s.activation(
                        PT[pslot][:, 0:ncols], pS[sb][:, off:512], AF.Exp, bias=0.0, scale=SCALE),
                        waits=ew, inc="exp")
                    sbank_free[sb] = ("exp", P.val("exp"))
                    masked = kc >= 8 * g
                    if masked:
                        jj, ii = kc // 2 - 4 * g, kc % 2
                        P.emit("vector", lambda v, pslot=pslot, jj=jj, ii=ii, g=g: v.tensor_mul(
                            PT[pslot][:, 0:128], PT[pslot][:, 0:128], masks_g[:, jj, ii, :]),
                            waits=[("exp", P.val("exp")), MSK_RDY[0]])
                        P.emit("vector", lambda v: v.drain(), inc="msk")
                    return (pslot, off, ncols, masked, P.val("exp"), P.val("msk"))

                def emit_av(g, h, kc, sinfo):
                    pslot, off, ncols, masked, expv, mskv = sinfo
                    hb = h % 2
                    K = kmax(g)
                    avw = [("msk", mskv)] if masked else [("exp", expv)]
                    if kc == 0 and avbank_free[hb]:
                        avw.append(avbank_free[hb])
                    def avmm(te, hb=hb, off=off, kc=kc, h=h, ncols=ncols, K=K, pslot=pslot):
                        return te.matmul(
                            pAV[hb][:, off:512], Vs[:, kc, h, :], PT[pslot][:, 0:ncols],
                            start=(kc == 0), stop=(kc == K - 1))
                    P.emit("tensor", avmm, waits=avw, inc="av")
                    ptslot_free[pslot] = ("av", P.val("av"))

                for g in range(NG):
                    av_counts, sr_vals, rbm_vals, rbr_vals = {}, {}, {}, {}
                    last_rbr = P.val("rbr")
                    P.emit("gpsimd", lambda gq, g=g: gq.dma_start(
                        out=masks_g[:], in_=masks_h[:, 4 * g:4 * g + 4, :, :]),
                        waits=[("bar1", 5), ("msk", P.cnt.get("_mskuse_%d" % (g - 1), 0))],
                        inc="mk0", inc_n=16)
                    MSK_RDY = [("mk0", P.val("mk0"))]
                    K = kmax(g)

                    def emit_chain_pre(hh):
                        hbb = hh % 2
                        srw = [("av", av_counts[hh])]
                        if rbm_vals:
                            srw.append(("rbm", max(rbm_vals.values())))
                        P.emit("vector", lambda v, hbb=hbb: v.tensor_copy(
                            srow_b[0:1, :], pAV[hbb][64:65, :]), waits=srw)
                        P.emit("vector", lambda v: v.drain(), inc="sr")
                        sr_vals[hh] = P.val("sr")
                        rw = [("sr", sr_vals[hh]), ("xt", ONES_RDY)]
                        prev_rbr = rbr_vals[hh - 1] if (hh - 1) in rbr_vals else last_rbr
                        if prev_rbr:
                            rw.append(("rbr", prev_rbr))
                        rw.append(("y1a", P.val("y1a")))
                        def rbmm(te):
                            return te.matmul(pWO[0:64, 512:1024], ones1b[:], srow_b[0:1, :], start=True, stop=True)
                        P.emit("tensor", rbmm, waits=rw, inc="rbm")
                        rbm_vals[hh] = P.val("rbm")

                    def emit_chain_act(hh):
                        hbb = hh % 2
                        P.emit("scalar", lambda s: s.activation(rbl_t[:], pWO[0:64, 512:1024], AF.Ln),
                               waits=[("rbm", rbm_vals[hh])])
                        P.emit("scalar", lambda s: s.drain())
                        P.emit("scalar", lambda s, hbb=hbb: s.activation(
                            rbr[hbb][:], rbl_t[:], AF.Exp, bias=0.0, scale=-1.0))
                        P.emit("scalar", lambda s: s.drain(), inc="rbr")
                        rbr_vals[hh] = P.val("rbr")

                    def emit_chain_post(hh):
                        hbb = hh % 2
                        cp, half = hh // 2, (hh % 2) * 64
                        P.emit("vector", lambda v, hbb=hbb, cp=cp, half=half: v.tensor_tensor(
                            attnT[half:half + 64, cp, :], pAV[hbb][0:64, :], rbr[hbb][:], ALU.mult),
                            waits=[("rbr", rbr_vals[hh]), ("wo", wo_prev_g)])
                        P.emit("vector", lambda v: v.drain(), inc="ate")
                        avbank_free[hbb] = ("ate", P.val("ate"))

                    def emit_chain(hh):
                        emit_chain_pre(hh)
                        emit_chain_act(hh)
                        emit_chain_post(hh)

                    for p in range(8):
                        a, b = 2 * p, 2 * p + 1
                        prev_a = prev_b = None
                        for kc in range(K):
                            sinfo_a = emit_score(g, a, kc)
                            sinfo_b = emit_score(g, b, kc)
                            if p >= 1:
                                if kc == 0:
                                    emit_chain_pre(2 * p - 2)
                                    emit_chain_act(2 * p - 2)
                                    emit_chain_pre(2 * p - 1)
                                    emit_chain_act(2 * p - 1)
                                elif kc == 1:
                                    emit_chain_post(2 * p - 2)
                                    emit_chain_post(2 * p - 1)
                            if kc >= 1:
                                emit_av(g, a, kc - 1, prev_a)
                                emit_av(g, b, kc - 1, prev_b)
                            prev_a, prev_b = sinfo_a, sinfo_b
                        emit_av(g, a, K - 1, prev_a)
                        emit_av(g, b, K - 1, prev_b)
                        av_counts[a] = P.val("av") - 1
                        av_counts[b] = P.val("av")
                    emit_chain(14)
                    emit_chain(14)
                    emit_chain(15)

                    ATE_G = P.val("ate")
                    for qq in range(4):
                        qt = 4 * g + qq
                        qb = qt % 2
                        P.emit("sync", lambda sy, qt=qt: sy.dma_start(
                            out=xqr_t[:], in_=xq_h[qt * 128:(qt + 1) * 128, :]),
                            waits=[("bar1", 5), ("y1a", P.val("y1a"))], inc="xqd", inc_n=16)
                        def womm(te, qq=qq):
                            ins = None
                            for cp in range(8):
                                for nb in range(2):
                                    ins = te.matmul(
                                        pWO[:, nb * 512:(nb + 1) * 512],
                                        attnT[:, cp, qq * 128:(qq + 1) * 128],
                                        wo_sb[:, cp, nb * 512:(nb + 1) * 512],
                                        start=(cp == 0), stop=(cp == 7))
                            return ins
                        wow = [("ate", ATE_G), ("wB", W_B), ("y1a", P.val("y1a"))]
                        P.emit("tensor", womm, waits=wow, inc="wo")
                        WO_T = P.val("wo")
                        P.emit("vector", lambda v: v.tensor_tensor(y1a_t[:], pWO[:], bob[:], ALU.add),
                               waits=[("wo", WO_T), ("xqd", P.val("xqd"))])
                        P.emit("vector", lambda v: v.drain())
                        P.emit("vector", lambda v: v.scalar_tensor_tensor(
                            y1b_t[:], y1a_t[:], 1.0, xqr_t[:], ALU.mult, ALU.add, accum_out=stq[:, 0:1]))
                        P.emit("vector", lambda v: v.drain(), inc="y1a")
                        Y1A = P.val("y1a")
                        P.emit("vector", lambda v: v.tensor_scalar(stq[:, 0:1], stq[:, 0:1], 1.0 / D, None, ALU.mult))
                        P.emit("vector", lambda v: v.drain())
                        P.emit("vector", lambda v: v.tensor_scalar(y1a_t[:], y1b_t[:], stq[:, 0:1], None, ALU.subtract))
                        P.emit("vector", lambda v: v.drain())
                        P.emit("vector", lambda v: v.scalar_tensor_tensor(
                            y1b_t[:], y1a_t[:], 1.0, y1a_t[:], ALU.mult, ALU.mult, accum_out=stq[:, 1:2]))
                        P.emit("vector", lambda v: v.drain(), inc="v1")
                        P.emit("scalar", lambda s: s.activation(rln1[:], stq[:, 1:2], AF.Ln,
                                                                bias=epsb[:, 0:1], scale=1.0 / D),
                               waits=[("v1", P.val("v1"))])
                        P.emit("scalar", lambda s: s.drain())
                        P.emit("scalar", lambda s: s.activation(rstd1[:], rln1[:], AF.Exp, bias=0.0, scale=-0.5))
                        P.emit("scalar", lambda s: s.drain(), inc="sd1")
                        P.emit("vector", lambda v: v.tensor_scalar(y1b_t[:], y1a_t[:], rstd1[:, 0:1], None, ALU.mult),
                               waits=[("sd1", P.val("sd1"))])
                        P.emit("vector", lambda v: v.drain())
                        P.emit("vector", lambda v, qt=qt: v.tensor_copy(acc_t[:, qt, :], y1b_t[:]))
                        P.emit("vector", lambda v: v.drain())
                        P.emit("vector", lambda v, qt=qt: v.tensor_tensor(xlo_t[:], y1b_t[:], acc_t[:, qt, :], ALU.subtract))
                        P.emit("vector", lambda v: v.drain(), inc="xf")
                        XF = P.val("xf")
                        def tph(te, qt=qt):
                            ins = None
                            for dc in range(8):
                                ins = te.transpose(pT2[:, dc, :], acc_t[:, qt, dc * 128:(dc + 1) * 128], ident[:])
                            return ins
                        P.emit("tensor", tph, waits=[("xf", XF), ("xft", P.val("xft"))], inc="tpf")
                        P.emit("vector", lambda v: v.tensor_copy(xfT_t[:], pT2[:]),
                               waits=[("tpf", P.val("tpf"))])
                        P.emit("vector", lambda v: v.drain(), inc="xft")
                        def tpl(te):
                            ins = None
                            for dc in range(8):
                                ins = te.transpose(pT2[:, dc, :], xlo_t[:, dc * 128:(dc + 1) * 128], ident[:])
                            return ins
                        P.emit("tensor", tpl, waits=[("xft", P.val("xft"))], inc="tpf")
                        P.emit("vector", lambda v: v.tensor_copy(xfTlo_t[:], pT2[:]),
                               waits=[("tpf", P.val("tpf"))])
                        P.emit("vector", lambda v: v.drain(), inc="xft")
                        XFT = P.val("xft")
                        rtw = [("xft", XFT)]
                        if sbank_free[qb]:
                            rtw.append(sbank_free[qb])
                        def rtmm(te, qb=qb):
                            ins = None
                            srcs = ((xfT_t, rtr_hi), (xfT_t, rtr_lo), (xfTlo_t, rtr_hi))
                            n = 0
                            for src, w in srcs:
                                for dc in range(8):
                                    ins = te.matmul(pS[qb][:, 0:8], src[:, dc, :], w[:, dc, :],
                                                    start=(n == 0), stop=(n == 23))
                                    n += 1
                            return ins
                        P.emit("tensor", rtmm, waits=rtw, inc="rt")
                        P.emit("vector", lambda v, qb=qb: v.reduce_max(
                            nmx_t[:], pS[qb][:, 0:8], axis=AX.X, negate=True),
                            waits=[("rt", P.val("rt"))])
                        P.emit("vector", lambda v: v.drain(), inc="rmax")
                        P.emit("scalar", lambda s, qb=qb: s.activation(
                            e8_t[:], pS[qb][:, 0:8], AF.Exp, bias=nmx_t[:, 0:1], scale=1.0),
                            waits=[("rmax", P.val("rmax"))])
                        P.emit("scalar", lambda s: s.drain(), inc="rexp")
                        sbank_free[qb] = ("rexp", P.val("rexp"))
                        P.emit("vector", lambda v: v.reduce_sum(s8_t[:], e8_t[:], axis=AX.X),
                               waits=[("rexp", P.val("rexp"))])
                        P.emit("vector", lambda v: v.drain())
                        P.emit("vector", lambda v: v.reciprocal(nmx_t[:], s8_t[:]))
                        P.emit("vector", lambda v: v.drain())
                        P.emit("vector", lambda v: v.tensor_scalar(scr8_t[:], e8_t[:], nmx_t[:, 0:1], None, ALU.mult))
                        P.emit("vector", lambda v: v.drain())
                        P.emit("vector", lambda v: v.max(st8_t[:], scr8_t[:]))
                        P.emit("vector", lambda v: v.drain())
                        P.emit("vector", lambda v: v.memset(st8_t[:, 2:8], -1.0e30))
                        P.emit("vector", lambda v: v.drain())
                        P.emit("vector", lambda v: v.match_replace(s2t_t[:], st8_t[:], scr8_t[:], -2.0e30))
                        P.emit("vector", lambda v: v.drain())
                        P.emit("vector", lambda v: v.tensor_single_scalar(mask8_t[:], s2t_t[:], -1.0e29, ALU.is_lt))
                        P.emit("vector", lambda v: v.drain())
                        P.emit("vector", lambda v: v.tensor_mul(mask8_t[:], mask8_t[:], scr8_t[:]))
                        P.emit("vector", lambda v: v.tensor_add(stq[:, 2:3], st8_t[:, 0:1], st8_t[:, 1:2]))
                        P.emit("vector", lambda v: v.drain())
                        P.emit("vector", lambda v: v.tensor_scalar(s2t_t[:, 0:1], stq[:, 2:3], 1.0, 1e-20, ALU.mult, ALU.add))
                        P.emit("vector", lambda v: v.drain())
                        P.emit("vector", lambda v: v.reciprocal(s8_t[:], s2t_t[:, 0:1]))
                        P.emit("vector", lambda v: v.drain())
                        P.emit("vector", lambda v, qt=qt: v.tensor_scalar(
                            comb_t[:, qt, :], mask8_t[:], s8_t[:, 0:1], None, ALU.mult))
                        P.emit("vector", lambda v: v.drain(), inc="cmb")

                    wo_prev_g = P.val("wo")
                    P.cnt["_mskuse_%d" % g] = P.val("msk")

                if debug >= 2:
                    P.emit("gpsimd", lambda g: g.dma_start(out=dbg["acc"], in_=acc_t[:]),
                           waits=[("xf", P.val("xf"))], inc="dbgs", inc_n=16)
                    P.emit("gpsimd", lambda g: g.dma_start(out=dbg["comb"], in_=comb_t[:]),
                           waits=[("cmb", P.val("cmb"))], inc="dbgs", inc_n=16)
                    P.emit("gpsimd", lambda g: g.dma_start(out=dbg["attnT"], in_=attnT[:]),
                           waits=[("ate", P.val("ate"))], inc="dbgs", inc_n=16)

                P.emit("sync", lambda sy: sy.sem_inc(sems["bar2"], 1), waits=[("xqd", P.val("xqd"))])
                P.emit("tensor", lambda te: te.sem_inc(sems["bar2"], 1))
                P.emit("vector", lambda v: v.sem_inc(sems["bar2"], 1), waits=[("cmb", P.val("cmb"))])
                P.emit("scalar", lambda s: s.sem_inc(sems["bar2"], 1))
                P.emit("gpsimd", lambda g: g.sem_inc(sems["bar2"], 1),
                       waits=[("wB", W_B), ("dbgs", P.val("dbgs"))])
                P.cnt["bar2"] = 5
                P.run_block(nc, sems)
                cnt2 = P.cnt

        if phases < 3:
            for c in sems_ctx:
                c.__exit__(None, None, None)
            return nc
        with ExitStack() as _stD:
            sbD = lambda *a: _stD.enter_context(nc.sbuf_tensor(*a))
            psD = lambda *a: _stD.enter_context(nc.psum_tensor(*a))
            xfT = sbD("xfT", [128, 8, NSLOT * 128], BF16)
            ht_sb = sbD("ht", [128, NHC, NSLOT * 128], BF16)
            wg_sb = sbD("wg_sb", [128, NHC * 8, 128], BF16)
            wu_sb = sbD("wu_sb", [128, NHC * 8, 128], BF16)
            wd_sb = sbD("wd_sb", [128, NHC, 1024], BF16)
            sG0 = sbD("sG0", [128, 512], BF16)
            sG1 = sbD("sG1", [128, 512], BF16)
            xc2 = sbD("xc2", [128, D], F32)
            sq2 = sbD("sq2", [128, D], F32)
            st2 = sbD("st2", [128, 4], F32)
            rstd2 = sbD("rstd2", [128, 1], F32)
            rln2 = sbD("rln2", [128, 1], F32)
            outst0 = sbD("outst0", [128, D], F32)
            outst1 = sbD("outst1", [128, D], F32)
            pG0 = psD("pG0", [128, 512], F32)
            pG1 = psD("pG1", [128, 512], F32)
            pU0 = psD("pU0", [128, 512], F32)
            pU1 = psD("pU1", [128, 512], F32)
            pD0 = psD("pD0", [128, 512], F32)
            pD1 = psD("pD1", [128, 512], F32)
            pD2 = psD("pD2", [128, 512], F32)
            
            pT3 = psD("pT3", [128, 8, 128], BF16)
            P = Prog(cnt2)
            pG, pU = [pG0, pG1], [pU0, pU1]
            pD = [pD0, pD1, pD2]
            pd_free = [None, None, None]
            sG = [sG0, sG1]
            outst = [outst0, outst1]

            for t in range(NSLOT):
                def tpx(te, t=t):
                    ins = None
                    for dc in range(8):
                        ins = te.transpose(pT3[:, dc, :], acc_t[:, t, dc * 128:(dc + 1) * 128], ident[:])
                    return ins
                P.emit("tensor", tpx, waits=[("bar2", 5), ("xft", P.val("xft"))], inc="tpf")
                P.emit("vector", lambda v, t=t: v.tensor_copy(xfT[:, :, t * 128:(t + 1) * 128], pT3[:]),
                       waits=[("bar2", 5), ("tpf", P.val("tpf"))])
                P.emit("vector", lambda v: v.drain(), inc="xft")
            XFT_ALL = P.val("xft")

            for e in range(NE):
                P.emit("gpsimd", lambda g, e=e: g.dma_start(
                    out=wg_sb[:], in_=wg_h[e].rearrange("(dc p) (hc q) -> p hc dc q", p=128, q=128)),
                    waits=[("bar2", 5), ("gmm", 44 * e)], inc="wg", inc_n=16)
                P.emit("gpsimd", lambda g, e=e: g.dma_start(
                    out=wu_sb[:], in_=wu_h[e].rearrange("(dc p) (hc q) -> p hc dc q", p=128, q=128)),
                    waits=[("bar2", 5), ("umm", 44 * e)], inc="wu", inc_n=16)
                P.emit("gpsimd", lambda g, e=e: g.dma_start(
                    out=wd_sb[:], in_=wd_h[e].rearrange("(hc p) n -> p hc n", p=128)),
                    waits=[("bar2", 5), ("dmm", 16 * e)], inc="wd", inc_n=16)

            for e in range(NE):
                for tc in range(4):
                    for hc in range(NHC):
                        gi = (tc * NHC + hc) % 2
                        def gmm(te, hc=hc, tc=tc, gi=gi, e=e):
                            ins = None
                            for dc in range(8):
                                ins = te.matmul(pG[gi][:], wg_sb[:, hc * 8 + dc, :],
                                                xfT[:, dc, tc * 512:(tc + 1) * 512],
                                                start=(dc == 0), stop=(dc == 7))
                            return ins
                        P.emit("tensor", gmm,
                               waits=[("wg", 16 * (e + 1)), ("xft", XFT_ALL),
                                      ("sg", P.val("sg") - 1)], inc="gmm")
                        def umm(te, hc=hc, tc=tc, gi=gi, e=e):
                            ins = None
                            for dc in range(8):
                                ins = te.matmul(pU[gi][:], wu_sb[:, hc * 8 + dc, :],
                                                xfT[:, dc, tc * 512:(tc + 1) * 512],
                                                start=(dc == 0), stop=(dc == 7))
                            return ins
                        P.emit("tensor", umm,
                               waits=[("wu", 16 * (e + 1)), ("hm", P.val("hm") - 1)], inc="umm")
                        P.emit("scalar", lambda s, gi=gi: s.activation(sG[gi][:], pG[gi][:], AF.Silu),
                               waits=[("gmm", P.val("gmm")), ("hm", P.val("hm") - 1)])
                        P.emit("scalar", lambda s: s.drain(), inc="sg")
                        P.emit("vector", lambda v, gi=gi, hc=hc, tc=tc: v.tensor_tensor(
                            ht_sb[:, hc, tc * 512:(tc + 1) * 512], sG[gi][:], pU[gi][:], ALU.mult),
                            waits=[("sg", P.val("sg")), ("umm", P.val("umm"))])
                        P.emit("vector", lambda v: v.drain(), inc="hm")
                HM_E = P.val("hm")
                for t in range(NSLOT):
                    ba, bb_ = (2 * t) % 3, (2 * t + 1) % 3
                    def dmm(te, t=t, ba=ba, bb_=bb_):
                        ins = None
                        for hc in range(NHC):
                            te.matmul(pD[ba][:], ht_sb[:, hc, t * 128:(t + 1) * 128],
                                      wd_sb[:, hc, 0:512],
                                      start=(hc == 0), stop=(hc == NHC - 1))
                            ins = te.matmul(pD[bb_][:], ht_sb[:, hc, t * 128:(t + 1) * 128],
                                            wd_sb[:, hc, 512:1024],
                                            start=(hc == 0), stop=(hc == NHC - 1))
                        return ins
                    dw = [("wd", 16 * (e + 1)), ("hm", HM_E)]
                    for bk in (ba, bb_):
                        if pd_free[bk]:
                            dw.append(pd_free[bk])
                    P.emit("tensor", dmm, waits=dw, inc="dmm")
                    if e == NE - 1:
                        P.emit("vector", lambda v, t=t, ba=ba: v.scalar_tensor_tensor(
                            acc_t[:, t, 0:512], pD[ba][:], comb_t[:, t, e:e + 1], acc_t[:, t, 0:512],
                            ALU.mult, ALU.add, accum_out=st2[:, 0:1]),
                            waits=[("dmm", P.val("dmm"))])
                        P.emit("vector", lambda v: v.drain())
                        P.emit("vector", lambda v, t=t, bb_=bb_: v.scalar_tensor_tensor(
                            acc_t[:, t, 512:1024], pD[bb_][:], comb_t[:, t, e:e + 1], acc_t[:, t, 512:1024],
                            ALU.mult, ALU.add, accum_out=st2[:, 2:3]),
                            waits=[("dmm", P.val("dmm"))])
                    else:
                        P.emit("vector", lambda v, e=e, t=t, ba=ba: v.scalar_tensor_tensor(
                            acc_t[:, t, 0:512], pD[ba][:], comb_t[:, t, e:e + 1], acc_t[:, t, 0:512],
                            ALU.mult, ALU.add),
                            waits=[("dmm", P.val("dmm"))])
                        P.emit("vector", lambda v: v.drain())
                        P.emit("vector", lambda v, e=e, t=t, bb_=bb_: v.scalar_tensor_tensor(
                            acc_t[:, t, 512:1024], pD[bb_][:], comb_t[:, t, e:e + 1], acc_t[:, t, 512:1024],
                            ALU.mult, ALU.add),
                            waits=[("dmm", P.val("dmm"))])
                    P.emit("vector", lambda v: v.drain(), inc="acc")
                    pd_free[ba] = ("acc", P.val("acc"))
                    pd_free[bb_] = ("acc", P.val("acc"))
                    if e == NE - 1:
                        ob = t % 2
                        P.emit("vector", lambda v: v.tensor_add(st2[:, 0:1], st2[:, 0:1], st2[:, 2:3]),
                               waits=[("acc", P.val("acc"))])
                        P.emit("vector", lambda v: v.drain())
                        P.emit("vector", lambda v: v.tensor_scalar(st2[:, 0:1], st2[:, 0:1], 1.0 / D, None, ALU.mult))
                        P.emit("vector", lambda v: v.drain())
                        P.emit("vector", lambda v, t=t: v.tensor_scalar(xc2[:], acc_t[:, t, :], st2[:, 0:1], None, ALU.subtract))
                        P.emit("vector", lambda v: v.drain())
                        P.emit("vector", lambda v: v.scalar_tensor_tensor(
                            sq2[:], xc2[:], 1.0, xc2[:], ALU.mult, ALU.mult, accum_out=st2[:, 1:2]))
                        P.emit("vector", lambda v: v.drain(), inc="ln2")
                        P.emit("scalar", lambda s: s.activation(rln2[:], st2[:, 1:2], AF.Ln,
                                                                bias=epsb[:, 0:1], scale=1.0 / D),
                               waits=[("ln2", P.val("ln2"))])
                        P.emit("scalar", lambda s: s.drain())
                        P.emit("scalar", lambda s: s.activation(rstd2[:], rln2[:], AF.Exp, bias=0.0, scale=-0.5))
                        P.emit("scalar", lambda s: s.drain(), inc="sd2")
                        P.emit("vector", lambda v, ob=ob: v.tensor_scalar(
                            outst[ob][:], xc2[:], rstd2[:, 0:1], None, ALU.mult),
                            waits=[("sd2", P.val("sd2")), ("out", 16 * t)])
                        P.emit("vector", lambda v: v.drain(), inc="eo")
                        P.emit("sync", lambda sy, t=t, ob=ob: sy.dma_start(
                            out=out_h[t * 128:(t + 1) * 128, :], in_=outst[ob][:]),
                            waits=[("eo", P.val("eo"))], inc="out", inc_n=16)

            P.emit("sync", lambda sy: sy.nop(), waits=[("out", 16 * NSLOT)])
            P.run_block(nc, sems)

        for c in sems_ctx:
            c.__exit__(None, None, None)
    return nc



def _make_masks(tiles):
    m = np.zeros((128, NSLOT, 2, 128), np.float32)
    tri = (np.arange(128)[:, None] <= np.arange(128)[None, :]).astype(np.float32)
    for j, t in enumerate(tiles):
        c = t + 1
        for i in range(2):
            kc = 2 * j + i
            if kc < c - 1:
                m[:, j, i, :] = 1.0
            elif kc == c - 1:
                m[:, j, i, :] = tri
    return np.ascontiguousarray(m.astype(ml_dtypes.bfloat16))


def kernel(L0_attn_x, L0_attn_W1_q, L0_attn_W2_q, L0_attn_W1_k, L0_attn_W2_k,
           L0_attn_W_v, L0_attn_b_v, L0_attn_W_o, L0_attn_b_o,
           L0_ffn_w_router, L0_ffn_w_gate, L0_ffn_w_up, L0_ffn_w_down,
           _debug=0):
    x = np.ascontiguousarray(np.asarray(L0_attn_x, dtype=np.float32))
    arrs = {
        "W1_q": L0_attn_W1_q, "W2_q": L0_attn_W2_q, "W1_k": L0_attn_W1_k,
        "W2_k": L0_attn_W2_k, "W_v": L0_attn_W_v, "W_o": L0_attn_W_o,
        "w_router": L0_ffn_w_router, "w_gate": L0_ffn_w_gate,
        "w_up": L0_ffn_w_up, "w_down": L0_ffn_w_down,
    }
    arrs = {k: np.ascontiguousarray(np.asarray(v, dtype=np.float32)) for k, v in arrs.items()}
    bv = np.asarray(L0_attn_b_v, dtype=np.float32)
    bo = np.asarray(L0_attn_b_o, dtype=np.float32)
    arrs["b_v_b"] = np.ascontiguousarray(np.broadcast_to(bv, (128, D)))
    arrs["b_o_b"] = np.ascontiguousarray(np.broadcast_to(bo, (128, D)))
    arrs["ident_hbm"] = np.eye(128, dtype=np.float32)
    masks = {0: _make_masks(TILES_A), 1: _make_masks(TILES_B)}

    in_maps = []
    for c in range(8):
        b, v_ = c // 2, c % 2
        tiles = TILES_A if v_ == 0 else TILES_B
        rows = np.concatenate([np.arange(t * 128, (t + 1) * 128) for t in tiles])
        xb = x[b]
        prev = rows - 1
        xqp = np.where((prev >= 0)[:, None], xb[np.maximum(prev, 0)], np.float32(0.0))
        m = dict(arrs)
        m["x_kv"] = xb
        m["xq"] = np.ascontiguousarray(xb[rows])
        m["xq_prev"] = np.ascontiguousarray(xqp.astype(np.float32))
        m["masks"] = masks[v_]
        in_maps.append(m)

    nc = build_program(debug=_debug, phases=_PHASES)
    res = run_bass_kernel_spmd(nc, in_maps, core_ids=list(range(8)))

    out = np.zeros((B, T, D), np.float32)
    for c in range(8):
        b, v_ = c // 2, c % 2
        tiles = TILES_A if v_ == 0 else TILES_B
        oq = np.asarray(res.results[c]["out_q"])
        for j, t in enumerate(tiles):
            out[b, t * 128:(t + 1) * 128, :] = oq[j * 128:(j + 1) * 128, :]
    if _debug:
        return out, res
    return out


# version 18
# speedup vs baseline: 1.0785x; 1.0352x over previous
from contextlib import ExitStack

import numpy as np
import ml_dtypes
import concourse.bass as bass
import concourse.mybir as mybir
from concourse.bass_utils import run_bass_kernel_spmd

F32 = mybir.dt.float32
BF16 = mybir.dt.bfloat16
AF = mybir.ActivationFunctionType
ALU = mybir.AluOpType
AX = mybir.AxisListType

B, T, D, H, DH, C6 = 4, 4096, 1024, 16, 64, 6
NKT = 32
NSLOT = 16
NG = 4
HFF = 1408
NE = 8
NHC = 11
SCALE = float(6.0 ** -0.5)
PAIRS = [(0, 1), (0, 2), (0, 3), (1, 2), (1, 3), (2, 3)]
JSIGN = [1.0, -1.0, 1.0, 1.0, -1.0, 1.0]
TILES_A = [0, 3, 4, 7, 8, 11, 12, 15, 16, 19, 20, 23, 24, 27, 28, 31]
TILES_B = [1, 2, 5, 6, 9, 10, 13, 14, 17, 18, 21, 22, 25, 26, 29, 30]
ENGINES = ("sync", "tensor", "vector", "scalar", "gpsimd")
_PHASES = 3
_SKIP_Q = False


def kmax(g):
    return 8 * g + 8


class Prog:

    def __init__(self, cnt=None):
        self.ops = {e: [] for e in ENGINES}
        self.cnt = dict(cnt) if cnt else {}

    def val(self, sem):
        return self.cnt.get(sem, 0)

    def emit(self, eng, fn, waits=(), inc=None, inc_n=1):
        w = [(s, v) for (s, v) in waits if v > 0]
        self.ops[eng].append((fn, w, inc, inc_n))
        if inc is not None:
            self.cnt[inc] = self.cnt.get(inc, 0) + inc_n

    def replay(self, eng_name, handle, sems):
        for fn, waits, inc, inc_n in self.ops[eng_name]:
            for s, v in waits:
                handle.wait_ge(sems[s], v)
            ins = fn(handle)
            if inc is not None:
                if ins is None:
                    raise RuntimeError(f"op with inc {inc} returned no instruction")
                ins.then_inc(sems[inc], inc_n)

    def run_block(self, nc, sems):
        with nc.Block() as blk:
            blk.sync(lambda e: self.replay("sync", e, sems))
            blk.tensor(lambda e: self.replay("tensor", e, sems))
            blk.vector(lambda e: self.replay("vector", e, sems))
            blk.scalar(lambda e: self.replay("scalar", e, sems))
            blk.gpsimd(lambda e: self.replay("gpsimd", e, sems))


def build_program(debug=0, phases=3):
    nc = bass.Bass("TRN2", target_bir_lowering=False)

    x_kv = nc.dram_tensor("x_kv", [T, D], F32, kind="ExternalInput").ap()
    xq_h = nc.dram_tensor("xq", [NSLOT * 128, D], F32, kind="ExternalInput").ap()
    xqp_h = nc.dram_tensor("xq_prev", [NSLOT * 128, D], F32, kind="ExternalInput").ap()
    masks_h = nc.dram_tensor("masks", [128, NSLOT, 2, 128], BF16, kind="ExternalInput").ap()
    ident_h = nc.dram_tensor("ident_hbm", [128, 128], F32, kind="ExternalInput").ap()
    w1q_h = nc.dram_tensor("W1_q", [2 * D, 64], F32, kind="ExternalInput").ap()
    w2q_h = nc.dram_tensor("W2_q", [2 * D, 64], F32, kind="ExternalInput").ap()
    w1k_h = nc.dram_tensor("W1_k", [D, 64], F32, kind="ExternalInput").ap()
    w2k_h = nc.dram_tensor("W2_k", [D, 64], F32, kind="ExternalInput").ap()
    wv_h = nc.dram_tensor("W_v", [D, D], F32, kind="ExternalInput").ap()
    bvb_h = nc.dram_tensor("b_v_b", [128, D], F32, kind="ExternalInput").ap()
    wo_h = nc.dram_tensor("W_o", [D, D], F32, kind="ExternalInput").ap()
    bob_h = nc.dram_tensor("b_o_b", [128, D], F32, kind="ExternalInput").ap()
    wr_h = nc.dram_tensor("w_router", [D, NE], F32, kind="ExternalInput").ap()
    wg_h = nc.dram_tensor("w_gate", [NE, D, HFF], F32, kind="ExternalInput").ap()
    wu_h = nc.dram_tensor("w_up", [NE, D, HFF], F32, kind="ExternalInput").ap()
    wd_h = nc.dram_tensor("w_down", [NE, HFF, D], F32, kind="ExternalInput").ap()
    out_h = nc.dram_tensor("out_q", [NSLOT * 128, D], F32, kind="ExternalOutput").ap()

    dbg = {}
    if debug >= 1:
        dbg["vs"] = nc.dram_tensor("dbg_vs", [128, NKT, H, 65], F32, kind="ExternalOutput").ap()
        dbg["jlkt"] = nc.dram_tensor("dbg_jlkt", [128, 4, NKT * 128], F32, kind="ExternalOutput").ap()
        dbg["lqt"] = nc.dram_tensor("dbg_lqt", [128, 4, NSLOT * 128], F32, kind="ExternalOutput").ap()
    if debug >= 3:
        dbg["kk"] = nc.dram_tensor("dbg_kk", [128, 128], F32, kind="ExternalOutput").ap()
        dbg["lraw"] = nc.dram_tensor("dbg_lraw", [128, H, C6], F32, kind="ExternalOutput").ap()
        dbg["ssb"] = nc.dram_tensor("dbg_ssb", [128, H], F32, kind="ExternalOutput").ap()
        dbg["rnrm"] = nc.dram_tensor("dbg_rnrm", [128, H], F32, kind="ExternalOutput").ap()
        dbg["lsc"] = nc.dram_tensor("dbg_lsc", [128, H, C6], F32, kind="ExternalOutput").ap()
    if debug >= 2:
        dbg["acc"] = nc.dram_tensor("dbg_acc", [128, NSLOT, D], F32, kind="ExternalOutput").ap()
        dbg["comb"] = nc.dram_tensor("dbg_comb", [128, NSLOT, NE], F32, kind="ExternalOutput").ap()
        dbg["attnT"] = nc.dram_tensor("dbg_attnT", [128, 8, 512], F32, kind="ExternalOutput").ap()

    sem_names = [
        "wA", "z0", "x0", "x1", "cast", "tp", "xt", "mmv", "ss", "rr", "lk", "tk", "jl",
        "bar1", "wB", "mk0", "mk1", "sc", "exp", "msk", "av", "sr", "rbm", "rbf", "rbr",
        "ate", "wo", "xqd", "y1a", "v1", "sd1", "xf", "tpf", "xft", "rt",
        "rmax", "rexp", "cmb", "bar2", "wg", "wu", "wd", "gmm", "umm", "sg",
        "hm", "dmm", "acc", "ln2", "sd2", "eo", "out", "dbgs",
    ]

    with ExitStack() as _st0:
        sb = lambda *a: _st0.enter_context(nc.sbuf_tensor(*a))
        ident = sb("ident", [128, 128], BF16)
        acc_t = sb("acc", [128, NSLOT, D], BF16)
        comb_t = sb("comb", [128, NSLOT, NE], F32)
        rtr_f = sb("rtr_f", [128, 8, NE], F32)
        rtr_hi = sb("rtr_hi", [128, 8, NE], BF16)
        rtr_lo = sb("rtr_lo", [128, 8, NE], BF16)
        epsb = sb("epsb", [128, 1], F32)
        sems_ctx = [nc.semaphore(n) for n in sem_names]
        sems = dict(zip(sem_names, (c.__enter__() for c in sems_ctx)))

        with ExitStack() as _st1:
            sb1 = lambda *a: _st1.enter_context(nc.sbuf_tensor(*a))
            Vs = sb1("Vs", [128, NKT, H, 65], BF16)
            JLKT = sb1("JLKT", [128, 4, NKT * 128], BF16)
            LQT = sb1("LQT", [128, 4, NSLOT * 128], BF16)
            with ExitStack() as _stA:
                sbA = lambda *a: _stA.enter_context(nc.sbuf_tensor(*a))
                psA = lambda *a: _stA.enter_context(nc.psum_tensor(*a))
                wvks = sbA("wvks", [128, 8, 1152], BF16)
                wq_sb = sbA("wq_sb", [128, 16, 128], BF16)
                bvb = sbA("bvb", [128, D], F32)
                xa0 = sbA("xa0", [128, D], F32)
                xa1 = sbA("xa1", [128, D], F32)
                xab0 = sbA("xab0", [128, D], BF16)
                xab1 = sbA("xab1", [128, D], BF16)
                xT0 = sbA("xT0", [128, 8, 128], BF16)
                xT1 = sbA("xT1", [128, 8, 128], BF16)
                xTP0 = sbA("xTP0", [128, 8, 128], BF16)
                xTP1 = sbA("xTP1", [128, 8, 128], BF16)
                Lraw0 = sbA("Lraw0", [128, H, C6], F32)
                Lraw1 = sbA("Lraw1", [128, H, C6], F32)
                Lsq = sbA("Lsq", [128, H, C6], F32)
                prodAll = sbA("prodAll", [128, 12, H], F32)
                sst = sbA("sst", [128, 3, H], F32)
                ssb = sbA("ssb", [128, H], F32)
                rnrm0 = sbA("rnrm0", [128, H], F32)
                rnrm1 = sbA("rnrm1", [128, H], F32)
                nrnrm = sbA("nrnrm", [128, H], F32)
                rln = sbA("rln", [128, H], F32)
                Lsc = sbA("Lsc", [128, H, C6], BF16)
                kk_sb = sbA("kk_sb", [128, 128], F32)
                zb = sbA("zb", [128, 128], BF16)
                pT = psA("pT", [128, 8, 128], BF16)
                pTk = psA("pTk", [128, 4, 128], BF16)
                pV0 = psA("pV0", [128, 1152], F32)
                pV1 = psA("pV1", [128, 1152], F32)
                P = Prog()
                xa, xab, xT, xTP = [xa0, xa1], [xab0, xab1], [xT0, xT1], [xTP0, xTP1]
                pV = [pV0, pV1]

                P.emit("gpsimd", lambda g: g.dma_start(out=ident[:], in_=ident_h), inc="wA", inc_n=16)
                P.emit("gpsimd", lambda g: g.dma_start(
                    out=wvks[:, :, 0:1024], in_=wv_h.rearrange("(dc p) n -> p dc n", p=128)), inc="wA", inc_n=16)
                P.emit("gpsimd", lambda g: g.dma_start(
                    out=wvks[:, :, 1024:1088], in_=w1k_h.rearrange("(dc p) n -> p dc n", p=128)), inc="wA", inc_n=16)
                P.emit("gpsimd", lambda g: g.dma_start(
                    out=wvks[:, :, 1088:1152], in_=w2k_h.rearrange("(dc p) n -> p dc n", p=128)), inc="wA", inc_n=16)
                P.emit("gpsimd", lambda g: g.dma_start(
                    out=wq_sb[:, :, 0:64], in_=w1q_h.rearrange("(kc p) n -> p kc n", p=128)), inc="wA", inc_n=16)
                P.emit("gpsimd", lambda g: g.dma_start(
                    out=wq_sb[:, :, 64:128], in_=w2q_h.rearrange("(kc p) n -> p kc n", p=128)), inc="wA", inc_n=16)
                P.emit("gpsimd", lambda g: g.dma_start(out=bvb[:], in_=bvb_h), inc="wA", inc_n=16)
                P.emit("gpsimd", lambda g: g.dma_start(
                    out=rtr_f[:], in_=wr_h.rearrange("(dc p) n -> p dc n", p=128)), inc="wA", inc_n=16)
                W_SETUP = P.val("wA")

                P.emit("vector", lambda v: v.memset(epsb[:], 1e-5))
                P.emit("vector", lambda v: v.drain())
                P.emit("vector", lambda v: v.memset(zb[:], 0.0))
                P.emit("vector", lambda v: v.drain(), inc="z0")
                def ztps(te):
                    ins = None
                    for m in range(4):
                        ins = te.transpose(pTk[:, m, :], zb[:], ident[:])
                    return ins
                P.emit("tensor", ztps, waits=[("z0", 1), ("wA", W_SETUP)], inc="tk")
                P.emit("vector", lambda v: v.tensor_copy(rtr_hi[:], rtr_f[:]), waits=[("wA", W_SETUP)])
                P.emit("vector", lambda v: v.drain())
                P.emit("vector", lambda v: v.tensor_tensor(rtr_lo[:], rtr_f[:], rtr_hi[:], ALU.subtract))
                P.emit("vector", lambda v: v.drain())


                xt_count = {}

                def tile_prep(idx, src_ap, row0, xT_list, xT_slot, xT_free):
                    b = idx % 2
                    xs = "x%d" % b
                    P.emit("sync", lambda sy, s=src_ap, r=row0, bb=b: sy.dma_start(
                        out=xa[bb][:], in_=s[r:r + 128, :]),
                        waits=[("cast", idx - 1)], inc=xs, inc_n=16)
                    P.emit("vector", lambda v, bb=b: v.tensor_copy(xab[bb][:], xa[bb][:]),
                           waits=[(xs, 16 * (idx // 2 + 1)), ("tp", idx - 1)])
                    P.emit("vector", lambda v: v.drain(), inc="cast")
                    def tps(te, bb=b):
                        ins = None
                        for dc in range(8):
                            ins = te.transpose(pT[:, dc, :], xab[bb][:, dc * 128:(dc + 1) * 128], ident[:])
                        return ins
                    P.emit("tensor", tps, waits=[("cast", idx + 1), ("xt", idx)], inc="tp")
                    P.emit("vector", lambda v, sl=xT_slot: v.tensor_copy(xT_list[sl][:], pT[:]),
                           waits=[("tp", idx + 1)] + list(xT_free))
                    P.emit("vector", lambda v: v.drain(), inc="xt")
                    xt_count[idx] = P.val("xt")

                Lraw = [Lraw0, Lraw1]
                rnrm = [rnrm0, rnrm1]
                ext_state = {}
                ext_serial = [0]

                def ext_front(pv_t, off0, jside, dest, col0):
                    se = ext_serial[0]
                    ext_serial[0] += 1
                    lb = se % 2
                    P.emit("vector", lambda v: v.tensor_copy(kk_sb[:], pv_t[:, off0:off0 + 128]),
                           waits=[("mmv", P.val("mmv"))])
                    P.emit("vector", lambda v: v.drain())
                    o1, o2 = 0, 64
                    for c, (i, j) in enumerate(PAIRS):
                        P.emit("vector", lambda v, i=i, j=j, c=c: v.tensor_mul(
                            prodAll[:, 2 * c, :], kk_sb[:, o1 + i: o1 + 64: 4], kk_sb[:, o2 + j: o2 + 64: 4]))
                        P.emit("vector", lambda v, i=i, j=j, c=c: v.tensor_mul(
                            prodAll[:, 2 * c + 1, :], kk_sb[:, o1 + j: o1 + 64: 4], kk_sb[:, o2 + i: o2 + 64: 4]))
                    P.emit("vector", lambda v: v.drain())
                    for c in range(C6):
                        P.emit("vector", lambda v, c=c, lb=lb: v.tensor_tensor(
                            Lraw[lb][:, :, c], prodAll[:, 2 * c, :], prodAll[:, 2 * c + 1, :], ALU.subtract))
                    P.emit("vector", lambda v: v.drain())
                    P.emit("vector", lambda v, lb=lb: v.tensor_tensor(Lsq[:], Lraw[lb][:], Lraw[lb][:], ALU.mult),
                           waits=[("rr", P.val("rr"))])
                    P.emit("vector", lambda v: v.drain())
                    for k in range(3):
                        P.emit("vector", lambda v, k=k: v.tensor_add(
                            sst[:, k, :], Lsq[:, :, 2 * k], Lsq[:, :, 2 * k + 1]))
                    P.emit("vector", lambda v: v.drain())
                    P.emit("vector", lambda v: v.tensor_add(sst[:, 0, :], sst[:, 0, :], sst[:, 1, :]))
                    P.emit("vector", lambda v: v.drain())
                    P.emit("vector", lambda v: v.tensor_add(sst[:, 1, :], sst[:, 0, :], sst[:, 2, :]))
                    P.emit("vector", lambda v: v.drain())
                    P.emit("vector", lambda v: v.tensor_scalar_max(ssb[:], sst[:, 1, :], 1e-24))
                    P.emit("vector", lambda v: v.drain(), inc="ss")
                    P.emit("scalar", lambda s: s.activation(rln[:], ssb[:], AF.Ln),
                           waits=[("ss", P.val("ss")), ("lk", P.val("lk"))])
                    P.emit("scalar", lambda s: s.drain())
                    P.emit("scalar", lambda s, lb=lb: s.activation(rnrm[lb][:], rln[:], AF.Exp, bias=0.0, scale=-0.5))
                    P.emit("scalar", lambda s: s.drain(), inc="rr")
                    ext_state[se] = (jside, dest, col0, lb)

                def ext_back(se):
                    jside, dest, col0, lb = ext_state.pop(se)
                    P.emit("vector", lambda v, lb=lb: v.tensor_scalar_mul(nrnrm[:], rnrm[lb][:], -1.0),
                           waits=[("rr", se + 1), ("tk", P.val("tk"))])
                    P.emit("vector", lambda v: v.drain())
                    for c in range(C6):
                        if jside:
                            oc = 5 - c
                            m = rnrm[lb] if JSIGN[oc] > 0 else nrnrm
                            P.emit("vector", lambda v, c=c, oc=oc, m=m, lb=lb: v.tensor_tensor(
                                Lsc[:, :, oc], Lraw[lb][:, :, c], m[:], ALU.mult))
                        else:
                            P.emit("vector", lambda v, c=c, lb=lb: v.tensor_tensor(
                                Lsc[:, :, c], Lraw[lb][:, :, c], rnrm[lb][:], ALU.mult))
                    P.emit("vector", lambda v: v.drain(), inc="lk")
                    head_transposes(dest, col0)

                def head_transposes(dest, col0):
                    def tps(te):
                        ins = None
                        for h in range(H):
                            g_, m = h % 4, h // 4
                            ins = te.transpose(
                                pTk[32 * g_:32 * g_ + 6, m, :], Lsc[:, h, :], ident[:],
                                tile_position=(0, 32 * g_))
                        return ins
                    P.emit("tensor", tps, waits=[("lk", P.val("lk")), ("jl", P.val("jl"))], inc="tk")
                    P.emit("vector", lambda v: v.tensor_copy(dest[:, :, col0:col0 + 128], pTk[:]),
                           waits=[("tk", P.val("tk"))])
                    P.emit("vector", lambda v: v.drain(), inc="jl")

                def emit_vmm(kt):
                    b = kt % 2
                    def vmm(te, b=b):
                        ins = None
                        for dc in range(8):
                            for (n0, n1) in ((0, 512), (512, 1024), (1024, 1152)):
                                ins = te.matmul(pV[b][:, n0:n1], xT[b][:, dc, :], wvks[:, dc, n0:n1],
                                                start=(dc == 0), stop=(dc == 7))
                        return ins
                    P.emit("tensor", vmm,
                           waits=[("xt", xt_count[kt]), ("ss", kt - 1), ("wA", W_SETUP)], inc="mmv")
                    P.emit("vector", lambda v, b=b, kt=kt: v.tensor_tensor(
                        Vs[:, kt, :, 0:64], pV[b][:, 0:1024], bvb[:], ALU.add),
                        waits=[("mmv", P.val("mmv"))])
                    P.emit("vector", lambda v: v.drain())

                tile_prep(0, x_kv, 0, xT, 0, [("mmv", -1)])
                for kt in range(NKT):
                    if kt + 1 < NKT:
                        tile_prep(kt + 1, x_kv, (kt + 1) * 128, xT, (kt + 1) % 2,
                                  [("mmv", kt)])
                    emit_vmm(kt)
                    ext_front(pV[kt % 2], 1024, True, JLKT, kt * 128)
                    if kt >= 1:
                        ext_back(kt - 1)
                P.emit("vector", lambda v: v.memset(Vs[:, :, :, 64:65], 1.0))
                P.emit("vector", lambda v: v.drain(), inc="jl")

                def emit_qmm(s_):
                    sb_ = s_ % 2
                    pvb = pV[s_ % 2]
                    def qmm(te, sb_=sb_, pvb=pvb):
                        ins = None
                        for kc in range(16):
                            lhs = xT[sb_][:, kc, :] if kc < 8 else xTP[sb_][:, kc - 8, :]
                            ins = te.matmul(pvb[:, 0:128], lhs, wq_sb[:, kc, :],
                                            start=(kc == 0), stop=(kc == 15))
                        return ins
                    P.emit("tensor", qmm,
                           waits=[("xt", xt_count[NKT + 2 * s_ + 1]), ("ss", NKT + s_ - 1), ("wA", W_SETUP)], inc="mmv")

                tile_prep(NKT, xq_h, 0, xT, 0, [("mmv", NKT - 1)])
                tile_prep(NKT + 1, xqp_h, 0, xTP, 0, [("mmv", NKT - 1)])
                for s_ in range(NSLOT):
                    if s_ + 1 < NSLOT:
                        i0 = NKT + 2 * (s_ + 1)
                        tile_prep(i0, xq_h, (s_ + 1) * 128, xT, (s_ + 1) % 2, [("mmv", NKT + s_)])
                        tile_prep(i0 + 1, xqp_h, (s_ + 1) * 128, xTP, (s_ + 1) % 2, [("mmv", NKT + s_)])
                    emit_qmm(s_)
                    ext_front(pV[s_ % 2], 0, False, LQT, s_ * 128)
                    ext_back(NKT + s_ - 1)
                ext_back(NKT + NSLOT - 1)

                if debug >= 3:
                    JL_D3 = P.val("jl")
                    P.emit("gpsimd", lambda g: g.dma_start(out=dbg["kk"], in_=kk_sb[:]),
                           waits=[("jl", JL_D3)], inc="dbgs", inc_n=16)
                    P.emit("gpsimd", lambda g: g.dma_start(out=dbg["lraw"], in_=Lraw[:]), inc="dbgs", inc_n=16)
                    P.emit("gpsimd", lambda g: g.dma_start(out=dbg["ssb"], in_=ssb[:]), inc="dbgs", inc_n=16)
                    P.emit("gpsimd", lambda g: g.dma_start(out=dbg["rnrm"], in_=rnrm[:]), inc="dbgs", inc_n=16)
                    P.emit("gpsimd", lambda g: g.dma_start(out=dbg["lsc"], in_=Lsc[:]), inc="dbgs", inc_n=16)
                if debug >= 1:
                    JL_D = P.val("jl")
                    P.emit("gpsimd", lambda g: g.dma_start(out=dbg["vs"], in_=Vs[:]),
                           waits=[("jl", JL_D)], inc="dbgs", inc_n=16)
                    P.emit("gpsimd", lambda g: g.dma_start(out=dbg["jlkt"], in_=JLKT[:]), inc="dbgs", inc_n=16)
                    P.emit("gpsimd", lambda g: g.dma_start(out=dbg["lqt"], in_=LQT[:]), inc="dbgs", inc_n=16)

                JL_ALL = P.val("jl")
                P.emit("sync", lambda sy: sy.sem_inc(sems["bar1"], 1), waits=[("x0", P.val("x0")), ("x1", P.val("x1"))])
                P.emit("tensor", lambda te: te.sem_inc(sems["bar1"], 1))
                P.emit("vector", lambda v: v.sem_inc(sems["bar1"], 1), waits=[("jl", JL_ALL)])
                P.emit("scalar", lambda s: s.sem_inc(sems["bar1"], 1), waits=[("rr", P.val("rr"))])
                P.emit("gpsimd", lambda g: g.sem_inc(sems["bar1"], 1),
                       waits=[("wA", W_SETUP), ("dbgs", P.val("dbgs"))])
                P.cnt["bar1"] = 5
                P.run_block(nc, sems)
                cnt1 = P.cnt

            if phases < 2:
                for c in sems_ctx:
                    c.__exit__(None, None, None)
                return nc
            with ExitStack() as _stB:
                sbB = lambda *a: _stB.enter_context(nc.sbuf_tensor(*a))
                psB = lambda *a: _stB.enter_context(nc.psum_tensor(*a))
                masks_g0 = sbB("masks_g0", [128, 4, 2, 128], BF16)
                attnT = sbB("attnT", [128, 8, 512], BF16)
                PT0 = sbB("PT0", [128, 512], BF16)
                PT1 = sbB("PT1", [128, 512], BF16)
                PT2 = sbB("PT2", [128, 512], BF16)
                PT3 = sbB("PT3", [128, 512], BF16)
                
                wo_sb = sbB("wo_sb", [128, 8, 1024], BF16)
                bob = sbB("bob", [128, D], F32)
                srow_b = sbB("srow_b", [1, 512], BF16)
                ones1b = sbB("ones1b", [1, 64], BF16)
                
                rbl_t = sbB("rbl", [64, 512], F32)
                rbr0 = sbB("rbr0", [64, 512], F32)
                rbr1 = sbB("rbr1", [64, 512], F32)
                xqr_t = sbB("xqr", [128, D], F32)
                y1a_t = sbB("y1a", [128, D], F32)
                y1b_t = sbB("y1b", [128, D], F32)
                xlo_t = sbB("xlo", [128, D], BF16)
                xfT_t = sbB("xfT_t", [128, 8, 128], BF16)
                xfTlo_t = sbB("xfTlo_t", [128, 8, 128], BF16)
                stq = sbB("stq", [128, 4], F32)
                rstd1 = sbB("rstd1", [128, 1], F32)
                rln1 = sbB("rln1", [128, 1], F32)
                e8_t = sbB("e8", [128, NE], F32)
                nmx_t = sbB("nmx", [128, 1], F32)
                s8_t = sbB("s8", [128, 1], F32)
                scr8_t = sbB("scr8", [128, NE], F32)
                st8_t = sbB("st8", [128, 8], F32)
                s2t_t = sbB("s2t", [128, NE], F32)
                mask8_t = sbB("mask8", [128, NE], F32)
                
                pS0 = psB("pS0", [128, 512], F32)
                pS1 = psB("pS1", [128, 512], F32)
                pS2 = psB("pS2", [128, 512], F32)
                pAV0 = psB("pAV0", [65, 512], F32)
                pAV1 = psB("pAV1", [65, 512], F32)
                
                pWO0 = psB("pWO0", [128, 1024], F32)
                
                pT2 = psB("pT2", [128, 8, 128], BF16)
                P = Prog(cnt1)
                PT = [PT0, PT1, PT2, PT3]
                pS, pAV = [pS0, pS1, pS2], [pAV0, pAV1]
                pWO = pWO0
                rbr = [rbr0, rbr1]
                masks_g = masks_g0

                P.emit("gpsimd", lambda g: g.dma_start(
                    out=wo_sb[:], in_=wo_h.rearrange("(dc p) n -> p dc n", p=128)), inc="wB", inc_n=16)
                P.emit("gpsimd", lambda g: g.dma_start(out=bob[:], in_=bob_h), inc="wB", inc_n=16)
                W_B = P.val("wB")
                P.emit("vector", lambda v: v.memset(ones1b[:], 1.0), waits=[("bar1", 5)])
                P.emit("vector", lambda v: v.drain(), inc="xt")
                ONES_RDY = P.val("xt")

                sbank_free = [None, None, None]
                ptslot_free = [None, None, None, None]
                avbank_free = [None, None]
                exp_serial = [0]
                wo_prev_g = 0

                def emit_score(g, h, kc):
                    gr, m = h % 4, h // 4
                    jlo = max(4 * g, kc // 2)
                    off = (jlo - 4 * g) * 128
                    sb = exp_serial[0] % 3
                    pslot = exp_serial[0] % 4
                    exp_serial[0] += 1
                    waits = []
                    if sbank_free[sb]:
                        waits.append(sbank_free[sb])
                    if h <= 1 and g == 0 and kc < 2:
                        waits.append(("jl", JL_ALL))
                    def scmm(te, sb=sb, off=off, gr=gr, m=m, kc=kc, g=g):
                        return te.matmul(
                            pS[sb][:, off:512],
                            JLKT[32 * gr:32 * gr + 6, m, kc * 128:(kc + 1) * 128],
                            LQT[32 * gr:32 * gr + 6, m, g * 512 + off:(g + 1) * 512],
                            start=True, stop=True, tile_position=(32 * gr, 0))
                    P.emit("tensor", scmm, waits=waits, inc="sc")
                    ncols = 512 - off
                    ew = [("sc", P.val("sc"))]
                    if ptslot_free[pslot]:
                        ew.append(ptslot_free[pslot])
                    P.emit("scalar", lambda s, sb=sb, off=off, pslot=pslot, ncols=ncols: s.activation(
                        PT[pslot][:, 0:ncols], pS[sb][:, off:512], AF.Exp, bias=0.0, scale=SCALE),
                        waits=ew, inc="exp")
                    sbank_free[sb] = ("exp", P.val("exp"))
                    masked = kc >= 8 * g
                    if masked:
                        jj, ii = kc // 2 - 4 * g, kc % 2
                        P.emit("vector", lambda v, pslot=pslot, jj=jj, ii=ii, g=g: v.tensor_mul(
                            PT[pslot][:, 0:128], PT[pslot][:, 0:128], masks_g[:, jj, ii, :]),
                            waits=[("exp", P.val("exp")), MSK_RDY[0]])
                        P.emit("vector", lambda v: v.drain(), inc="msk")
                    return (pslot, off, ncols, masked, P.val("exp"), P.val("msk"))

                def emit_av(g, h, kc, sinfo):
                    pslot, off, ncols, masked, expv, mskv = sinfo
                    hb = h % 2
                    K = kmax(g)
                    avw = [("msk", mskv)] if masked else [("exp", expv)]
                    if kc == 0 and avbank_free[hb]:
                        avw.append(avbank_free[hb])
                    def avmm(te, hb=hb, off=off, kc=kc, h=h, ncols=ncols, K=K, pslot=pslot):
                        return te.matmul(
                            pAV[hb][:, off:512], Vs[:, kc, h, :], PT[pslot][:, 0:ncols],
                            start=(kc == 0), stop=(kc == K - 1))
                    P.emit("tensor", avmm, waits=avw, inc="av")
                    ptslot_free[pslot] = ("av", P.val("av"))

                for g in range(NG):
                    av_counts, sr_vals, rbm_vals, rbr_vals = {}, {}, {}, {}
                    last_rbr = P.val("rbr")
                    P.emit("gpsimd", lambda gq, g=g: gq.dma_start(
                        out=masks_g[:], in_=masks_h[:, 4 * g:4 * g + 4, :, :]),
                        waits=[("bar1", 5), ("msk", P.cnt.get("_mskuse_%d" % (g - 1), 0))],
                        inc="mk0", inc_n=16)
                    MSK_RDY = [("mk0", P.val("mk0"))]
                    K = kmax(g)

                    def emit_chain_pre(hh):
                        hbb = hh % 2
                        srw = [("av", av_counts[hh])]
                        if rbm_vals:
                            srw.append(("rbm", max(rbm_vals.values())))
                        P.emit("vector", lambda v, hbb=hbb: v.tensor_copy(
                            srow_b[0:1, :], pAV[hbb][64:65, :]), waits=srw)
                        P.emit("vector", lambda v: v.drain(), inc="sr")
                        sr_vals[hh] = P.val("sr")
                        rw = [("sr", sr_vals[hh]), ("xt", ONES_RDY)]
                        prev_rbr = rbr_vals[hh - 1] if (hh - 1) in rbr_vals else last_rbr
                        if prev_rbr:
                            rw.append(("rbr", prev_rbr))
                        rw.append(("y1a", P.val("y1a")))
                        def rbmm(te):
                            return te.matmul(pWO[0:64, 512:1024], ones1b[:], srow_b[0:1, :], start=True, stop=True)
                        P.emit("tensor", rbmm, waits=rw, inc="rbm")
                        rbm_vals[hh] = P.val("rbm")

                    def emit_chain_act(hh):
                        hbb = hh % 2
                        P.emit("scalar", lambda s: s.activation(rbl_t[:], pWO[0:64, 512:1024], AF.Ln),
                               waits=[("rbm", rbm_vals[hh])])
                        P.emit("scalar", lambda s: s.drain())
                        P.emit("scalar", lambda s, hbb=hbb: s.activation(
                            rbr[hbb][:], rbl_t[:], AF.Exp, bias=0.0, scale=-1.0))
                        P.emit("scalar", lambda s: s.drain(), inc="rbr")
                        rbr_vals[hh] = P.val("rbr")

                    def emit_chain_post(hh):
                        hbb = hh % 2
                        cp, half = hh // 2, (hh % 2) * 64
                        P.emit("vector", lambda v, hbb=hbb, cp=cp, half=half: v.tensor_tensor(
                            attnT[half:half + 64, cp, :], pAV[hbb][0:64, :], rbr[hbb][:], ALU.mult),
                            waits=[("rbr", rbr_vals[hh]), ("wo", wo_prev_g)])
                        P.emit("vector", lambda v: v.drain(), inc="ate")
                        avbank_free[hbb] = ("ate", P.val("ate"))

                    def emit_chain(hh):
                        emit_chain_pre(hh)
                        emit_chain_act(hh)
                        emit_chain_post(hh)

                    for p in range(8):
                        a, b = 2 * p, 2 * p + 1
                        prev_a = prev_b = None
                        for kc in range(K):
                            sinfo_a = emit_score(g, a, kc)
                            sinfo_b = emit_score(g, b, kc)
                            if p >= 1:
                                if kc == 0:
                                    emit_chain_pre(2 * p - 2)
                                    emit_chain_act(2 * p - 2)
                                    emit_chain_pre(2 * p - 1)
                                    emit_chain_act(2 * p - 1)
                                elif kc == 1:
                                    emit_chain_post(2 * p - 2)
                                    emit_chain_post(2 * p - 1)
                            if kc >= 1:
                                emit_av(g, a, kc - 1, prev_a)
                                emit_av(g, b, kc - 1, prev_b)
                            prev_a, prev_b = sinfo_a, sinfo_b
                        emit_av(g, a, K - 1, prev_a)
                        emit_av(g, b, K - 1, prev_b)
                        av_counts[a] = P.val("av") - 1
                        av_counts[b] = P.val("av")
                    emit_chain(14)
                    emit_chain(14)
                    emit_chain(15)

                    ATE_G = P.val("ate")
                    for qq in range(4):
                        qt = 4 * g + qq
                        qb = qt % 2
                        P.emit("sync", lambda sy, qt=qt: sy.dma_start(
                            out=xqr_t[:], in_=xq_h[qt * 128:(qt + 1) * 128, :]),
                            waits=[("bar1", 5), ("y1a", P.val("y1a"))], inc="xqd", inc_n=16)
                        def womm(te, qq=qq):
                            ins = None
                            for cp in range(8):
                                for nb in range(2):
                                    ins = te.matmul(
                                        pWO[:, nb * 512:(nb + 1) * 512],
                                        attnT[:, cp, qq * 128:(qq + 1) * 128],
                                        wo_sb[:, cp, nb * 512:(nb + 1) * 512],
                                        start=(cp == 0), stop=(cp == 7))
                            return ins
                        wow = [("ate", ATE_G), ("wB", W_B), ("y1a", P.val("y1a"))]
                        P.emit("tensor", womm, waits=wow, inc="wo")
                        WO_T = P.val("wo")
                        P.emit("vector", lambda v: v.tensor_tensor(y1a_t[:], pWO[:], bob[:], ALU.add),
                               waits=[("wo", WO_T), ("xqd", P.val("xqd"))])
                        P.emit("vector", lambda v: v.drain())
                        P.emit("vector", lambda v: v.scalar_tensor_tensor(
                            y1b_t[:], y1a_t[:], 1.0, xqr_t[:], ALU.mult, ALU.add, accum_out=stq[:, 0:1]))
                        P.emit("vector", lambda v: v.drain(), inc="y1a")
                        Y1A = P.val("y1a")
                        P.emit("vector", lambda v: v.tensor_scalar(stq[:, 0:1], stq[:, 0:1], 1.0 / D, None, ALU.mult))
                        P.emit("vector", lambda v: v.drain())
                        P.emit("vector", lambda v: v.tensor_scalar(y1a_t[:], y1b_t[:], stq[:, 0:1], None, ALU.subtract))
                        P.emit("vector", lambda v: v.drain())
                        P.emit("vector", lambda v: v.scalar_tensor_tensor(
                            y1b_t[:], y1a_t[:], 1.0, y1a_t[:], ALU.mult, ALU.mult, accum_out=stq[:, 1:2]))
                        P.emit("vector", lambda v: v.drain(), inc="v1")
                        P.emit("scalar", lambda s: s.activation(rln1[:], stq[:, 1:2], AF.Ln,
                                                                bias=epsb[:, 0:1], scale=1.0 / D),
                               waits=[("v1", P.val("v1"))])
                        P.emit("scalar", lambda s: s.drain())
                        P.emit("scalar", lambda s: s.activation(rstd1[:], rln1[:], AF.Exp, bias=0.0, scale=-0.5))
                        P.emit("scalar", lambda s: s.drain(), inc="sd1")
                        P.emit("vector", lambda v: v.tensor_scalar(y1b_t[:], y1a_t[:], rstd1[:, 0:1], None, ALU.mult),
                               waits=[("sd1", P.val("sd1"))])
                        P.emit("vector", lambda v: v.drain())
                        P.emit("vector", lambda v, qt=qt: v.tensor_copy(acc_t[:, qt, :], y1b_t[:]))
                        P.emit("vector", lambda v: v.drain())
                        P.emit("vector", lambda v, qt=qt: v.tensor_tensor(xlo_t[:], y1b_t[:], acc_t[:, qt, :], ALU.subtract))
                        P.emit("vector", lambda v: v.drain(), inc="xf")
                        XF = P.val("xf")
                        def tph(te, qt=qt):
                            ins = None
                            for dc in range(8):
                                ins = te.transpose(pT2[:, dc, :], acc_t[:, qt, dc * 128:(dc + 1) * 128], ident[:])
                            return ins
                        P.emit("tensor", tph, waits=[("xf", XF), ("xft", P.val("xft"))], inc="tpf")
                        P.emit("vector", lambda v: v.tensor_copy(xfT_t[:], pT2[:]),
                               waits=[("tpf", P.val("tpf"))])
                        P.emit("vector", lambda v: v.drain(), inc="xft")
                        def tpl(te):
                            ins = None
                            for dc in range(8):
                                ins = te.transpose(pT2[:, dc, :], xlo_t[:, dc * 128:(dc + 1) * 128], ident[:])
                            return ins
                        P.emit("tensor", tpl, waits=[("xft", P.val("xft"))], inc="tpf")
                        P.emit("vector", lambda v: v.tensor_copy(xfTlo_t[:], pT2[:]),
                               waits=[("tpf", P.val("tpf"))])
                        P.emit("vector", lambda v: v.drain(), inc="xft")
                        XFT = P.val("xft")
                        rtw = [("xft", XFT)]
                        if sbank_free[qb]:
                            rtw.append(sbank_free[qb])
                        def rtmm(te, qb=qb):
                            ins = None
                            srcs = ((xfT_t, rtr_hi), (xfT_t, rtr_lo), (xfTlo_t, rtr_hi))
                            n = 0
                            for src, w in srcs:
                                for dc in range(8):
                                    ins = te.matmul(pS[qb][:, 0:8], src[:, dc, :], w[:, dc, :],
                                                    start=(n == 0), stop=(n == 23))
                                    n += 1
                            return ins
                        P.emit("tensor", rtmm, waits=rtw, inc="rt")
                        P.emit("vector", lambda v, qb=qb: v.reduce_max(
                            nmx_t[:], pS[qb][:, 0:8], axis=AX.X, negate=True),
                            waits=[("rt", P.val("rt"))])
                        P.emit("vector", lambda v: v.drain(), inc="rmax")
                        P.emit("scalar", lambda s, qb=qb: s.activation(
                            e8_t[:], pS[qb][:, 0:8], AF.Exp, bias=nmx_t[:, 0:1], scale=1.0),
                            waits=[("rmax", P.val("rmax"))])
                        P.emit("scalar", lambda s: s.drain(), inc="rexp")
                        sbank_free[qb] = ("rexp", P.val("rexp"))
                        P.emit("vector", lambda v: v.reduce_sum(s8_t[:], e8_t[:], axis=AX.X),
                               waits=[("rexp", P.val("rexp"))])
                        P.emit("vector", lambda v: v.drain())
                        P.emit("vector", lambda v: v.reciprocal(nmx_t[:], s8_t[:]))
                        P.emit("vector", lambda v: v.drain())
                        P.emit("vector", lambda v: v.tensor_scalar(scr8_t[:], e8_t[:], nmx_t[:, 0:1], None, ALU.mult))
                        P.emit("vector", lambda v: v.drain())
                        P.emit("vector", lambda v: v.max(st8_t[:], scr8_t[:]))
                        P.emit("vector", lambda v: v.drain())
                        P.emit("vector", lambda v: v.memset(st8_t[:, 2:8], -1.0e30))
                        P.emit("vector", lambda v: v.drain())
                        P.emit("vector", lambda v: v.match_replace(s2t_t[:], st8_t[:], scr8_t[:], -2.0e30))
                        P.emit("vector", lambda v: v.drain())
                        P.emit("vector", lambda v: v.tensor_single_scalar(mask8_t[:], s2t_t[:], -1.0e29, ALU.is_lt))
                        P.emit("vector", lambda v: v.drain())
                        P.emit("vector", lambda v: v.tensor_mul(mask8_t[:], mask8_t[:], scr8_t[:]))
                        P.emit("vector", lambda v: v.tensor_add(stq[:, 2:3], st8_t[:, 0:1], st8_t[:, 1:2]))
                        P.emit("vector", lambda v: v.drain())
                        P.emit("vector", lambda v: v.tensor_scalar(s2t_t[:, 0:1], stq[:, 2:3], 1.0, 1e-20, ALU.mult, ALU.add))
                        P.emit("vector", lambda v: v.drain())
                        P.emit("vector", lambda v: v.reciprocal(s8_t[:], s2t_t[:, 0:1]))
                        P.emit("vector", lambda v: v.drain())
                        P.emit("vector", lambda v, qt=qt: v.tensor_scalar(
                            comb_t[:, qt, :], mask8_t[:], s8_t[:, 0:1], None, ALU.mult))
                        P.emit("vector", lambda v: v.drain(), inc="cmb")

                    wo_prev_g = P.val("wo")
                    P.cnt["_mskuse_%d" % g] = P.val("msk")

                if debug >= 2:
                    P.emit("gpsimd", lambda g: g.dma_start(out=dbg["acc"], in_=acc_t[:]),
                           waits=[("xf", P.val("xf"))], inc="dbgs", inc_n=16)
                    P.emit("gpsimd", lambda g: g.dma_start(out=dbg["comb"], in_=comb_t[:]),
                           waits=[("cmb", P.val("cmb"))], inc="dbgs", inc_n=16)
                    P.emit("gpsimd", lambda g: g.dma_start(out=dbg["attnT"], in_=attnT[:]),
                           waits=[("ate", P.val("ate"))], inc="dbgs", inc_n=16)

                P.emit("sync", lambda sy: sy.sem_inc(sems["bar2"], 1), waits=[("xqd", P.val("xqd"))])
                P.emit("tensor", lambda te: te.sem_inc(sems["bar2"], 1))
                P.emit("vector", lambda v: v.sem_inc(sems["bar2"], 1), waits=[("cmb", P.val("cmb"))])
                P.emit("scalar", lambda s: s.sem_inc(sems["bar2"], 1))
                P.emit("gpsimd", lambda g: g.sem_inc(sems["bar2"], 1),
                       waits=[("wB", W_B), ("dbgs", P.val("dbgs"))])
                P.cnt["bar2"] = 5
                P.run_block(nc, sems)
                cnt2 = P.cnt

        if phases < 3:
            for c in sems_ctx:
                c.__exit__(None, None, None)
            return nc
        with ExitStack() as _stD:
            sbD = lambda *a: _stD.enter_context(nc.sbuf_tensor(*a))
            psD = lambda *a: _stD.enter_context(nc.psum_tensor(*a))
            xfT = sbD("xfT", [128, 8, NSLOT * 128], BF16)
            ht_sb = sbD("ht", [128, NHC, NSLOT * 128], BF16)
            wg_sb = sbD("wg_sb", [128, NHC * 8, 128], BF16)
            wu_sb = sbD("wu_sb", [128, NHC * 8, 128], BF16)
            wd_sb = sbD("wd_sb", [128, NHC, 1024], BF16)
            sG0 = sbD("sG0", [128, 512], BF16)
            sG1 = sbD("sG1", [128, 512], BF16)
            xc2 = sbD("xc2", [128, D], F32)
            sq2 = sbD("sq2", [128, D], F32)
            st2 = sbD("st2", [128, 2], F32)
            rstd2 = sbD("rstd2", [128, 1], F32)
            rln2 = sbD("rln2", [128, 1], F32)
            outst0 = sbD("outst0", [128, D], F32)
            outst1 = sbD("outst1", [128, D], F32)
            pG0 = psD("pG0", [128, 512], F32)
            pG1 = psD("pG1", [128, 512], F32)
            pU0 = psD("pU0", [128, 512], F32)
            pU1 = psD("pU1", [128, 512], F32)
            pD0 = psD("pD0", [128, 1024], F32)
            
            pT3 = psD("pT3", [128, 8, 128], BF16)
            P = Prog(cnt2)
            pG, pU = [pG0, pG1], [pU0, pU1]
            pD = pD0
            sG = [sG0, sG1]
            outst = [outst0, outst1]

            for t in range(NSLOT):
                def tpx(te, t=t):
                    ins = None
                    for dc in range(8):
                        ins = te.transpose(pT3[:, dc, :], acc_t[:, t, dc * 128:(dc + 1) * 128], ident[:])
                    return ins
                P.emit("tensor", tpx, waits=[("bar2", 5), ("xft", P.val("xft"))], inc="tpf")
                P.emit("vector", lambda v, t=t: v.tensor_copy(xfT[:, :, t * 128:(t + 1) * 128], pT3[:]),
                       waits=[("bar2", 5), ("tpf", P.val("tpf"))])
                P.emit("vector", lambda v: v.drain(), inc="xft")
            XFT_ALL = P.val("xft")

            for e in range(NE):
                for hc in range(NHC):
                    P.emit("gpsimd", lambda g, e=e, hc=hc: g.dma_start(
                        out=wg_sb[:, hc * 8:(hc + 1) * 8, :],
                        in_=wg_h[e][:, hc * 128:(hc + 1) * 128].rearrange("(dc p) q -> p dc q", p=128)),
                        waits=[("bar2", 5), ("gmm", 44 * (e - 1) + 4 * (hc + 1) if e else 0)],
                        inc="wg", inc_n=16)
                    P.emit("gpsimd", lambda g, e=e, hc=hc: g.dma_start(
                        out=wu_sb[:, hc * 8:(hc + 1) * 8, :],
                        in_=wu_h[e][:, hc * 128:(hc + 1) * 128].rearrange("(dc p) q -> p dc q", p=128)),
                        waits=[("umm", 44 * (e - 1) + 4 * (hc + 1) if e else 0)],
                        inc="wu", inc_n=16)
                P.emit("gpsimd", lambda g, e=e: g.dma_start(
                    out=wd_sb[:], in_=wd_h[e].rearrange("(hc p) n -> p hc n", p=128)),
                    waits=[("dmm", 16 * e)], inc="wd", inc_n=16)

            for e in range(NE):
                for hc in range(NHC):
                    for tc in range(4):
                        gi = (hc * 4 + tc) % 2
                        def gmm(te, hc=hc, tc=tc, gi=gi, e=e):
                            ins = None
                            for dc in range(8):
                                ins = te.matmul(pG[gi][:], wg_sb[:, hc * 8 + dc, :],
                                                xfT[:, dc, tc * 512:(tc + 1) * 512],
                                                start=(dc == 0), stop=(dc == 7))
                            return ins
                        P.emit("tensor", gmm,
                               waits=[("wg", 16 * NHC * (e + 1)), ("xft", XFT_ALL),
                                      ("sg", P.val("sg") - 1)], inc="gmm")
                        def umm(te, hc=hc, tc=tc, gi=gi, e=e):
                            ins = None
                            for dc in range(8):
                                ins = te.matmul(pU[gi][:], wu_sb[:, hc * 8 + dc, :],
                                                xfT[:, dc, tc * 512:(tc + 1) * 512],
                                                start=(dc == 0), stop=(dc == 7))
                            return ins
                        P.emit("tensor", umm,
                               waits=[("wu", 16 * NHC * (e + 1)), ("hm", P.val("hm") - 1)], inc="umm")
                        P.emit("scalar", lambda s, gi=gi: s.activation(sG[gi][:], pG[gi][:], AF.Silu),
                               waits=[("gmm", P.val("gmm")), ("hm", P.val("hm") - 1)])
                        P.emit("scalar", lambda s: s.drain(), inc="sg")
                        P.emit("vector", lambda v, gi=gi, hc=hc, tc=tc: v.tensor_tensor(
                            ht_sb[:, hc, tc * 512:(tc + 1) * 512], sG[gi][:], pU[gi][:], ALU.mult),
                            waits=[("sg", P.val("sg")), ("umm", P.val("umm"))])
                        P.emit("vector", lambda v: v.drain(), inc="hm")
                HM_E = P.val("hm")
                for t in range(NSLOT):
                    def dmm(te, t=t):
                        ins = None
                        for hc in range(NHC):
                            for nb in range(2):
                                ins = te.matmul(pD[:, nb * 512:(nb + 1) * 512],
                                                ht_sb[:, hc, t * 128:(t + 1) * 128],
                                                wd_sb[:, hc, nb * 512:(nb + 1) * 512],
                                                start=(hc == 0), stop=(hc == NHC - 1))
                        return ins
                    P.emit("tensor", dmm,
                           waits=[("wd", 16 * (e + 1)), ("hm", HM_E), ("acc", P.val("acc"))],
                           inc="dmm")
                    if e == NE - 1:
                        P.emit("vector", lambda v, e=e, t=t: v.scalar_tensor_tensor(
                            acc_t[:, t, :], pD[:], comb_t[:, t, e:e + 1], acc_t[:, t, :],
                            ALU.mult, ALU.add, accum_out=st2[:, 0:1]),
                            waits=[("dmm", P.val("dmm"))])
                    else:
                        P.emit("vector", lambda v, e=e, t=t: v.scalar_tensor_tensor(
                            acc_t[:, t, :], pD[:], comb_t[:, t, e:e + 1], acc_t[:, t, :],
                            ALU.mult, ALU.add),
                            waits=[("dmm", P.val("dmm"))])
                    P.emit("vector", lambda v: v.drain(), inc="acc")
                    if e == NE - 1:
                        ob = t % 2
                        P.emit("vector", lambda v: v.tensor_scalar(st2[:, 0:1], st2[:, 0:1], 1.0 / D, None, ALU.mult),
                               waits=[("acc", P.val("acc"))])
                        P.emit("vector", lambda v: v.drain())
                        P.emit("vector", lambda v, t=t: v.tensor_scalar(xc2[:], acc_t[:, t, :], st2[:, 0:1], None, ALU.subtract))
                        P.emit("vector", lambda v: v.drain())
                        P.emit("vector", lambda v: v.scalar_tensor_tensor(
                            sq2[:], xc2[:], 1.0, xc2[:], ALU.mult, ALU.mult, accum_out=st2[:, 1:2]))
                        P.emit("vector", lambda v: v.drain(), inc="ln2")
                        P.emit("scalar", lambda s: s.activation(rln2[:], st2[:, 1:2], AF.Ln,
                                                                bias=epsb[:, 0:1], scale=1.0 / D),
                               waits=[("ln2", P.val("ln2"))])
                        P.emit("scalar", lambda s: s.drain())
                        P.emit("scalar", lambda s: s.activation(rstd2[:], rln2[:], AF.Exp, bias=0.0, scale=-0.5))
                        P.emit("scalar", lambda s: s.drain(), inc="sd2")
                        P.emit("vector", lambda v, ob=ob: v.tensor_scalar(
                            outst[ob][:], xc2[:], rstd2[:, 0:1], None, ALU.mult),
                            waits=[("sd2", P.val("sd2")), ("out", 16 * t)])
                        P.emit("vector", lambda v: v.drain(), inc="eo")
                        P.emit("sync", lambda sy, t=t, ob=ob: sy.dma_start(
                            out=out_h[t * 128:(t + 1) * 128, :], in_=outst[ob][:]),
                            waits=[("eo", P.val("eo"))], inc="out", inc_n=16)

            P.emit("sync", lambda sy: sy.nop(), waits=[("out", 16 * NSLOT)])
            P.run_block(nc, sems)

        for c in sems_ctx:
            c.__exit__(None, None, None)
    return nc



def _make_masks(tiles):
    m = np.zeros((128, NSLOT, 2, 128), np.float32)
    tri = (np.arange(128)[:, None] <= np.arange(128)[None, :]).astype(np.float32)
    for j, t in enumerate(tiles):
        c = t + 1
        for i in range(2):
            kc = 2 * j + i
            if kc < c - 1:
                m[:, j, i, :] = 1.0
            elif kc == c - 1:
                m[:, j, i, :] = tri
    return np.ascontiguousarray(m.astype(ml_dtypes.bfloat16))


def kernel(L0_attn_x, L0_attn_W1_q, L0_attn_W2_q, L0_attn_W1_k, L0_attn_W2_k,
           L0_attn_W_v, L0_attn_b_v, L0_attn_W_o, L0_attn_b_o,
           L0_ffn_w_router, L0_ffn_w_gate, L0_ffn_w_up, L0_ffn_w_down,
           _debug=0):
    x = np.ascontiguousarray(np.asarray(L0_attn_x, dtype=np.float32))
    arrs = {
        "W1_q": L0_attn_W1_q, "W2_q": L0_attn_W2_q, "W1_k": L0_attn_W1_k,
        "W2_k": L0_attn_W2_k, "W_v": L0_attn_W_v, "W_o": L0_attn_W_o,
        "w_router": L0_ffn_w_router, "w_gate": L0_ffn_w_gate,
        "w_up": L0_ffn_w_up, "w_down": L0_ffn_w_down,
    }
    arrs = {k: np.ascontiguousarray(np.asarray(v, dtype=np.float32)) for k, v in arrs.items()}
    bv = np.asarray(L0_attn_b_v, dtype=np.float32)
    bo = np.asarray(L0_attn_b_o, dtype=np.float32)
    arrs["b_v_b"] = np.ascontiguousarray(np.broadcast_to(bv, (128, D)))
    arrs["b_o_b"] = np.ascontiguousarray(np.broadcast_to(bo, (128, D)))
    arrs["ident_hbm"] = np.eye(128, dtype=np.float32)
    masks = {0: _make_masks(TILES_A), 1: _make_masks(TILES_B)}

    in_maps = []
    for c in range(8):
        b, v_ = c // 2, c % 2
        tiles = TILES_A if v_ == 0 else TILES_B
        rows = np.concatenate([np.arange(t * 128, (t + 1) * 128) for t in tiles])
        xb = x[b]
        prev = rows - 1
        xqp = np.where((prev >= 0)[:, None], xb[np.maximum(prev, 0)], np.float32(0.0))
        m = dict(arrs)
        m["x_kv"] = xb
        m["xq"] = np.ascontiguousarray(xb[rows])
        m["xq_prev"] = np.ascontiguousarray(xqp.astype(np.float32))
        m["masks"] = masks[v_]
        in_maps.append(m)

    nc = build_program(debug=_debug, phases=_PHASES)
    res = run_bass_kernel_spmd(nc, in_maps, core_ids=list(range(8)))

    out = np.zeros((B, T, D), np.float32)
    for c in range(8):
        b, v_ = c // 2, c % 2
        tiles = TILES_A if v_ == 0 else TILES_B
        oq = np.asarray(res.results[c]["out_q"])
        for j, t in enumerate(tiles):
            out[b, t * 128:(t + 1) * 128, :] = oq[j * 128:(j + 1) * 128, :]
    if _debug:
        return out, res
    return out


# version 19
# speedup vs baseline: 1.1181x; 1.0367x over previous
from contextlib import ExitStack

import numpy as np
import ml_dtypes
import concourse.bass as bass
import concourse.mybir as mybir
from concourse.bass_utils import run_bass_kernel_spmd

F32 = mybir.dt.float32
BF16 = mybir.dt.bfloat16
AF = mybir.ActivationFunctionType
ALU = mybir.AluOpType
AX = mybir.AxisListType

B, T, D, H, DH, C6 = 4, 4096, 1024, 16, 64, 6
NKT = 32
NSLOT = 16
NG = 4
HFF = 1408
NE = 8
NHC = 11
SCALE = float(6.0 ** -0.5)
PAIRS = [(0, 1), (0, 2), (0, 3), (1, 2), (1, 3), (2, 3)]
JSIGN = [1.0, -1.0, 1.0, 1.0, -1.0, 1.0]
TILES_A = [0, 3, 4, 7, 8, 11, 12, 15, 16, 19, 20, 23, 24, 27, 28, 31]
TILES_B = [1, 2, 5, 6, 9, 10, 13, 14, 17, 18, 21, 22, 25, 26, 29, 30]
ENGINES = ("sync", "tensor", "vector", "scalar", "gpsimd")
_PHASES = 3
_SKIP_Q = False


def kmax(g):
    return 8 * g + 8


class Prog:

    def __init__(self, cnt=None):
        self.ops = {e: [] for e in ENGINES}
        self.cnt = dict(cnt) if cnt else {}

    def val(self, sem):
        return self.cnt.get(sem, 0)

    def emit(self, eng, fn, waits=(), inc=None, inc_n=1):
        w = [(s, v) for (s, v) in waits if v > 0]
        self.ops[eng].append((fn, w, inc, inc_n))
        if inc is not None:
            self.cnt[inc] = self.cnt.get(inc, 0) + inc_n

    def replay(self, eng_name, handle, sems):
        for fn, waits, inc, inc_n in self.ops[eng_name]:
            for s, v in waits:
                handle.wait_ge(sems[s], v)
            ins = fn(handle)
            if inc is not None:
                if ins is None:
                    raise RuntimeError(f"op with inc {inc} returned no instruction")
                ins.then_inc(sems[inc], inc_n)

    def run_block(self, nc, sems):
        with nc.Block() as blk:
            blk.sync(lambda e: self.replay("sync", e, sems))
            blk.tensor(lambda e: self.replay("tensor", e, sems))
            blk.vector(lambda e: self.replay("vector", e, sems))
            blk.scalar(lambda e: self.replay("scalar", e, sems))
            blk.gpsimd(lambda e: self.replay("gpsimd", e, sems))


def build_program(debug=0, phases=3):
    nc = bass.Bass("TRN2", target_bir_lowering=False)

    x_kv = nc.dram_tensor("x_kv", [T, D], F32, kind="ExternalInput").ap()
    xq_h = nc.dram_tensor("xq", [NSLOT * 128, D], F32, kind="ExternalInput").ap()
    xqp_h = nc.dram_tensor("xq_prev", [NSLOT * 128, D], F32, kind="ExternalInput").ap()
    masks_h = nc.dram_tensor("masks", [128, NSLOT, 2, 128], BF16, kind="ExternalInput").ap()
    ident_h = nc.dram_tensor("ident_hbm", [128, 128], F32, kind="ExternalInput").ap()
    w1q_h = nc.dram_tensor("W1_q", [2 * D, 64], F32, kind="ExternalInput").ap()
    w2q_h = nc.dram_tensor("W2_q", [2 * D, 64], F32, kind="ExternalInput").ap()
    w1k_h = nc.dram_tensor("W1_k", [D, 64], F32, kind="ExternalInput").ap()
    w2k_h = nc.dram_tensor("W2_k", [D, 64], F32, kind="ExternalInput").ap()
    wv_h = nc.dram_tensor("W_v", [D, D], F32, kind="ExternalInput").ap()
    bvb_h = nc.dram_tensor("b_v_b", [128, D], F32, kind="ExternalInput").ap()
    wo_h = nc.dram_tensor("W_o", [D, D], F32, kind="ExternalInput").ap()
    bob_h = nc.dram_tensor("b_o_b", [128, D], F32, kind="ExternalInput").ap()
    wr_h = nc.dram_tensor("w_router", [D, NE], F32, kind="ExternalInput").ap()
    wg_h = nc.dram_tensor("w_gate", [NE, D, HFF], F32, kind="ExternalInput").ap()
    wu_h = nc.dram_tensor("w_up", [NE, D, HFF], F32, kind="ExternalInput").ap()
    wd_h = nc.dram_tensor("w_down", [NE, HFF, D], F32, kind="ExternalInput").ap()
    out_h = nc.dram_tensor("out_q", [NSLOT * 128, D], F32, kind="ExternalOutput").ap()

    dbg = {}
    if debug >= 1:
        dbg["vs"] = nc.dram_tensor("dbg_vs", [128, NKT, H, 65], F32, kind="ExternalOutput").ap()
        dbg["jlkt"] = nc.dram_tensor("dbg_jlkt", [128, 4, NKT * 128], F32, kind="ExternalOutput").ap()
        dbg["lqt"] = nc.dram_tensor("dbg_lqt", [128, 4, NSLOT * 128], F32, kind="ExternalOutput").ap()
    if debug >= 3:
        dbg["kk"] = nc.dram_tensor("dbg_kk", [128, 128], F32, kind="ExternalOutput").ap()
        dbg["lraw"] = nc.dram_tensor("dbg_lraw", [128, H, C6], F32, kind="ExternalOutput").ap()
        dbg["ssb"] = nc.dram_tensor("dbg_ssb", [128, H], F32, kind="ExternalOutput").ap()
        dbg["rnrm"] = nc.dram_tensor("dbg_rnrm", [128, H], F32, kind="ExternalOutput").ap()
        dbg["lsc"] = nc.dram_tensor("dbg_lsc", [128, H, C6], F32, kind="ExternalOutput").ap()
    if debug >= 2:
        dbg["acc"] = nc.dram_tensor("dbg_acc", [128, NSLOT, D], F32, kind="ExternalOutput").ap()
        dbg["comb"] = nc.dram_tensor("dbg_comb", [128, NSLOT, NE], F32, kind="ExternalOutput").ap()
        dbg["attnT"] = nc.dram_tensor("dbg_attnT", [128, 8, 512], F32, kind="ExternalOutput").ap()

    sem_names = [
        "wA", "z0", "x0", "x1", "cast", "tp", "xt", "mmv", "ss", "rr", "lk", "tk", "jl",
        "bar1", "wB", "mk0", "mk1", "sc", "exp", "msk", "av", "sr", "rbm", "rbf", "rbr",
        "ate", "wo", "xqd", "y1a", "v1", "sd1", "xf", "tpf", "xft", "rt",
        "rmax", "rexp", "cmb", "bar2", "wg", "wu", "wd", "gmm", "umm", "sg",
        "hm", "dmm", "acc", "ln2", "sd2", "eo", "out", "dbgs",
    ]

    with ExitStack() as _st0:
        sb = lambda *a: _st0.enter_context(nc.sbuf_tensor(*a))
        ident = sb("ident", [128, 128], BF16)
        acc_t = sb("acc", [128, NSLOT, D], BF16)
        comb_t = sb("comb", [128, NSLOT, NE], F32)
        rtr_f = sb("rtr_f", [128, 8, NE], F32)
        rtr_hi = sb("rtr_hi", [128, 8, NE], BF16)
        rtr_lo = sb("rtr_lo", [128, 8, NE], BF16)
        epsb = sb("epsb", [128, 1], F32)
        sems_ctx = [nc.semaphore(n) for n in sem_names]
        sems = dict(zip(sem_names, (c.__enter__() for c in sems_ctx)))

        with ExitStack() as _st1:
            sb1 = lambda *a: _st1.enter_context(nc.sbuf_tensor(*a))
            Vs = sb1("Vs", [128, NKT, H, 65], BF16)
            JLKT = sb1("JLKT", [128, 4, NKT * 128], BF16)
            LQT = sb1("LQT", [128, 4, NSLOT * 128], BF16)
            with ExitStack() as _stA:
                sbA = lambda *a: _stA.enter_context(nc.sbuf_tensor(*a))
                psA = lambda *a: _stA.enter_context(nc.psum_tensor(*a))
                wvks = sbA("wvks", [128, 8, 1152], BF16)
                wq_sb = sbA("wq_sb", [128, 16, 128], BF16)
                bvb = sbA("bvb", [128, D], F32)
                xa0 = sbA("xa0", [128, D], F32)
                xa1 = sbA("xa1", [128, D], F32)
                xab0 = sbA("xab0", [128, D], BF16)
                xab1 = sbA("xab1", [128, D], BF16)
                xT0 = sbA("xT0", [128, 8, 128], BF16)
                xT1 = sbA("xT1", [128, 8, 128], BF16)
                xTP0 = sbA("xTP0", [128, 8, 128], BF16)
                xTP1 = sbA("xTP1", [128, 8, 128], BF16)
                Lraw0 = sbA("Lraw0", [128, H, C6], F32)
                Lraw1 = sbA("Lraw1", [128, H, C6], F32)
                Lsq = sbA("Lsq", [128, H, C6], F32)
                prodAll = sbA("prodAll", [128, 12, H], F32)
                sst = sbA("sst", [128, 3, H], F32)
                ssb = sbA("ssb", [128, H], F32)
                rnrm0 = sbA("rnrm0", [128, H], F32)
                rnrm1 = sbA("rnrm1", [128, H], F32)
                nrnrm = sbA("nrnrm", [128, H], F32)
                rln = sbA("rln", [128, H], F32)
                Lsc = sbA("Lsc", [128, H, C6], BF16)
                kk_sb = sbA("kk_sb", [128, 128], F32)
                zb = sbA("zb", [128, 128], BF16)
                pT = psA("pT", [128, 8, 128], BF16)
                pTk = psA("pTk", [128, 4, 128], BF16)
                pV0 = psA("pV0", [128, 1152], F32)
                pV1 = psA("pV1", [128, 1152], F32)
                P = Prog()
                xa, xab, xT, xTP = [xa0, xa1], [xab0, xab1], [xT0, xT1], [xTP0, xTP1]
                pV = [pV0, pV1]

                P.emit("gpsimd", lambda g: g.dma_start(out=ident[:], in_=ident_h), inc="wA", inc_n=16)
                P.emit("gpsimd", lambda g: g.dma_start(
                    out=wvks[:, :, 0:1024], in_=wv_h.rearrange("(dc p) n -> p dc n", p=128)), inc="wA", inc_n=16)
                P.emit("gpsimd", lambda g: g.dma_start(
                    out=wvks[:, :, 1024:1088], in_=w1k_h.rearrange("(dc p) n -> p dc n", p=128)), inc="wA", inc_n=16)
                P.emit("gpsimd", lambda g: g.dma_start(
                    out=wvks[:, :, 1088:1152], in_=w2k_h.rearrange("(dc p) n -> p dc n", p=128)), inc="wA", inc_n=16)
                P.emit("gpsimd", lambda g: g.dma_start(
                    out=wq_sb[:, :, 0:64], in_=w1q_h.rearrange("(kc p) n -> p kc n", p=128)), inc="wA", inc_n=16)
                P.emit("gpsimd", lambda g: g.dma_start(
                    out=wq_sb[:, :, 64:128], in_=w2q_h.rearrange("(kc p) n -> p kc n", p=128)), inc="wA", inc_n=16)
                P.emit("gpsimd", lambda g: g.dma_start(out=bvb[:], in_=bvb_h), inc="wA", inc_n=16)
                P.emit("gpsimd", lambda g: g.dma_start(
                    out=rtr_f[:], in_=wr_h.rearrange("(dc p) n -> p dc n", p=128)), inc="wA", inc_n=16)
                W_SETUP = P.val("wA")

                P.emit("vector", lambda v: v.memset(epsb[:], 1e-5))
                P.emit("vector", lambda v: v.drain())
                P.emit("vector", lambda v: v.memset(zb[:], 0.0))
                P.emit("vector", lambda v: v.drain(), inc="z0")
                def ztps(te):
                    ins = None
                    for m in range(4):
                        ins = te.transpose(pTk[:, m, :], zb[:], ident[:])
                    return ins
                P.emit("tensor", ztps, waits=[("z0", 1), ("wA", W_SETUP)], inc="tk")
                P.emit("vector", lambda v: v.tensor_copy(rtr_hi[:], rtr_f[:]), waits=[("wA", W_SETUP)])
                P.emit("vector", lambda v: v.drain())
                P.emit("vector", lambda v: v.tensor_tensor(rtr_lo[:], rtr_f[:], rtr_hi[:], ALU.subtract))
                P.emit("vector", lambda v: v.drain())


                xt_count = {}

                def tile_prep(idx, src_ap, row0, xT_list, xT_slot, xT_free):
                    b = idx % 2
                    xs = "x%d" % b
                    P.emit("sync", lambda sy, s=src_ap, r=row0, bb=b: sy.dma_start(
                        out=xa[bb][:], in_=s[r:r + 128, :]),
                        waits=[("cast", idx - 1)], inc=xs, inc_n=16)
                    P.emit("vector", lambda v, bb=b: v.tensor_copy(xab[bb][:], xa[bb][:]),
                           waits=[(xs, 16 * (idx // 2 + 1)), ("tp", idx - 1)])
                    P.emit("vector", lambda v: v.drain(), inc="cast")
                    def tps(te, bb=b):
                        ins = None
                        for dc in range(8):
                            ins = te.transpose(pT[:, dc, :], xab[bb][:, dc * 128:(dc + 1) * 128], ident[:])
                        return ins
                    P.emit("tensor", tps, waits=[("cast", idx + 1), ("xt", idx)], inc="tp")
                    P.emit("vector", lambda v, sl=xT_slot: v.tensor_copy(xT_list[sl][:], pT[:]),
                           waits=[("tp", idx + 1)] + list(xT_free))
                    P.emit("vector", lambda v: v.drain(), inc="xt")
                    xt_count[idx] = P.val("xt")

                Lraw = [Lraw0, Lraw1]
                rnrm = [rnrm0, rnrm1]
                ext_state = {}
                ext_serial = [0]

                def ext_front(pv_t, off0, jside, dest, col0):
                    se = ext_serial[0]
                    ext_serial[0] += 1
                    lb = se % 2
                    P.emit("vector", lambda v: v.tensor_copy(kk_sb[:], pv_t[:, off0:off0 + 128]),
                           waits=[("mmv", P.val("mmv"))])
                    P.emit("vector", lambda v: v.drain())
                    o1, o2 = 0, 64
                    for c, (i, j) in enumerate(PAIRS):
                        P.emit("vector", lambda v, i=i, j=j, c=c: v.tensor_mul(
                            prodAll[:, 2 * c, :], kk_sb[:, o1 + i: o1 + 64: 4], kk_sb[:, o2 + j: o2 + 64: 4]))
                        P.emit("vector", lambda v, i=i, j=j, c=c: v.tensor_mul(
                            prodAll[:, 2 * c + 1, :], kk_sb[:, o1 + j: o1 + 64: 4], kk_sb[:, o2 + i: o2 + 64: 4]))
                    P.emit("vector", lambda v: v.drain())
                    for c in range(C6):
                        P.emit("vector", lambda v, c=c, lb=lb: v.tensor_tensor(
                            Lraw[lb][:, :, c], prodAll[:, 2 * c, :], prodAll[:, 2 * c + 1, :], ALU.subtract))
                    P.emit("vector", lambda v: v.drain())
                    P.emit("vector", lambda v, lb=lb: v.tensor_tensor(Lsq[:], Lraw[lb][:], Lraw[lb][:], ALU.mult),
                           waits=[("rr", P.val("rr"))])
                    P.emit("vector", lambda v: v.drain())
                    for k in range(3):
                        P.emit("vector", lambda v, k=k: v.tensor_add(
                            sst[:, k, :], Lsq[:, :, 2 * k], Lsq[:, :, 2 * k + 1]))
                    P.emit("vector", lambda v: v.drain())
                    P.emit("vector", lambda v: v.tensor_add(sst[:, 0, :], sst[:, 0, :], sst[:, 1, :]))
                    P.emit("vector", lambda v: v.drain())
                    P.emit("vector", lambda v: v.tensor_add(sst[:, 1, :], sst[:, 0, :], sst[:, 2, :]))
                    P.emit("vector", lambda v: v.drain())
                    P.emit("vector", lambda v: v.tensor_scalar_max(ssb[:], sst[:, 1, :], 1e-24))
                    P.emit("vector", lambda v: v.drain(), inc="ss")
                    P.emit("scalar", lambda s: s.activation(rln[:], ssb[:], AF.Ln),
                           waits=[("ss", P.val("ss")), ("lk", P.val("lk"))])
                    P.emit("scalar", lambda s: s.drain())
                    P.emit("scalar", lambda s, lb=lb: s.activation(rnrm[lb][:], rln[:], AF.Exp, bias=0.0, scale=-0.5))
                    P.emit("scalar", lambda s: s.drain(), inc="rr")
                    ext_state[se] = (jside, dest, col0, lb)

                def ext_back(se):
                    jside, dest, col0, lb = ext_state.pop(se)
                    P.emit("vector", lambda v, lb=lb: v.tensor_scalar_mul(nrnrm[:], rnrm[lb][:], -1.0),
                           waits=[("rr", se + 1), ("tk", P.val("tk"))])
                    P.emit("vector", lambda v: v.drain())
                    for c in range(C6):
                        if jside:
                            oc = 5 - c
                            m = rnrm[lb] if JSIGN[oc] > 0 else nrnrm
                            P.emit("vector", lambda v, c=c, oc=oc, m=m, lb=lb: v.tensor_tensor(
                                Lsc[:, :, oc], Lraw[lb][:, :, c], m[:], ALU.mult))
                        else:
                            P.emit("vector", lambda v, c=c, lb=lb: v.tensor_tensor(
                                Lsc[:, :, c], Lraw[lb][:, :, c], rnrm[lb][:], ALU.mult))
                    P.emit("vector", lambda v: v.drain(), inc="lk")
                    head_transposes(dest, col0)

                def head_transposes(dest, col0):
                    def tps(te):
                        ins = None
                        for h in range(H):
                            g_, m = h % 4, h // 4
                            ins = te.transpose(
                                pTk[32 * g_:32 * g_ + 6, m, :], Lsc[:, h, :], ident[:],
                                tile_position=(0, 32 * g_))
                        return ins
                    P.emit("tensor", tps, waits=[("lk", P.val("lk")), ("jl", P.val("jl"))], inc="tk")
                    P.emit("vector", lambda v: v.tensor_copy(dest[:, :, col0:col0 + 128], pTk[:]),
                           waits=[("tk", P.val("tk"))])
                    P.emit("vector", lambda v: v.drain(), inc="jl")

                def emit_vmm(kt):
                    b = kt % 2
                    def vmm(te, b=b):
                        ins = None
                        for dc in range(8):
                            for (n0, n1) in ((0, 512), (512, 1024), (1024, 1152)):
                                ins = te.matmul(pV[b][:, n0:n1], xT[b][:, dc, :], wvks[:, dc, n0:n1],
                                                start=(dc == 0), stop=(dc == 7))
                        return ins
                    P.emit("tensor", vmm,
                           waits=[("xt", xt_count[kt]), ("ss", kt - 1), ("wA", W_SETUP)], inc="mmv")
                    P.emit("vector", lambda v, b=b, kt=kt: v.tensor_tensor(
                        Vs[:, kt, :, 0:64], pV[b][:, 0:1024], bvb[:], ALU.add),
                        waits=[("mmv", P.val("mmv"))])
                    P.emit("vector", lambda v: v.drain())

                tile_prep(0, x_kv, 0, xT, 0, [("mmv", -1)])
                for kt in range(NKT):
                    if kt + 1 < NKT:
                        tile_prep(kt + 1, x_kv, (kt + 1) * 128, xT, (kt + 1) % 2,
                                  [("mmv", kt)])
                    emit_vmm(kt)
                    ext_front(pV[kt % 2], 1024, True, JLKT, kt * 128)
                    if kt >= 1:
                        ext_back(kt - 1)
                P.emit("vector", lambda v: v.memset(Vs[:, :, :, 64:65], 1.0))
                P.emit("vector", lambda v: v.drain(), inc="jl")

                def emit_qmm(s_):
                    sb_ = s_ % 2
                    pvb = pV[s_ % 2]
                    def qmm(te, sb_=sb_, pvb=pvb):
                        ins = None
                        for kc in range(16):
                            lhs = xT[sb_][:, kc, :] if kc < 8 else xTP[sb_][:, kc - 8, :]
                            ins = te.matmul(pvb[:, 0:128], lhs, wq_sb[:, kc, :],
                                            start=(kc == 0), stop=(kc == 15))
                        return ins
                    P.emit("tensor", qmm,
                           waits=[("xt", xt_count[NKT + 2 * s_ + 1]), ("ss", NKT + s_ - 1), ("wA", W_SETUP)], inc="mmv")

                tile_prep(NKT, xq_h, 0, xT, 0, [("mmv", NKT - 1)])
                tile_prep(NKT + 1, xqp_h, 0, xTP, 0, [("mmv", NKT - 1)])
                for s_ in range(NSLOT):
                    if s_ + 1 < NSLOT:
                        i0 = NKT + 2 * (s_ + 1)
                        tile_prep(i0, xq_h, (s_ + 1) * 128, xT, (s_ + 1) % 2, [("mmv", NKT + s_)])
                        tile_prep(i0 + 1, xqp_h, (s_ + 1) * 128, xTP, (s_ + 1) % 2, [("mmv", NKT + s_)])
                    emit_qmm(s_)
                    ext_front(pV[s_ % 2], 0, False, LQT, s_ * 128)
                    ext_back(NKT + s_ - 1)
                ext_back(NKT + NSLOT - 1)

                if debug >= 3:
                    JL_D3 = P.val("jl")
                    P.emit("gpsimd", lambda g: g.dma_start(out=dbg["kk"], in_=kk_sb[:]),
                           waits=[("jl", JL_D3)], inc="dbgs", inc_n=16)
                    P.emit("gpsimd", lambda g: g.dma_start(out=dbg["lraw"], in_=Lraw[:]), inc="dbgs", inc_n=16)
                    P.emit("gpsimd", lambda g: g.dma_start(out=dbg["ssb"], in_=ssb[:]), inc="dbgs", inc_n=16)
                    P.emit("gpsimd", lambda g: g.dma_start(out=dbg["rnrm"], in_=rnrm[:]), inc="dbgs", inc_n=16)
                    P.emit("gpsimd", lambda g: g.dma_start(out=dbg["lsc"], in_=Lsc[:]), inc="dbgs", inc_n=16)
                if debug >= 1:
                    JL_D = P.val("jl")
                    P.emit("gpsimd", lambda g: g.dma_start(out=dbg["vs"], in_=Vs[:]),
                           waits=[("jl", JL_D)], inc="dbgs", inc_n=16)
                    P.emit("gpsimd", lambda g: g.dma_start(out=dbg["jlkt"], in_=JLKT[:]), inc="dbgs", inc_n=16)
                    P.emit("gpsimd", lambda g: g.dma_start(out=dbg["lqt"], in_=LQT[:]), inc="dbgs", inc_n=16)

                JL_ALL = P.val("jl")
                P.emit("sync", lambda sy: sy.sem_inc(sems["bar1"], 1), waits=[("x0", P.val("x0")), ("x1", P.val("x1"))])
                P.emit("tensor", lambda te: te.sem_inc(sems["bar1"], 1))
                P.emit("vector", lambda v: v.sem_inc(sems["bar1"], 1), waits=[("jl", JL_ALL)])
                P.emit("scalar", lambda s: s.sem_inc(sems["bar1"], 1), waits=[("rr", P.val("rr"))])
                P.emit("gpsimd", lambda g: g.sem_inc(sems["bar1"], 1),
                       waits=[("wA", W_SETUP), ("dbgs", P.val("dbgs"))])
                P.cnt["bar1"] = 5
                P.run_block(nc, sems)
                cnt1 = P.cnt

            if phases < 2:
                for c in sems_ctx:
                    c.__exit__(None, None, None)
                return nc
            with ExitStack() as _stB:
                sbB = lambda *a: _stB.enter_context(nc.sbuf_tensor(*a))
                psB = lambda *a: _stB.enter_context(nc.psum_tensor(*a))
                masks_g0 = sbB("masks_g0", [128, 4, 2, 128], BF16)
                attnT = sbB("attnT", [128, 8, 512], BF16)
                PT0 = sbB("PT0", [128, 512], BF16)
                PT1 = sbB("PT1", [128, 512], BF16)
                PT2 = sbB("PT2", [128, 512], BF16)
                PT3 = sbB("PT3", [128, 512], BF16)
                
                wo_sb = sbB("wo_sb", [128, 8, 1024], BF16)
                bob = sbB("bob", [128, D], F32)
                srow_b = sbB("srow_b", [1, 512], BF16)
                ones1b = sbB("ones1b", [1, 64], BF16)
                
                rbl_t = sbB("rbl", [64, 512], F32)
                rbr0 = sbB("rbr0", [64, 512], F32)
                rbr1 = sbB("rbr1", [64, 512], F32)
                xqr_t = sbB("xqr", [128, D], F32)
                y1a_t = sbB("y1a", [128, D], F32)
                y1b_t = sbB("y1b", [128, D], F32)
                xlo_t = sbB("xlo", [128, D], BF16)
                xfT_t = sbB("xfT_t", [128, 8, 128], BF16)
                xfTlo_t = sbB("xfTlo_t", [128, 8, 128], BF16)
                stq = sbB("stq", [128, 4], F32)
                rstd1 = sbB("rstd1", [128, 1], F32)
                rln1 = sbB("rln1", [128, 1], F32)
                e8_t = sbB("e8", [128, NE], F32)
                nmx_t = sbB("nmx", [128, 1], F32)
                s8_t = sbB("s8", [128, 1], F32)
                scr8_t = sbB("scr8", [128, NE], F32)
                st8_t = sbB("st8", [128, 8], F32)
                s2t_t = sbB("s2t", [128, NE], F32)
                mask8_t = sbB("mask8", [128, NE], F32)
                
                pS0 = psB("pS0", [128, 512], F32)
                pS1 = psB("pS1", [128, 512], F32)
                pS2 = psB("pS2", [128, 512], F32)
                pAV0 = psB("pAV0", [65, 512], F32)
                pAV1 = psB("pAV1", [65, 512], F32)
                
                pWO0 = psB("pWO0", [128, 1024], F32)
                
                pT2 = psB("pT2", [128, 8, 128], BF16)
                P = Prog(cnt1)
                PT = [PT0, PT1, PT2, PT3]
                pS, pAV = [pS0, pS1, pS2], [pAV0, pAV1]
                pWO = pWO0
                rbr = [rbr0, rbr1]
                masks_g = masks_g0

                P.emit("gpsimd", lambda g: g.dma_start(
                    out=wo_sb[:], in_=wo_h.rearrange("(dc p) n -> p dc n", p=128)), inc="wB", inc_n=16)
                P.emit("gpsimd", lambda g: g.dma_start(out=bob[:], in_=bob_h), inc="wB", inc_n=16)
                W_B = P.val("wB")
                P.emit("vector", lambda v: v.memset(ones1b[:], 1.0), waits=[("bar1", 5)])
                P.emit("vector", lambda v: v.drain(), inc="xt")
                ONES_RDY = P.val("xt")

                sbank_free = [None, None, None]
                ptslot_free = [None, None, None, None]
                avbank_free = [None, None]
                exp_serial = [0]
                wo_prev_g = 0

                def emit_score(g, h, kc):
                    gr, m = h % 4, h // 4
                    jlo = max(4 * g, kc // 2)
                    off = (jlo - 4 * g) * 128
                    sb = exp_serial[0] % 3
                    pslot = exp_serial[0] % 4
                    exp_serial[0] += 1
                    waits = []
                    if sbank_free[sb]:
                        waits.append(sbank_free[sb])
                    if h <= 1 and g == 0 and kc < 2:
                        waits.append(("jl", JL_ALL))
                    def scmm(te, sb=sb, off=off, gr=gr, m=m, kc=kc, g=g):
                        return te.matmul(
                            pS[sb][:, off:512],
                            JLKT[32 * gr:32 * gr + 6, m, kc * 128:(kc + 1) * 128],
                            LQT[32 * gr:32 * gr + 6, m, g * 512 + off:(g + 1) * 512],
                            start=True, stop=True, tile_position=(32 * gr, 0))
                    P.emit("tensor", scmm, waits=waits, inc="sc")
                    ncols = 512 - off
                    ew = [("sc", P.val("sc"))]
                    if ptslot_free[pslot]:
                        ew.append(ptslot_free[pslot])
                    P.emit("scalar", lambda s, sb=sb, off=off, pslot=pslot, ncols=ncols: s.activation(
                        PT[pslot][:, 0:ncols], pS[sb][:, off:512], AF.Exp, bias=0.0, scale=SCALE),
                        waits=ew, inc="exp")
                    sbank_free[sb] = ("exp", P.val("exp"))
                    masked = kc >= 8 * g
                    if masked:
                        jj, ii = kc // 2 - 4 * g, kc % 2
                        P.emit("vector", lambda v, pslot=pslot, jj=jj, ii=ii, g=g: v.tensor_mul(
                            PT[pslot][:, 0:128], PT[pslot][:, 0:128], masks_g[:, jj, ii, :]),
                            waits=[("exp", P.val("exp")), MSK_RDY[0]])
                        P.emit("vector", lambda v: v.drain(), inc="msk")
                    return (pslot, off, ncols, masked, P.val("exp"), P.val("msk"))

                def emit_av(g, h, kc, sinfo):
                    pslot, off, ncols, masked, expv, mskv = sinfo
                    hb = h % 2
                    K = kmax(g)
                    avw = [("msk", mskv)] if masked else [("exp", expv)]
                    if kc == 0 and avbank_free[hb]:
                        avw.append(avbank_free[hb])
                    def avmm(te, hb=hb, off=off, kc=kc, h=h, ncols=ncols, K=K, pslot=pslot):
                        return te.matmul(
                            pAV[hb][:, off:512], Vs[:, kc, h, :], PT[pslot][:, 0:ncols],
                            start=(kc == 0), stop=(kc == K - 1))
                    P.emit("tensor", avmm, waits=avw, inc="av")
                    ptslot_free[pslot] = ("av", P.val("av"))

                for g in range(NG):
                    av_counts, sr_vals, rbm_vals, rbr_vals = {}, {}, {}, {}
                    last_rbr = P.val("rbr")
                    P.emit("gpsimd", lambda gq, g=g: gq.dma_start(
                        out=masks_g[:], in_=masks_h[:, 4 * g:4 * g + 4, :, :]),
                        waits=[("bar1", 5), ("msk", P.cnt.get("_mskuse_%d" % (g - 1), 0))],
                        inc="mk0", inc_n=16)
                    MSK_RDY = [("mk0", P.val("mk0"))]
                    K = kmax(g)

                    def emit_chain_pre(hh):
                        hbb = hh % 2
                        srw = [("av", av_counts[hh])]
                        if rbm_vals:
                            srw.append(("rbm", max(rbm_vals.values())))
                        P.emit("vector", lambda v, hbb=hbb: v.tensor_copy(
                            srow_b[0:1, :], pAV[hbb][64:65, :]), waits=srw)
                        P.emit("vector", lambda v: v.drain(), inc="sr")
                        sr_vals[hh] = P.val("sr")
                        rw = [("sr", sr_vals[hh]), ("xt", ONES_RDY)]
                        prev_rbr = rbr_vals[hh - 1] if (hh - 1) in rbr_vals else last_rbr
                        if prev_rbr:
                            rw.append(("rbr", prev_rbr))
                        rw.append(("y1a", P.val("y1a")))
                        def rbmm(te):
                            return te.matmul(pWO[0:64, 512:1024], ones1b[:], srow_b[0:1, :], start=True, stop=True)
                        P.emit("tensor", rbmm, waits=rw, inc="rbm")
                        rbm_vals[hh] = P.val("rbm")

                    def emit_chain_act(hh):
                        hbb = hh % 2
                        P.emit("scalar", lambda s: s.activation(rbl_t[:], pWO[0:64, 512:1024], AF.Ln),
                               waits=[("rbm", rbm_vals[hh])])
                        P.emit("scalar", lambda s: s.drain())
                        P.emit("scalar", lambda s, hbb=hbb: s.activation(
                            rbr[hbb][:], rbl_t[:], AF.Exp, bias=0.0, scale=-1.0))
                        P.emit("scalar", lambda s: s.drain(), inc="rbr")
                        rbr_vals[hh] = P.val("rbr")

                    def emit_chain_post(hh):
                        hbb = hh % 2
                        cp, half = hh // 2, (hh % 2) * 64
                        P.emit("vector", lambda v, hbb=hbb, cp=cp, half=half: v.tensor_tensor(
                            attnT[half:half + 64, cp, :], pAV[hbb][0:64, :], rbr[hbb][:], ALU.mult),
                            waits=[("rbr", rbr_vals[hh]), ("wo", wo_prev_g)])
                        P.emit("vector", lambda v: v.drain(), inc="ate")
                        avbank_free[hbb] = ("ate", P.val("ate"))

                    def emit_chain(hh):
                        emit_chain_pre(hh)
                        emit_chain_act(hh)
                        emit_chain_post(hh)

                    for p in range(8):
                        a, b = 2 * p, 2 * p + 1
                        prev_a = prev_b = None
                        for kc in range(K):
                            sinfo_a = emit_score(g, a, kc)
                            sinfo_b = emit_score(g, b, kc)
                            if p >= 1:
                                if kc == 0:
                                    emit_chain_pre(2 * p - 2)
                                    emit_chain_act(2 * p - 2)
                                    emit_chain_pre(2 * p - 1)
                                    emit_chain_act(2 * p - 1)
                                elif kc == 1:
                                    emit_chain_post(2 * p - 2)
                                    emit_chain_post(2 * p - 1)
                            if kc >= 1:
                                emit_av(g, a, kc - 1, prev_a)
                                emit_av(g, b, kc - 1, prev_b)
                            prev_a, prev_b = sinfo_a, sinfo_b
                        emit_av(g, a, K - 1, prev_a)
                        emit_av(g, b, K - 1, prev_b)
                        av_counts[a] = P.val("av") - 1
                        av_counts[b] = P.val("av")
                    emit_chain(14)
                    emit_chain(14)
                    emit_chain(15)

                    ATE_G = P.val("ate")
                    for qq in range(4):
                        qt = 4 * g + qq
                        qb = qt % 2
                        P.emit("sync", lambda sy, qt=qt: sy.dma_start(
                            out=xqr_t[:], in_=xq_h[qt * 128:(qt + 1) * 128, :]),
                            waits=[("bar1", 5), ("y1a", P.val("y1a"))], inc="xqd", inc_n=16)
                        def womm(te, qq=qq):
                            ins = None
                            for cp in range(8):
                                for nb in range(2):
                                    ins = te.matmul(
                                        pWO[:, nb * 512:(nb + 1) * 512],
                                        attnT[:, cp, qq * 128:(qq + 1) * 128],
                                        wo_sb[:, cp, nb * 512:(nb + 1) * 512],
                                        start=(cp == 0), stop=(cp == 7))
                            return ins
                        wow = [("ate", ATE_G), ("wB", W_B), ("y1a", P.val("y1a"))]
                        P.emit("tensor", womm, waits=wow, inc="wo")
                        WO_T = P.val("wo")
                        P.emit("vector", lambda v: v.tensor_tensor(y1a_t[:], pWO[:], bob[:], ALU.add),
                               waits=[("wo", WO_T), ("xqd", P.val("xqd"))])
                        P.emit("vector", lambda v: v.drain())
                        P.emit("vector", lambda v: v.scalar_tensor_tensor(
                            y1b_t[:], y1a_t[:], 1.0, xqr_t[:], ALU.mult, ALU.add, accum_out=stq[:, 0:1]))
                        P.emit("vector", lambda v: v.drain(), inc="y1a")
                        Y1A = P.val("y1a")
                        P.emit("vector", lambda v: v.tensor_scalar(stq[:, 0:1], stq[:, 0:1], 1.0 / D, None, ALU.mult))
                        P.emit("vector", lambda v: v.drain())
                        P.emit("vector", lambda v: v.tensor_scalar(y1a_t[:], y1b_t[:], stq[:, 0:1], None, ALU.subtract))
                        P.emit("vector", lambda v: v.drain())
                        P.emit("vector", lambda v: v.scalar_tensor_tensor(
                            y1b_t[:], y1a_t[:], 1.0, y1a_t[:], ALU.mult, ALU.mult, accum_out=stq[:, 1:2]))
                        P.emit("vector", lambda v: v.drain(), inc="v1")
                        P.emit("scalar", lambda s: s.activation(rln1[:], stq[:, 1:2], AF.Ln,
                                                                bias=epsb[:, 0:1], scale=1.0 / D),
                               waits=[("v1", P.val("v1"))])
                        P.emit("scalar", lambda s: s.drain())
                        P.emit("scalar", lambda s: s.activation(rstd1[:], rln1[:], AF.Exp, bias=0.0, scale=-0.5))
                        P.emit("scalar", lambda s: s.drain(), inc="sd1")
                        P.emit("vector", lambda v: v.tensor_scalar(y1b_t[:], y1a_t[:], rstd1[:, 0:1], None, ALU.mult),
                               waits=[("sd1", P.val("sd1"))])
                        P.emit("vector", lambda v: v.drain())
                        P.emit("vector", lambda v, qt=qt: v.tensor_copy(acc_t[:, qt, :], y1b_t[:]))
                        P.emit("vector", lambda v: v.drain())
                        P.emit("vector", lambda v, qt=qt: v.tensor_tensor(xlo_t[:], y1b_t[:], acc_t[:, qt, :], ALU.subtract))
                        P.emit("vector", lambda v: v.drain(), inc="xf")
                        XF = P.val("xf")
                        def tph(te, qt=qt):
                            ins = None
                            for dc in range(8):
                                ins = te.transpose(pT2[:, dc, :], acc_t[:, qt, dc * 128:(dc + 1) * 128], ident[:])
                            return ins
                        P.emit("tensor", tph, waits=[("xf", XF), ("xft", P.val("xft"))], inc="tpf")
                        P.emit("vector", lambda v: v.tensor_copy(xfT_t[:], pT2[:]),
                               waits=[("tpf", P.val("tpf"))])
                        P.emit("vector", lambda v: v.drain(), inc="xft")
                        def tpl(te):
                            ins = None
                            for dc in range(8):
                                ins = te.transpose(pT2[:, dc, :], xlo_t[:, dc * 128:(dc + 1) * 128], ident[:])
                            return ins
                        P.emit("tensor", tpl, waits=[("xft", P.val("xft"))], inc="tpf")
                        P.emit("vector", lambda v: v.tensor_copy(xfTlo_t[:], pT2[:]),
                               waits=[("tpf", P.val("tpf"))])
                        P.emit("vector", lambda v: v.drain(), inc="xft")
                        XFT = P.val("xft")
                        rtw = [("xft", XFT)]
                        if sbank_free[qb]:
                            rtw.append(sbank_free[qb])
                        def rtmm(te, qb=qb):
                            ins = None
                            srcs = ((xfT_t, rtr_hi), (xfT_t, rtr_lo), (xfTlo_t, rtr_hi))
                            n = 0
                            for src, w in srcs:
                                for dc in range(8):
                                    ins = te.matmul(pS[qb][:, 0:8], src[:, dc, :], w[:, dc, :],
                                                    start=(n == 0), stop=(n == 23))
                                    n += 1
                            return ins
                        P.emit("tensor", rtmm, waits=rtw, inc="rt")
                        P.emit("vector", lambda v, qb=qb: v.reduce_max(
                            nmx_t[:], pS[qb][:, 0:8], axis=AX.X, negate=True),
                            waits=[("rt", P.val("rt"))])
                        P.emit("vector", lambda v: v.drain(), inc="rmax")
                        P.emit("scalar", lambda s, qb=qb: s.activation(
                            e8_t[:], pS[qb][:, 0:8], AF.Exp, bias=nmx_t[:, 0:1], scale=1.0),
                            waits=[("rmax", P.val("rmax"))])
                        P.emit("scalar", lambda s: s.drain(), inc="rexp")
                        sbank_free[qb] = ("rexp", P.val("rexp"))
                        P.emit("vector", lambda v: v.reduce_sum(s8_t[:], e8_t[:], axis=AX.X),
                               waits=[("rexp", P.val("rexp"))])
                        P.emit("vector", lambda v: v.drain())
                        P.emit("vector", lambda v: v.reciprocal(nmx_t[:], s8_t[:]))
                        P.emit("vector", lambda v: v.drain())
                        P.emit("vector", lambda v: v.tensor_scalar(scr8_t[:], e8_t[:], nmx_t[:, 0:1], None, ALU.mult))
                        P.emit("vector", lambda v: v.drain())
                        P.emit("vector", lambda v: v.max(st8_t[:], scr8_t[:]))
                        P.emit("vector", lambda v: v.drain())
                        P.emit("vector", lambda v: v.memset(st8_t[:, 2:8], -1.0e30))
                        P.emit("vector", lambda v: v.drain())
                        P.emit("vector", lambda v: v.match_replace(s2t_t[:], st8_t[:], scr8_t[:], -2.0e30))
                        P.emit("vector", lambda v: v.drain())
                        P.emit("vector", lambda v: v.tensor_single_scalar(mask8_t[:], s2t_t[:], -1.0e29, ALU.is_lt))
                        P.emit("vector", lambda v: v.drain())
                        P.emit("vector", lambda v: v.tensor_mul(mask8_t[:], mask8_t[:], scr8_t[:]))
                        P.emit("vector", lambda v: v.tensor_add(stq[:, 2:3], st8_t[:, 0:1], st8_t[:, 1:2]))
                        P.emit("vector", lambda v: v.drain())
                        P.emit("vector", lambda v: v.tensor_scalar(s2t_t[:, 0:1], stq[:, 2:3], 1.0, 1e-20, ALU.mult, ALU.add))
                        P.emit("vector", lambda v: v.drain())
                        P.emit("vector", lambda v: v.reciprocal(s8_t[:], s2t_t[:, 0:1]))
                        P.emit("vector", lambda v: v.drain())
                        P.emit("vector", lambda v, qt=qt: v.tensor_scalar(
                            comb_t[:, qt, :], mask8_t[:], s8_t[:, 0:1], None, ALU.mult))
                        P.emit("vector", lambda v: v.drain(), inc="cmb")

                    wo_prev_g = P.val("wo")
                    P.cnt["_mskuse_%d" % g] = P.val("msk")

                if debug >= 2:
                    P.emit("gpsimd", lambda g: g.dma_start(out=dbg["acc"], in_=acc_t[:]),
                           waits=[("xf", P.val("xf"))], inc="dbgs", inc_n=16)
                    P.emit("gpsimd", lambda g: g.dma_start(out=dbg["comb"], in_=comb_t[:]),
                           waits=[("cmb", P.val("cmb"))], inc="dbgs", inc_n=16)
                    P.emit("gpsimd", lambda g: g.dma_start(out=dbg["attnT"], in_=attnT[:]),
                           waits=[("ate", P.val("ate"))], inc="dbgs", inc_n=16)

                P.emit("sync", lambda sy: sy.sem_inc(sems["bar2"], 1), waits=[("xqd", P.val("xqd"))])
                P.emit("tensor", lambda te: te.sem_inc(sems["bar2"], 1))
                P.emit("vector", lambda v: v.sem_inc(sems["bar2"], 1), waits=[("cmb", P.val("cmb"))])
                P.emit("scalar", lambda s: s.sem_inc(sems["bar2"], 1))
                P.emit("gpsimd", lambda g: g.sem_inc(sems["bar2"], 1),
                       waits=[("wB", W_B), ("dbgs", P.val("dbgs"))])
                P.cnt["bar2"] = 5
                P.run_block(nc, sems)
                cnt2 = P.cnt

        if phases < 3:
            for c in sems_ctx:
                c.__exit__(None, None, None)
            return nc
        with ExitStack() as _stD:
            sbD = lambda *a: _stD.enter_context(nc.sbuf_tensor(*a))
            psD = lambda *a: _stD.enter_context(nc.psum_tensor(*a))
            xfT = sbD("xfT", [128, 8, NSLOT * 128], BF16)
            ht_sb = sbD("ht", [128, NHC, NSLOT * 128], BF16)
            wg_sb = sbD("wg_sb", [128, NHC * 8, 128], BF16)
            wu_sb = sbD("wu_sb", [128, NHC * 8, 128], BF16)
            wd_sb = sbD("wd_sb", [128, NHC, 1024], BF16)
            sG0 = sbD("sG0", [128, 512], BF16)
            sG1 = sbD("sG1", [128, 512], BF16)
            xc2 = sbD("xc2", [128, D], F32)
            sq2 = sbD("sq2", [128, D], F32)
            st2 = sbD("st2", [128, 2], F32)
            st3 = sbD("st3", [128, 1], F32)
            rstd2 = sbD("rstd2", [128, 1], F32)
            rln2 = sbD("rln2", [128, 1], F32)
            outst0 = sbD("outst0", [128, D], F32)
            outst1 = sbD("outst1", [128, D], F32)
            pG0 = psD("pG0", [128, 512], F32)
            pG1 = psD("pG1", [128, 512], F32)
            pU0 = psD("pU0", [128, 512], F32)
            pU1 = psD("pU1", [128, 512], F32)
            pD0 = psD("pD0", [128, 1024], F32)
            
            pT3 = psD("pT3", [128, 8, 128], BF16)
            P = Prog(cnt2)
            pG, pU = [pG0, pG1], [pU0, pU1]
            pD = pD0
            sG = [sG0, sG1]
            outst = [outst0, outst1]

            for t in range(NSLOT):
                def tpx(te, t=t):
                    ins = None
                    for dc in range(8):
                        ins = te.transpose(pT3[:, dc, :], acc_t[:, t, dc * 128:(dc + 1) * 128], ident[:])
                    return ins
                P.emit("tensor", tpx, waits=[("bar2", 5), ("xft", P.val("xft"))], inc="tpf")
                P.emit("vector", lambda v, t=t: v.tensor_copy(xfT[:, :, t * 128:(t + 1) * 128], pT3[:]),
                       waits=[("bar2", 5), ("tpf", P.val("tpf"))])
                P.emit("vector", lambda v: v.drain(), inc="xft")
            XFT_ALL = P.val("xft")

            for e in range(NE):
                for hc in range(NHC):
                    P.emit("gpsimd", lambda g, e=e, hc=hc: g.dma_start(
                        out=wg_sb[:, hc * 8:(hc + 1) * 8, :],
                        in_=wg_h[e][:, hc * 128:(hc + 1) * 128].rearrange("(dc p) q -> p dc q", p=128)),
                        waits=[("bar2", 5), ("gmm", 44 * (e - 1) + 4 * (hc + 1) if e else 0)],
                        inc="wg", inc_n=16)
                    P.emit("gpsimd", lambda g, e=e, hc=hc: g.dma_start(
                        out=wu_sb[:, hc * 8:(hc + 1) * 8, :],
                        in_=wu_h[e][:, hc * 128:(hc + 1) * 128].rearrange("(dc p) q -> p dc q", p=128)),
                        waits=[("umm", 44 * (e - 1) + 4 * (hc + 1) if e else 0)],
                        inc="wu", inc_n=16)
                P.emit("gpsimd", lambda g, e=e: g.dma_start(
                    out=wd_sb[:], in_=wd_h[e].rearrange("(hc p) n -> p hc n", p=128)),
                    waits=[("dmm", 16 * e)], inc="wd", inc_n=16)

            for e in range(NE):
                for hc in range(NHC):
                    for tc in range(4):
                        gi = (hc * 4 + tc) % 2
                        def gmm(te, hc=hc, tc=tc, gi=gi, e=e):
                            ins = None
                            for dc in range(8):
                                ins = te.matmul(pG[gi][:], wg_sb[:, hc * 8 + dc, :],
                                                xfT[:, dc, tc * 512:(tc + 1) * 512],
                                                start=(dc == 0), stop=(dc == 7))
                            return ins
                        gwaits = [("wg", 16 * NHC * (e + 1)), ("xft", XFT_ALL),
                                  ("sg", P.val("sg") - 1)]
                        if e and hc == 0 and tc == 0:
                            gwaits.append(("acc", 16 * e))
                        P.emit("tensor", gmm, waits=gwaits, inc="gmm")
                        def umm(te, hc=hc, tc=tc, gi=gi, e=e):
                            ins = None
                            for dc in range(8):
                                ins = te.matmul(pU[gi][:], wu_sb[:, hc * 8 + dc, :],
                                                xfT[:, dc, tc * 512:(tc + 1) * 512],
                                                start=(dc == 0), stop=(dc == 7))
                            return ins
                        P.emit("tensor", umm,
                               waits=[("wu", 16 * NHC * (e + 1)), ("hm", P.val("hm") - 1)], inc="umm")
                        P.emit("scalar", lambda s, gi=gi: s.activation(sG[gi][:], pG[gi][:], AF.Silu),
                               waits=[("gmm", P.val("gmm")), ("hm", P.val("hm") - 1)])
                        P.emit("scalar", lambda s: s.drain(), inc="sg")
                        P.emit("vector", lambda v, gi=gi, hc=hc, tc=tc: v.tensor_tensor(
                            ht_sb[:, hc, tc * 512:(tc + 1) * 512], sG[gi][:], pU[gi][:], ALU.mult),
                            waits=[("sg", P.val("sg")), ("umm", P.val("umm"))])
                        P.emit("vector", lambda v: v.drain(), inc="hm")
                HM_E = P.val("hm")
                for t in range(NSLOT):
                    odd = t % 2 == 1
                    if odd:
                        def dmm(te, t=t):
                            ins = None
                            for hc in range(NHC):
                                ins = te.matmul(pG0[:], ht_sb[:, hc, t * 128:(t + 1) * 128],
                                                wd_sb[:, hc, 0:512],
                                                start=(hc == 0), stop=(hc == NHC - 1))
                                ins = te.matmul(pG1[:], ht_sb[:, hc, t * 128:(t + 1) * 128],
                                                wd_sb[:, hc, 512:1024],
                                                start=(hc == 0), stop=(hc == NHC - 1))
                            return ins
                    else:
                        def dmm(te, t=t):
                            ins = None
                            for hc in range(NHC):
                                for nb in range(2):
                                    ins = te.matmul(pD[:, nb * 512:(nb + 1) * 512],
                                                    ht_sb[:, hc, t * 128:(t + 1) * 128],
                                                    wd_sb[:, hc, nb * 512:(nb + 1) * 512],
                                                    start=(hc == 0), stop=(hc == NHC - 1))
                            return ins
                    P.emit("tensor", dmm,
                           waits=[("wd", 16 * (e + 1)), ("hm", HM_E),
                                  ("acc", max(0, P.val("acc") - 1))],
                           inc="dmm")
                    if odd:
                        if e == NE - 1:
                            P.emit("vector", lambda v, e=e, t=t: v.scalar_tensor_tensor(
                                acc_t[:, t, 0:512], pG0[:], comb_t[:, t, e:e + 1], acc_t[:, t, 0:512],
                                ALU.mult, ALU.add, accum_out=st2[:, 0:1]),
                                waits=[("dmm", P.val("dmm"))])
                            P.emit("vector", lambda v, e=e, t=t: v.scalar_tensor_tensor(
                                acc_t[:, t, 512:1024], pG1[:], comb_t[:, t, e:e + 1], acc_t[:, t, 512:1024],
                                ALU.mult, ALU.add, accum_out=st3[:, 0:1]))
                            P.emit("vector", lambda v: v.drain())
                            P.emit("vector", lambda v: v.tensor_add(st2[:, 0:1], st2[:, 0:1], st3[:, 0:1]))
                        else:
                            P.emit("vector", lambda v, e=e, t=t: v.scalar_tensor_tensor(
                                acc_t[:, t, 0:512], pG0[:], comb_t[:, t, e:e + 1], acc_t[:, t, 0:512],
                                ALU.mult, ALU.add),
                                waits=[("dmm", P.val("dmm"))])
                            P.emit("vector", lambda v, e=e, t=t: v.scalar_tensor_tensor(
                                acc_t[:, t, 512:1024], pG1[:], comb_t[:, t, e:e + 1], acc_t[:, t, 512:1024],
                                ALU.mult, ALU.add))
                    elif e == NE - 1:
                        P.emit("vector", lambda v, e=e, t=t: v.scalar_tensor_tensor(
                            acc_t[:, t, :], pD[:], comb_t[:, t, e:e + 1], acc_t[:, t, :],
                            ALU.mult, ALU.add, accum_out=st2[:, 0:1]),
                            waits=[("dmm", P.val("dmm"))])
                    else:
                        P.emit("vector", lambda v, e=e, t=t: v.scalar_tensor_tensor(
                            acc_t[:, t, :], pD[:], comb_t[:, t, e:e + 1], acc_t[:, t, :],
                            ALU.mult, ALU.add),
                            waits=[("dmm", P.val("dmm"))])
                    P.emit("vector", lambda v: v.drain(), inc="acc")
                    if e == NE - 1:
                        ob = t % 2
                        P.emit("vector", lambda v: v.tensor_scalar(st2[:, 0:1], st2[:, 0:1], 1.0 / D, None, ALU.mult),
                               waits=[("acc", P.val("acc"))])
                        P.emit("vector", lambda v: v.drain())
                        P.emit("vector", lambda v, t=t: v.tensor_scalar(xc2[:], acc_t[:, t, :], st2[:, 0:1], None, ALU.subtract))
                        P.emit("vector", lambda v: v.drain())
                        P.emit("vector", lambda v: v.scalar_tensor_tensor(
                            sq2[:], xc2[:], 1.0, xc2[:], ALU.mult, ALU.mult, accum_out=st2[:, 1:2]))
                        P.emit("vector", lambda v: v.drain(), inc="ln2")
                        P.emit("scalar", lambda s: s.activation(rln2[:], st2[:, 1:2], AF.Ln,
                                                                bias=epsb[:, 0:1], scale=1.0 / D),
                               waits=[("ln2", P.val("ln2"))])
                        P.emit("scalar", lambda s: s.drain())
                        P.emit("scalar", lambda s: s.activation(rstd2[:], rln2[:], AF.Exp, bias=0.0, scale=-0.5))
                        P.emit("scalar", lambda s: s.drain(), inc="sd2")
                        P.emit("vector", lambda v, ob=ob: v.tensor_scalar(
                            outst[ob][:], xc2[:], rstd2[:, 0:1], None, ALU.mult),
                            waits=[("sd2", P.val("sd2")), ("out", 16 * t)])
                        P.emit("vector", lambda v: v.drain(), inc="eo")
                        P.emit("sync", lambda sy, t=t, ob=ob: sy.dma_start(
                            out=out_h[t * 128:(t + 1) * 128, :], in_=outst[ob][:]),
                            waits=[("eo", P.val("eo"))], inc="out", inc_n=16)

            P.emit("sync", lambda sy: sy.nop(), waits=[("out", 16 * NSLOT)])
            P.run_block(nc, sems)

        for c in sems_ctx:
            c.__exit__(None, None, None)
    return nc



def _make_masks(tiles):
    m = np.zeros((128, NSLOT, 2, 128), np.float32)
    tri = (np.arange(128)[:, None] <= np.arange(128)[None, :]).astype(np.float32)
    for j, t in enumerate(tiles):
        c = t + 1
        for i in range(2):
            kc = 2 * j + i
            if kc < c - 1:
                m[:, j, i, :] = 1.0
            elif kc == c - 1:
                m[:, j, i, :] = tri
    return np.ascontiguousarray(m.astype(ml_dtypes.bfloat16))


def kernel(L0_attn_x, L0_attn_W1_q, L0_attn_W2_q, L0_attn_W1_k, L0_attn_W2_k,
           L0_attn_W_v, L0_attn_b_v, L0_attn_W_o, L0_attn_b_o,
           L0_ffn_w_router, L0_ffn_w_gate, L0_ffn_w_up, L0_ffn_w_down,
           _debug=0):
    x = np.ascontiguousarray(np.asarray(L0_attn_x, dtype=np.float32))
    arrs = {
        "W1_q": L0_attn_W1_q, "W2_q": L0_attn_W2_q, "W1_k": L0_attn_W1_k,
        "W2_k": L0_attn_W2_k, "W_v": L0_attn_W_v, "W_o": L0_attn_W_o,
        "w_router": L0_ffn_w_router, "w_gate": L0_ffn_w_gate,
        "w_up": L0_ffn_w_up, "w_down": L0_ffn_w_down,
    }
    arrs = {k: np.ascontiguousarray(np.asarray(v, dtype=np.float32)) for k, v in arrs.items()}
    bv = np.asarray(L0_attn_b_v, dtype=np.float32)
    bo = np.asarray(L0_attn_b_o, dtype=np.float32)
    arrs["b_v_b"] = np.ascontiguousarray(np.broadcast_to(bv, (128, D)))
    arrs["b_o_b"] = np.ascontiguousarray(np.broadcast_to(bo, (128, D)))
    arrs["ident_hbm"] = np.eye(128, dtype=np.float32)
    masks = {0: _make_masks(TILES_A), 1: _make_masks(TILES_B)}

    in_maps = []
    for c in range(8):
        b, v_ = c // 2, c % 2
        tiles = TILES_A if v_ == 0 else TILES_B
        rows = np.concatenate([np.arange(t * 128, (t + 1) * 128) for t in tiles])
        xb = x[b]
        prev = rows - 1
        xqp = np.where((prev >= 0)[:, None], xb[np.maximum(prev, 0)], np.float32(0.0))
        m = dict(arrs)
        m["x_kv"] = xb
        m["xq"] = np.ascontiguousarray(xb[rows])
        m["xq_prev"] = np.ascontiguousarray(xqp.astype(np.float32))
        m["masks"] = masks[v_]
        in_maps.append(m)

    nc = build_program(debug=_debug, phases=_PHASES)
    res = run_bass_kernel_spmd(nc, in_maps, core_ids=list(range(8)))

    out = np.zeros((B, T, D), np.float32)
    for c in range(8):
        b, v_ = c // 2, c % 2
        tiles = TILES_A if v_ == 0 else TILES_B
        oq = np.asarray(res.results[c]["out_q"])
        for j, t in enumerate(tiles):
            out[b, t * 128:(t + 1) * 128, :] = oq[j * 128:(j + 1) * 128, :]
    if _debug:
        return out, res
    return out


# version 20
# speedup vs baseline: 1.1230x; 1.0044x over previous
from contextlib import ExitStack

import numpy as np
import ml_dtypes
import concourse.bass as bass
import concourse.mybir as mybir
from concourse.bass_utils import run_bass_kernel_spmd

F32 = mybir.dt.float32
BF16 = mybir.dt.bfloat16
AF = mybir.ActivationFunctionType
ALU = mybir.AluOpType
AX = mybir.AxisListType

B, T, D, H, DH, C6 = 4, 4096, 1024, 16, 64, 6
NKT = 32
NSLOT = 16
NG = 4
HFF = 1408
NE = 8
NHC = 11
SCALE = float(6.0 ** -0.5)
PAIRS = [(0, 1), (0, 2), (0, 3), (1, 2), (1, 3), (2, 3)]
JSIGN = [1.0, -1.0, 1.0, 1.0, -1.0, 1.0]
TILES_A = [0, 3, 4, 7, 8, 11, 12, 15, 16, 19, 20, 23, 24, 27, 28, 31]
TILES_B = [1, 2, 5, 6, 9, 10, 13, 14, 17, 18, 21, 22, 25, 26, 29, 30]
ENGINES = ("sync", "tensor", "vector", "scalar", "gpsimd")
_PHASES = 3
_SKIP_Q = False


def kmax(g):
    return 8 * g + 8


class Prog:

    def __init__(self, cnt=None):
        self.ops = {e: [] for e in ENGINES}
        self.cnt = dict(cnt) if cnt else {}

    def val(self, sem):
        return self.cnt.get(sem, 0)

    def emit(self, eng, fn, waits=(), inc=None, inc_n=1):
        w = [(s, v) for (s, v) in waits if v > 0]
        self.ops[eng].append((fn, w, inc, inc_n))
        if inc is not None:
            self.cnt[inc] = self.cnt.get(inc, 0) + inc_n

    def replay(self, eng_name, handle, sems):
        for fn, waits, inc, inc_n in self.ops[eng_name]:
            for s, v in waits:
                handle.wait_ge(sems[s], v)
            ins = fn(handle)
            if inc is not None:
                if ins is None:
                    raise RuntimeError(f"op with inc {inc} returned no instruction")
                ins.then_inc(sems[inc], inc_n)

    def run_block(self, nc, sems):
        with nc.Block() as blk:
            blk.sync(lambda e: self.replay("sync", e, sems))
            blk.tensor(lambda e: self.replay("tensor", e, sems))
            blk.vector(lambda e: self.replay("vector", e, sems))
            blk.scalar(lambda e: self.replay("scalar", e, sems))
            blk.gpsimd(lambda e: self.replay("gpsimd", e, sems))


def build_program(debug=0, phases=3):
    nc = bass.Bass("TRN2", target_bir_lowering=False)

    x_kv = nc.dram_tensor("x_kv", [T, D], F32, kind="ExternalInput").ap()
    xq_h = nc.dram_tensor("xq", [NSLOT * 128, D], F32, kind="ExternalInput").ap()
    xqp_h = nc.dram_tensor("xq_prev", [NSLOT * 128, D], F32, kind="ExternalInput").ap()
    masks_h = nc.dram_tensor("masks", [128, NSLOT, 2, 128], BF16, kind="ExternalInput").ap()
    ident_h = nc.dram_tensor("ident_hbm", [128, 128], F32, kind="ExternalInput").ap()
    w1q_h = nc.dram_tensor("W1_q", [2 * D, 64], F32, kind="ExternalInput").ap()
    w2q_h = nc.dram_tensor("W2_q", [2 * D, 64], F32, kind="ExternalInput").ap()
    w1k_h = nc.dram_tensor("W1_k", [D, 64], F32, kind="ExternalInput").ap()
    w2k_h = nc.dram_tensor("W2_k", [D, 64], F32, kind="ExternalInput").ap()
    wv_h = nc.dram_tensor("W_v", [D, D], F32, kind="ExternalInput").ap()
    bvb_h = nc.dram_tensor("b_v_b", [128, D], F32, kind="ExternalInput").ap()
    wo_h = nc.dram_tensor("W_o", [D, D], F32, kind="ExternalInput").ap()
    bob_h = nc.dram_tensor("b_o_b", [128, D], F32, kind="ExternalInput").ap()
    wr_h = nc.dram_tensor("w_router", [D, NE], F32, kind="ExternalInput").ap()
    wg_h = nc.dram_tensor("w_gate", [NE, D, HFF], F32, kind="ExternalInput").ap()
    wu_h = nc.dram_tensor("w_up", [NE, D, HFF], F32, kind="ExternalInput").ap()
    wd_h = nc.dram_tensor("w_down", [NE, HFF, D], F32, kind="ExternalInput").ap()
    out_h = nc.dram_tensor("out_q", [NSLOT * 128, D], F32, kind="ExternalOutput").ap()

    dbg = {}
    if debug >= 1:
        dbg["vs"] = nc.dram_tensor("dbg_vs", [128, NKT, H, 65], F32, kind="ExternalOutput").ap()
        dbg["jlkt"] = nc.dram_tensor("dbg_jlkt", [128, 4, NKT * 128], F32, kind="ExternalOutput").ap()
        dbg["lqt"] = nc.dram_tensor("dbg_lqt", [128, 4, NSLOT * 128], F32, kind="ExternalOutput").ap()
    if debug >= 3:
        dbg["kk"] = nc.dram_tensor("dbg_kk", [128, 128], F32, kind="ExternalOutput").ap()
        dbg["lraw"] = nc.dram_tensor("dbg_lraw", [128, H, C6], F32, kind="ExternalOutput").ap()
        dbg["ssb"] = nc.dram_tensor("dbg_ssb", [128, H], F32, kind="ExternalOutput").ap()
        dbg["rnrm"] = nc.dram_tensor("dbg_rnrm", [128, H], F32, kind="ExternalOutput").ap()
        dbg["lsc"] = nc.dram_tensor("dbg_lsc", [128, H, C6], F32, kind="ExternalOutput").ap()
    if debug >= 2:
        dbg["acc"] = nc.dram_tensor("dbg_acc", [128, NSLOT, D], F32, kind="ExternalOutput").ap()
        dbg["comb"] = nc.dram_tensor("dbg_comb", [128, NSLOT, NE], F32, kind="ExternalOutput").ap()
        dbg["attnT"] = nc.dram_tensor("dbg_attnT", [128, 8, 512], F32, kind="ExternalOutput").ap()

    sem_names = [
        "wA", "z0", "x0", "x1", "cast", "tp", "xt", "mmv", "ss", "rr", "lk", "tk", "jl",
        "bar1", "wB", "mk0", "mk1", "sc", "exp", "msk", "av", "sr", "rbm", "rbf", "rbr",
        "ate", "wo", "xqd", "y1a", "v1", "sd1", "xf", "tpf", "xft", "rt",
        "rmax", "rexp", "cmb", "bar2", "wg", "wu", "wd", "gmm", "umm", "sg",
        "hm", "dmm", "acc", "ln2", "sd2", "eo", "out", "dbgs",
    ]

    with ExitStack() as _st0:
        sb = lambda *a: _st0.enter_context(nc.sbuf_tensor(*a))
        ident = sb("ident", [128, 128], BF16)
        acc_t = sb("acc", [128, NSLOT, D], BF16)
        comb_t = sb("comb", [128, NSLOT, NE], F32)
        rtr_f = sb("rtr_f", [128, 8, NE], F32)
        rtr_hi = sb("rtr_hi", [128, 8, NE], BF16)
        rtr_lo = sb("rtr_lo", [128, 8, NE], BF16)
        epsb = sb("epsb", [128, 1], F32)
        sems_ctx = [nc.semaphore(n) for n in sem_names]
        sems = dict(zip(sem_names, (c.__enter__() for c in sems_ctx)))

        with ExitStack() as _st1:
            sb1 = lambda *a: _st1.enter_context(nc.sbuf_tensor(*a))
            Vs = sb1("Vs", [128, NKT, H, 65], BF16)
            JLKT = sb1("JLKT", [128, 4, NKT * 128], BF16)
            LQT = sb1("LQT", [128, 4, NSLOT * 128], BF16)
            with ExitStack() as _stA:
                sbA = lambda *a: _stA.enter_context(nc.sbuf_tensor(*a))
                psA = lambda *a: _stA.enter_context(nc.psum_tensor(*a))
                wvks = sbA("wvks", [128, 8, 1152], BF16)
                wq_sb = sbA("wq_sb", [128, 16, 128], BF16)
                bvb = sbA("bvb", [128, D], F32)
                xa0 = sbA("xa0", [128, D], F32)
                xa1 = sbA("xa1", [128, D], F32)
                xab0 = sbA("xab0", [128, D], BF16)
                xab1 = sbA("xab1", [128, D], BF16)
                xT0 = sbA("xT0", [128, 8, 128], BF16)
                xT1 = sbA("xT1", [128, 8, 128], BF16)
                xTP0 = sbA("xTP0", [128, 8, 128], BF16)
                xTP1 = sbA("xTP1", [128, 8, 128], BF16)
                Lraw0 = sbA("Lraw0", [128, H, C6], F32)
                Lraw1 = sbA("Lraw1", [128, H, C6], F32)
                Lsq = sbA("Lsq", [128, H, C6], F32)
                prodAll = sbA("prodAll", [128, 12, H], F32)
                sst = sbA("sst", [128, 3, H], F32)
                ssb = sbA("ssb", [128, H], F32)
                rnrm0 = sbA("rnrm0", [128, H], F32)
                rnrm1 = sbA("rnrm1", [128, H], F32)
                nrnrm = sbA("nrnrm", [128, H], F32)
                rln = sbA("rln", [128, H], F32)
                Lsc = sbA("Lsc", [128, H, C6], BF16)
                kk_sb = sbA("kk_sb", [128, 128], F32)
                zb = sbA("zb", [128, 128], BF16)
                pT = psA("pT", [128, 8, 128], BF16)
                pTk = psA("pTk", [128, 4, 128], BF16)
                pV0 = psA("pV0", [128, 1152], F32)
                pV1 = psA("pV1", [128, 1152], F32)
                P = Prog()
                xa, xab, xT, xTP = [xa0, xa1], [xab0, xab1], [xT0, xT1], [xTP0, xTP1]
                pV = [pV0, pV1]

                P.emit("gpsimd", lambda g: g.dma_start(out=ident[:], in_=ident_h), inc="wA", inc_n=16)
                P.emit("gpsimd", lambda g: g.dma_start(
                    out=wvks[:, :, 0:1024], in_=wv_h.rearrange("(dc p) n -> p dc n", p=128)), inc="wA", inc_n=16)
                P.emit("gpsimd", lambda g: g.dma_start(
                    out=wvks[:, :, 1024:1088], in_=w1k_h.rearrange("(dc p) n -> p dc n", p=128)), inc="wA", inc_n=16)
                P.emit("gpsimd", lambda g: g.dma_start(
                    out=wvks[:, :, 1088:1152], in_=w2k_h.rearrange("(dc p) n -> p dc n", p=128)), inc="wA", inc_n=16)
                P.emit("gpsimd", lambda g: g.dma_start(
                    out=wq_sb[:, :, 0:64], in_=w1q_h.rearrange("(kc p) n -> p kc n", p=128)), inc="wA", inc_n=16)
                P.emit("gpsimd", lambda g: g.dma_start(
                    out=wq_sb[:, :, 64:128], in_=w2q_h.rearrange("(kc p) n -> p kc n", p=128)), inc="wA", inc_n=16)
                P.emit("gpsimd", lambda g: g.dma_start(out=bvb[:], in_=bvb_h), inc="wA", inc_n=16)
                P.emit("gpsimd", lambda g: g.dma_start(
                    out=rtr_f[:], in_=wr_h.rearrange("(dc p) n -> p dc n", p=128)), inc="wA", inc_n=16)
                W_SETUP = P.val("wA")

                P.emit("vector", lambda v: v.memset(epsb[:], 1e-5))
                P.emit("vector", lambda v: v.drain())
                P.emit("vector", lambda v: v.memset(zb[:], 0.0))
                P.emit("vector", lambda v: v.drain(), inc="z0")
                def ztps(te):
                    ins = None
                    for m in range(4):
                        ins = te.transpose(pTk[:, m, :], zb[:], ident[:])
                    return ins
                P.emit("tensor", ztps, waits=[("z0", 1), ("wA", W_SETUP)], inc="tk")
                P.emit("vector", lambda v: v.tensor_copy(rtr_hi[:], rtr_f[:]), waits=[("wA", W_SETUP)])
                P.emit("vector", lambda v: v.drain())
                P.emit("vector", lambda v: v.tensor_tensor(rtr_lo[:], rtr_f[:], rtr_hi[:], ALU.subtract))
                P.emit("vector", lambda v: v.drain())


                xt_count = {}

                def tile_prep(idx, src_ap, row0, xT_list, xT_slot, xT_free):
                    b = idx % 2
                    xs = "x%d" % b
                    P.emit("sync", lambda sy, s=src_ap, r=row0, bb=b: sy.dma_start(
                        out=xa[bb][:], in_=s[r:r + 128, :]),
                        waits=[("cast", idx - 1)], inc=xs, inc_n=16)
                    P.emit("vector", lambda v, bb=b: v.tensor_copy(xab[bb][:], xa[bb][:]),
                           waits=[(xs, 16 * (idx // 2 + 1)), ("tp", idx - 1)])
                    P.emit("vector", lambda v: v.drain(), inc="cast")
                    def tps(te, bb=b):
                        ins = None
                        for dc in range(8):
                            ins = te.transpose(pT[:, dc, :], xab[bb][:, dc * 128:(dc + 1) * 128], ident[:])
                        return ins
                    P.emit("tensor", tps, waits=[("cast", idx + 1), ("xt", idx)], inc="tp")
                    P.emit("vector", lambda v, sl=xT_slot: v.tensor_copy(xT_list[sl][:], pT[:]),
                           waits=[("tp", idx + 1)] + list(xT_free))
                    P.emit("vector", lambda v: v.drain(), inc="xt")
                    xt_count[idx] = P.val("xt")

                Lraw = [Lraw0, Lraw1]
                rnrm = [rnrm0, rnrm1]
                ext_state = {}
                ext_serial = [0]

                def ext_front(pv_t, off0, jside, dest, col0):
                    se = ext_serial[0]
                    ext_serial[0] += 1
                    lb = se % 2
                    P.emit("vector", lambda v: v.tensor_copy(kk_sb[:], pv_t[:, off0:off0 + 128]),
                           waits=[("mmv", P.val("mmv"))])
                    P.emit("vector", lambda v: v.drain())
                    o1, o2 = 0, 64
                    for c, (i, j) in enumerate(PAIRS):
                        P.emit("vector", lambda v, i=i, j=j, c=c: v.tensor_mul(
                            prodAll[:, 2 * c, :], kk_sb[:, o1 + i: o1 + 64: 4], kk_sb[:, o2 + j: o2 + 64: 4]))
                        P.emit("vector", lambda v, i=i, j=j, c=c: v.tensor_mul(
                            prodAll[:, 2 * c + 1, :], kk_sb[:, o1 + j: o1 + 64: 4], kk_sb[:, o2 + i: o2 + 64: 4]))
                    P.emit("vector", lambda v: v.drain())
                    for c in range(C6):
                        P.emit("vector", lambda v, c=c, lb=lb: v.tensor_tensor(
                            Lraw[lb][:, :, c], prodAll[:, 2 * c, :], prodAll[:, 2 * c + 1, :], ALU.subtract))
                    P.emit("vector", lambda v: v.drain())
                    P.emit("vector", lambda v, lb=lb: v.tensor_tensor(Lsq[:], Lraw[lb][:], Lraw[lb][:], ALU.mult),
                           waits=[("rr", P.val("rr"))])
                    P.emit("vector", lambda v: v.drain())
                    for k in range(3):
                        P.emit("vector", lambda v, k=k: v.tensor_add(
                            sst[:, k, :], Lsq[:, :, 2 * k], Lsq[:, :, 2 * k + 1]))
                    P.emit("vector", lambda v: v.drain())
                    P.emit("vector", lambda v: v.tensor_add(sst[:, 0, :], sst[:, 0, :], sst[:, 1, :]))
                    P.emit("vector", lambda v: v.drain())
                    P.emit("vector", lambda v: v.tensor_add(sst[:, 1, :], sst[:, 0, :], sst[:, 2, :]))
                    P.emit("vector", lambda v: v.drain())
                    P.emit("vector", lambda v: v.tensor_scalar_max(ssb[:], sst[:, 1, :], 1e-24))
                    P.emit("vector", lambda v: v.drain(), inc="ss")
                    P.emit("scalar", lambda s: s.activation(rln[:], ssb[:], AF.Ln),
                           waits=[("ss", P.val("ss")), ("lk", P.val("lk"))])
                    P.emit("scalar", lambda s: s.drain())
                    P.emit("scalar", lambda s, lb=lb: s.activation(rnrm[lb][:], rln[:], AF.Exp, bias=0.0, scale=-0.5))
                    P.emit("scalar", lambda s: s.drain(), inc="rr")
                    ext_state[se] = (jside, dest, col0, lb)

                def ext_back(se):
                    jside, dest, col0, lb = ext_state.pop(se)
                    P.emit("vector", lambda v, lb=lb: v.tensor_scalar_mul(nrnrm[:], rnrm[lb][:], -1.0),
                           waits=[("rr", se + 1), ("tk", P.val("tk"))])
                    P.emit("vector", lambda v: v.drain())
                    for c in range(C6):
                        if jside:
                            oc = 5 - c
                            m = rnrm[lb] if JSIGN[oc] > 0 else nrnrm
                            P.emit("vector", lambda v, c=c, oc=oc, m=m, lb=lb: v.tensor_tensor(
                                Lsc[:, :, oc], Lraw[lb][:, :, c], m[:], ALU.mult))
                        else:
                            P.emit("vector", lambda v, c=c, lb=lb: v.tensor_tensor(
                                Lsc[:, :, c], Lraw[lb][:, :, c], rnrm[lb][:], ALU.mult))
                    P.emit("vector", lambda v: v.drain(), inc="lk")
                    head_transposes(dest, col0)

                def head_transposes(dest, col0):
                    def tps(te):
                        ins = None
                        for h in range(H):
                            g_, m = h % 4, h // 4
                            ins = te.transpose(
                                pTk[32 * g_:32 * g_ + 6, m, :], Lsc[:, h, :], ident[:],
                                tile_position=(0, 32 * g_))
                        return ins
                    P.emit("tensor", tps, waits=[("lk", P.val("lk")), ("jl", P.val("jl"))], inc="tk")
                    P.emit("vector", lambda v: v.tensor_copy(dest[:, :, col0:col0 + 128], pTk[:]),
                           waits=[("tk", P.val("tk"))])
                    P.emit("vector", lambda v: v.drain(), inc="jl")

                def emit_vmm(kt):
                    b = kt % 2
                    def vmm(te, b=b):
                        ins = None
                        for dc in range(8):
                            for (n0, n1) in ((0, 512), (512, 1024), (1024, 1152)):
                                ins = te.matmul(pV[b][:, n0:n1], xT[b][:, dc, :], wvks[:, dc, n0:n1],
                                                start=(dc == 0), stop=(dc == 7))
                        return ins
                    P.emit("tensor", vmm,
                           waits=[("xt", xt_count[kt]), ("ss", kt - 1), ("wA", W_SETUP)], inc="mmv")
                    P.emit("vector", lambda v, b=b, kt=kt: v.tensor_tensor(
                        Vs[:, kt, :, 0:64], pV[b][:, 0:1024], bvb[:], ALU.add),
                        waits=[("mmv", P.val("mmv"))])
                    P.emit("vector", lambda v: v.drain())

                tile_prep(0, x_kv, 0, xT, 0, [("mmv", -1)])
                for kt in range(NKT):
                    if kt + 1 < NKT:
                        tile_prep(kt + 1, x_kv, (kt + 1) * 128, xT, (kt + 1) % 2,
                                  [("mmv", kt)])
                    emit_vmm(kt)
                    ext_front(pV[kt % 2], 1024, True, JLKT, kt * 128)
                    if kt >= 1:
                        ext_back(kt - 1)
                P.emit("vector", lambda v: v.memset(Vs[:, :, :, 64:65], 1.0))
                P.emit("vector", lambda v: v.drain(), inc="jl")

                def emit_qmm(s_):
                    sb_ = s_ % 2
                    pvb = pV[s_ % 2]
                    def qmm(te, sb_=sb_, pvb=pvb):
                        ins = None
                        for kc in range(16):
                            lhs = xT[sb_][:, kc, :] if kc < 8 else xTP[sb_][:, kc - 8, :]
                            ins = te.matmul(pvb[:, 0:128], lhs, wq_sb[:, kc, :],
                                            start=(kc == 0), stop=(kc == 15))
                        return ins
                    P.emit("tensor", qmm,
                           waits=[("xt", xt_count[NKT + 2 * s_ + 1]), ("ss", NKT + s_ - 1), ("wA", W_SETUP)], inc="mmv")

                tile_prep(NKT, xq_h, 0, xT, 0, [("mmv", NKT - 1)])
                tile_prep(NKT + 1, xqp_h, 0, xTP, 0, [("mmv", NKT - 1)])
                for s_ in range(NSLOT):
                    if s_ + 1 < NSLOT:
                        i0 = NKT + 2 * (s_ + 1)
                        tile_prep(i0, xq_h, (s_ + 1) * 128, xT, (s_ + 1) % 2, [("mmv", NKT + s_)])
                        tile_prep(i0 + 1, xqp_h, (s_ + 1) * 128, xTP, (s_ + 1) % 2, [("mmv", NKT + s_)])
                    emit_qmm(s_)
                    ext_front(pV[s_ % 2], 0, False, LQT, s_ * 128)
                    ext_back(NKT + s_ - 1)
                ext_back(NKT + NSLOT - 1)

                if debug >= 3:
                    JL_D3 = P.val("jl")
                    P.emit("gpsimd", lambda g: g.dma_start(out=dbg["kk"], in_=kk_sb[:]),
                           waits=[("jl", JL_D3)], inc="dbgs", inc_n=16)
                    P.emit("gpsimd", lambda g: g.dma_start(out=dbg["lraw"], in_=Lraw[:]), inc="dbgs", inc_n=16)
                    P.emit("gpsimd", lambda g: g.dma_start(out=dbg["ssb"], in_=ssb[:]), inc="dbgs", inc_n=16)
                    P.emit("gpsimd", lambda g: g.dma_start(out=dbg["rnrm"], in_=rnrm[:]), inc="dbgs", inc_n=16)
                    P.emit("gpsimd", lambda g: g.dma_start(out=dbg["lsc"], in_=Lsc[:]), inc="dbgs", inc_n=16)
                if debug >= 1:
                    JL_D = P.val("jl")
                    P.emit("gpsimd", lambda g: g.dma_start(out=dbg["vs"], in_=Vs[:]),
                           waits=[("jl", JL_D)], inc="dbgs", inc_n=16)
                    P.emit("gpsimd", lambda g: g.dma_start(out=dbg["jlkt"], in_=JLKT[:]), inc="dbgs", inc_n=16)
                    P.emit("gpsimd", lambda g: g.dma_start(out=dbg["lqt"], in_=LQT[:]), inc="dbgs", inc_n=16)

                JL_ALL = P.val("jl")
                P.emit("sync", lambda sy: sy.sem_inc(sems["bar1"], 1), waits=[("x0", P.val("x0")), ("x1", P.val("x1"))])
                P.emit("tensor", lambda te: te.sem_inc(sems["bar1"], 1))
                P.emit("vector", lambda v: v.sem_inc(sems["bar1"], 1), waits=[("jl", JL_ALL)])
                P.emit("scalar", lambda s: s.sem_inc(sems["bar1"], 1), waits=[("rr", P.val("rr"))])
                P.emit("gpsimd", lambda g: g.sem_inc(sems["bar1"], 1),
                       waits=[("wA", W_SETUP), ("dbgs", P.val("dbgs"))])
                P.cnt["bar1"] = 5
                P.run_block(nc, sems)
                cnt1 = P.cnt

            if phases < 2:
                for c in sems_ctx:
                    c.__exit__(None, None, None)
                return nc
            with ExitStack() as _stB:
                sbB = lambda *a: _stB.enter_context(nc.sbuf_tensor(*a))
                psB = lambda *a: _stB.enter_context(nc.psum_tensor(*a))
                masks_g0 = sbB("masks_g0", [128, 4, 2, 128], BF16)
                attnT = sbB("attnT", [128, 8, 512], BF16)
                PT0 = sbB("PT0", [128, 512], BF16)
                PT1 = sbB("PT1", [128, 512], BF16)
                PT2 = sbB("PT2", [128, 512], BF16)
                PT3 = sbB("PT3", [128, 512], BF16)
                
                wo_sb = sbB("wo_sb", [128, 8, 1024], BF16)
                bob = sbB("bob", [128, D], F32)
                srow_b = sbB("srow_b", [1, 512], BF16)
                ones1b = sbB("ones1b", [1, 64], BF16)
                
                rbl_t = sbB("rbl", [64, 512], F32)
                rbr0 = sbB("rbr0", [64, 512], F32)
                rbr1 = sbB("rbr1", [64, 512], F32)
                xqr_t = sbB("xqr", [128, D], F32)
                y1a_t = sbB("y1a", [128, D], F32)
                y1b_t = sbB("y1b", [128, D], F32)
                xlo_t = sbB("xlo", [128, D], BF16)
                xfT_t = sbB("xfT_t", [128, 8, 128], BF16)
                xfTlo_t = sbB("xfTlo_t", [128, 8, 128], BF16)
                stq = sbB("stq", [128, 4], F32)
                rstd1 = sbB("rstd1", [128, 1], F32)
                rln1 = sbB("rln1", [128, 1], F32)
                e8_t = sbB("e8", [128, NE], F32)
                nmx_t = sbB("nmx", [128, 1], F32)
                s8_t = sbB("s8", [128, 1], F32)
                scr8_t = sbB("scr8", [128, NE], F32)
                st8_t = sbB("st8", [128, 8], F32)
                s2t_t = sbB("s2t", [128, NE], F32)
                mask8_t = sbB("mask8", [128, NE], F32)
                
                pS0 = psB("pS0", [128, 512], F32)
                pS1 = psB("pS1", [128, 512], F32)
                pS2 = psB("pS2", [128, 512], F32)
                pAV0 = psB("pAV0", [65, 512], F32)
                pAV1 = psB("pAV1", [65, 512], F32)
                
                pWO0 = psB("pWO0", [128, 1024], F32)
                
                pT2 = psB("pT2", [128, 8, 128], BF16)
                P = Prog(cnt1)
                PT = [PT0, PT1, PT2, PT3]
                pS, pAV = [pS0, pS1, pS2], [pAV0, pAV1]
                pWO = pWO0
                rbr = [rbr0, rbr1]
                masks_g = masks_g0

                P.emit("gpsimd", lambda g: g.dma_start(
                    out=wo_sb[:], in_=wo_h.rearrange("(dc p) n -> p dc n", p=128)), inc="wB", inc_n=16)
                P.emit("gpsimd", lambda g: g.dma_start(out=bob[:], in_=bob_h), inc="wB", inc_n=16)
                W_B = P.val("wB")
                P.emit("vector", lambda v: v.memset(ones1b[:], 1.0), waits=[("bar1", 5)])
                P.emit("vector", lambda v: v.drain(), inc="xt")
                ONES_RDY = P.val("xt")

                sbank_free = [None, None, None]
                ptslot_free = [None, None, None, None]
                avbank_free = [None, None]
                exp_serial = [0]
                wo_prev_g = 0

                def emit_score(g, h, kc):
                    gr, m = h % 4, h // 4
                    jlo = max(4 * g, kc // 2)
                    off = (jlo - 4 * g) * 128
                    sb = exp_serial[0] % 3
                    pslot = exp_serial[0] % 4
                    exp_serial[0] += 1
                    waits = []
                    if sbank_free[sb]:
                        waits.append(sbank_free[sb])
                    if h <= 1 and g == 0 and kc < 2:
                        waits.append(("jl", JL_ALL))
                    def scmm(te, sb=sb, off=off, gr=gr, m=m, kc=kc, g=g):
                        return te.matmul(
                            pS[sb][:, off:512],
                            JLKT[32 * gr:32 * gr + 6, m, kc * 128:(kc + 1) * 128],
                            LQT[32 * gr:32 * gr + 6, m, g * 512 + off:(g + 1) * 512],
                            start=True, stop=True, tile_position=(32 * gr, 0))
                    P.emit("tensor", scmm, waits=waits, inc="sc")
                    ncols = 512 - off
                    ew = [("sc", P.val("sc"))]
                    if ptslot_free[pslot]:
                        ew.append(ptslot_free[pslot])
                    P.emit("scalar", lambda s, sb=sb, off=off, pslot=pslot, ncols=ncols: s.activation(
                        PT[pslot][:, 0:ncols], pS[sb][:, off:512], AF.Exp, bias=0.0, scale=SCALE),
                        waits=ew, inc="exp")
                    sbank_free[sb] = ("exp", P.val("exp"))
                    masked = kc >= 8 * g
                    if masked:
                        jj, ii = kc // 2 - 4 * g, kc % 2
                        P.emit("vector", lambda v, pslot=pslot, jj=jj, ii=ii, g=g: v.tensor_mul(
                            PT[pslot][:, 0:128], PT[pslot][:, 0:128], masks_g[:, jj, ii, :]),
                            waits=[("exp", P.val("exp")), MSK_RDY[0]])
                        P.emit("vector", lambda v: v.drain(), inc="msk")
                    return (pslot, off, ncols, masked, P.val("exp"), P.val("msk"))

                def emit_av(g, h, kc, sinfo):
                    pslot, off, ncols, masked, expv, mskv = sinfo
                    hb = h % 2
                    K = kmax(g)
                    avw = [("msk", mskv)] if masked else [("exp", expv)]
                    if kc == 0 and avbank_free[hb]:
                        avw.append(avbank_free[hb])
                    def avmm(te, hb=hb, off=off, kc=kc, h=h, ncols=ncols, K=K, pslot=pslot):
                        return te.matmul(
                            pAV[hb][:, off:512], Vs[:, kc, h, :], PT[pslot][:, 0:ncols],
                            start=(kc == 0), stop=(kc == K - 1))
                    P.emit("tensor", avmm, waits=avw, inc="av")
                    ptslot_free[pslot] = ("av", P.val("av"))

                for g in range(NG):
                    av_counts, sr_vals, rbm_vals, rbr_vals = {}, {}, {}, {}
                    last_rbr = P.val("rbr")
                    P.emit("gpsimd", lambda gq, g=g: gq.dma_start(
                        out=masks_g[:], in_=masks_h[:, 4 * g:4 * g + 4, :, :]),
                        waits=[("bar1", 5), ("msk", P.cnt.get("_mskuse_%d" % (g - 1), 0))],
                        inc="mk0", inc_n=16)
                    MSK_RDY = [("mk0", P.val("mk0"))]
                    K = kmax(g)

                    def emit_chain_pre(hh):
                        hbb = hh % 2
                        srw = [("av", av_counts[hh])]
                        if rbm_vals:
                            srw.append(("rbm", max(rbm_vals.values())))
                        P.emit("vector", lambda v, hbb=hbb: v.tensor_copy(
                            srow_b[0:1, :], pAV[hbb][64:65, :]), waits=srw)
                        P.emit("vector", lambda v: v.drain(), inc="sr")
                        sr_vals[hh] = P.val("sr")
                        rw = [("sr", sr_vals[hh]), ("xt", ONES_RDY)]
                        prev_rbr = rbr_vals[hh - 1] if (hh - 1) in rbr_vals else last_rbr
                        if prev_rbr:
                            rw.append(("rbr", prev_rbr))
                        rw.append(("y1a", P.val("y1a")))
                        def rbmm(te):
                            return te.matmul(pWO[0:64, 512:1024], ones1b[:], srow_b[0:1, :], start=True, stop=True)
                        P.emit("tensor", rbmm, waits=rw, inc="rbm")
                        rbm_vals[hh] = P.val("rbm")

                    def emit_chain_act(hh):
                        hbb = hh % 2
                        P.emit("scalar", lambda s: s.activation(rbl_t[:], pWO[0:64, 512:1024], AF.Ln),
                               waits=[("rbm", rbm_vals[hh])])
                        P.emit("scalar", lambda s: s.drain())
                        P.emit("scalar", lambda s, hbb=hbb: s.activation(
                            rbr[hbb][:], rbl_t[:], AF.Exp, bias=0.0, scale=-1.0))
                        P.emit("scalar", lambda s: s.drain(), inc="rbr")
                        rbr_vals[hh] = P.val("rbr")

                    def emit_chain_post(hh):
                        hbb = hh % 2
                        cp, half = hh // 2, (hh % 2) * 64
                        P.emit("vector", lambda v, hbb=hbb, cp=cp, half=half: v.tensor_tensor(
                            attnT[half:half + 64, cp, :], pAV[hbb][0:64, :], rbr[hbb][:], ALU.mult),
                            waits=[("rbr", rbr_vals[hh]), ("wo", wo_prev_g)])
                        P.emit("vector", lambda v: v.drain(), inc="ate")
                        avbank_free[hbb] = ("ate", P.val("ate"))

                    def emit_chain(hh):
                        emit_chain_pre(hh)
                        emit_chain_act(hh)
                        emit_chain_post(hh)

                    for p in range(8):
                        a, b = 2 * p, 2 * p + 1
                        prev_a = prev_b = None
                        for kc in range(K):
                            sinfo_a = emit_score(g, a, kc)
                            sinfo_b = emit_score(g, b, kc)
                            if p >= 1:
                                if kc == 0:
                                    emit_chain_pre(2 * p - 2)
                                    emit_chain_act(2 * p - 2)
                                    emit_chain_pre(2 * p - 1)
                                    emit_chain_act(2 * p - 1)
                                elif kc == 1:
                                    emit_chain_post(2 * p - 2)
                                    emit_chain_post(2 * p - 1)
                            if kc >= 1:
                                emit_av(g, a, kc - 1, prev_a)
                                emit_av(g, b, kc - 1, prev_b)
                            prev_a, prev_b = sinfo_a, sinfo_b
                        emit_av(g, a, K - 1, prev_a)
                        emit_av(g, b, K - 1, prev_b)
                        av_counts[a] = P.val("av") - 1
                        av_counts[b] = P.val("av")
                    emit_chain(14)
                    emit_chain(14)
                    emit_chain(15)

                    ATE_G = P.val("ate")
                    for qq in range(4):
                        qt = 4 * g + qq
                        qb = qt % 2
                        P.emit("sync", lambda sy, qt=qt: sy.dma_start(
                            out=xqr_t[:], in_=xq_h[qt * 128:(qt + 1) * 128, :]),
                            waits=[("bar1", 5), ("y1a", P.val("y1a"))], inc="xqd", inc_n=16)
                        def womm(te, qq=qq):
                            ins = None
                            for cp in range(8):
                                for nb in range(2):
                                    ins = te.matmul(
                                        pWO[:, nb * 512:(nb + 1) * 512],
                                        attnT[:, cp, qq * 128:(qq + 1) * 128],
                                        wo_sb[:, cp, nb * 512:(nb + 1) * 512],
                                        start=(cp == 0), stop=(cp == 7))
                            return ins
                        wow = [("ate", ATE_G), ("wB", W_B), ("y1a", P.val("y1a"))]
                        P.emit("tensor", womm, waits=wow, inc="wo")
                        WO_T = P.val("wo")
                        P.emit("vector", lambda v: v.tensor_tensor(y1a_t[:], pWO[:], bob[:], ALU.add),
                               waits=[("wo", WO_T), ("xqd", P.val("xqd"))])
                        P.emit("vector", lambda v: v.drain())
                        P.emit("vector", lambda v: v.scalar_tensor_tensor(
                            y1b_t[:], y1a_t[:], 1.0, xqr_t[:], ALU.mult, ALU.add, accum_out=stq[:, 0:1]))
                        P.emit("vector", lambda v: v.drain(), inc="y1a")
                        Y1A = P.val("y1a")
                        P.emit("vector", lambda v: v.tensor_scalar(stq[:, 0:1], stq[:, 0:1], 1.0 / D, None, ALU.mult))
                        P.emit("vector", lambda v: v.drain())
                        P.emit("vector", lambda v: v.tensor_scalar(y1a_t[:], y1b_t[:], stq[:, 0:1], None, ALU.subtract))
                        P.emit("vector", lambda v: v.drain())
                        P.emit("vector", lambda v: v.scalar_tensor_tensor(
                            y1b_t[:], y1a_t[:], 1.0, y1a_t[:], ALU.mult, ALU.mult, accum_out=stq[:, 1:2]))
                        P.emit("vector", lambda v: v.drain(), inc="v1")
                        P.emit("scalar", lambda s: s.activation(rln1[:], stq[:, 1:2], AF.Ln,
                                                                bias=epsb[:, 0:1], scale=1.0 / D),
                               waits=[("v1", P.val("v1"))])
                        P.emit("scalar", lambda s: s.drain())
                        P.emit("scalar", lambda s: s.activation(rstd1[:], rln1[:], AF.Exp, bias=0.0, scale=-0.5))
                        P.emit("scalar", lambda s: s.drain(), inc="sd1")
                        P.emit("vector", lambda v: v.tensor_scalar(y1b_t[:], y1a_t[:], rstd1[:, 0:1], None, ALU.mult),
                               waits=[("sd1", P.val("sd1"))])
                        P.emit("vector", lambda v: v.drain())
                        P.emit("vector", lambda v, qt=qt: v.tensor_copy(acc_t[:, qt, :], y1b_t[:]))
                        P.emit("vector", lambda v: v.drain())
                        P.emit("vector", lambda v, qt=qt: v.tensor_tensor(xlo_t[:], y1b_t[:], acc_t[:, qt, :], ALU.subtract))
                        P.emit("vector", lambda v: v.drain(), inc="xf")
                        XF = P.val("xf")
                        def tph(te, qt=qt):
                            ins = None
                            for dc in range(8):
                                ins = te.transpose(pT2[:, dc, :], acc_t[:, qt, dc * 128:(dc + 1) * 128], ident[:])
                            return ins
                        P.emit("tensor", tph, waits=[("xf", XF), ("xft", P.val("xft"))], inc="tpf")
                        P.emit("vector", lambda v: v.tensor_copy(xfT_t[:], pT2[:]),
                               waits=[("tpf", P.val("tpf"))])
                        P.emit("vector", lambda v: v.drain(), inc="xft")
                        def tpl(te):
                            ins = None
                            for dc in range(8):
                                ins = te.transpose(pT2[:, dc, :], xlo_t[:, dc * 128:(dc + 1) * 128], ident[:])
                            return ins
                        P.emit("tensor", tpl, waits=[("xft", P.val("xft"))], inc="tpf")
                        P.emit("vector", lambda v: v.tensor_copy(xfTlo_t[:], pT2[:]),
                               waits=[("tpf", P.val("tpf"))])
                        P.emit("vector", lambda v: v.drain(), inc="xft")
                        XFT = P.val("xft")
                        rtw = [("xft", XFT)]
                        if sbank_free[qb]:
                            rtw.append(sbank_free[qb])
                        def rtmm(te, qb=qb):
                            ins = None
                            srcs = ((xfT_t, rtr_hi), (xfT_t, rtr_lo), (xfTlo_t, rtr_hi))
                            n = 0
                            for src, w in srcs:
                                for dc in range(8):
                                    ins = te.matmul(pS[qb][:, 0:8], src[:, dc, :], w[:, dc, :],
                                                    start=(n == 0), stop=(n == 23))
                                    n += 1
                            return ins
                        P.emit("tensor", rtmm, waits=rtw, inc="rt")
                        P.emit("vector", lambda v, qb=qb: v.reduce_max(
                            nmx_t[:], pS[qb][:, 0:8], axis=AX.X, negate=True),
                            waits=[("rt", P.val("rt"))])
                        P.emit("vector", lambda v: v.drain(), inc="rmax")
                        P.emit("scalar", lambda s, qb=qb: s.activation(
                            e8_t[:], pS[qb][:, 0:8], AF.Exp, bias=nmx_t[:, 0:1], scale=1.0),
                            waits=[("rmax", P.val("rmax"))])
                        P.emit("scalar", lambda s: s.drain(), inc="rexp")
                        sbank_free[qb] = ("rexp", P.val("rexp"))
                        P.emit("vector", lambda v: v.reduce_sum(s8_t[:], e8_t[:], axis=AX.X),
                               waits=[("rexp", P.val("rexp"))])
                        P.emit("vector", lambda v: v.drain())
                        P.emit("vector", lambda v: v.reciprocal(nmx_t[:], s8_t[:]))
                        P.emit("vector", lambda v: v.drain())
                        P.emit("vector", lambda v: v.tensor_scalar(scr8_t[:], e8_t[:], nmx_t[:, 0:1], None, ALU.mult))
                        P.emit("vector", lambda v: v.drain())
                        P.emit("vector", lambda v: v.max(st8_t[:], scr8_t[:]))
                        P.emit("vector", lambda v: v.drain())
                        P.emit("vector", lambda v: v.memset(st8_t[:, 2:8], -1.0e30))
                        P.emit("vector", lambda v: v.drain())
                        P.emit("vector", lambda v: v.match_replace(s2t_t[:], st8_t[:], scr8_t[:], -2.0e30))
                        P.emit("vector", lambda v: v.drain())
                        P.emit("vector", lambda v: v.tensor_single_scalar(mask8_t[:], s2t_t[:], -1.0e29, ALU.is_lt))
                        P.emit("vector", lambda v: v.drain())
                        P.emit("vector", lambda v: v.tensor_mul(mask8_t[:], mask8_t[:], scr8_t[:]))
                        P.emit("vector", lambda v: v.tensor_add(stq[:, 2:3], st8_t[:, 0:1], st8_t[:, 1:2]))
                        P.emit("vector", lambda v: v.drain())
                        P.emit("vector", lambda v: v.tensor_scalar(s2t_t[:, 0:1], stq[:, 2:3], 1.0, 1e-20, ALU.mult, ALU.add))
                        P.emit("vector", lambda v: v.drain())
                        P.emit("vector", lambda v: v.reciprocal(s8_t[:], s2t_t[:, 0:1]))
                        P.emit("vector", lambda v: v.drain())
                        P.emit("vector", lambda v, qt=qt: v.tensor_scalar(
                            comb_t[:, qt, :], mask8_t[:], s8_t[:, 0:1], None, ALU.mult))
                        P.emit("vector", lambda v: v.drain(), inc="cmb")

                    wo_prev_g = P.val("wo")
                    P.cnt["_mskuse_%d" % g] = P.val("msk")

                if debug >= 2:
                    P.emit("gpsimd", lambda g: g.dma_start(out=dbg["acc"], in_=acc_t[:]),
                           waits=[("xf", P.val("xf"))], inc="dbgs", inc_n=16)
                    P.emit("gpsimd", lambda g: g.dma_start(out=dbg["comb"], in_=comb_t[:]),
                           waits=[("cmb", P.val("cmb"))], inc="dbgs", inc_n=16)
                    P.emit("gpsimd", lambda g: g.dma_start(out=dbg["attnT"], in_=attnT[:]),
                           waits=[("ate", P.val("ate"))], inc="dbgs", inc_n=16)

                P.emit("sync", lambda sy: sy.sem_inc(sems["bar2"], 1), waits=[("xqd", P.val("xqd"))])
                P.emit("tensor", lambda te: te.sem_inc(sems["bar2"], 1))
                P.emit("vector", lambda v: v.sem_inc(sems["bar2"], 1), waits=[("cmb", P.val("cmb"))])
                P.emit("scalar", lambda s: s.sem_inc(sems["bar2"], 1))
                P.emit("gpsimd", lambda g: g.sem_inc(sems["bar2"], 1),
                       waits=[("wB", W_B), ("dbgs", P.val("dbgs"))])
                P.cnt["bar2"] = 5
                P.run_block(nc, sems)
                cnt2 = P.cnt

        if phases < 3:
            for c in sems_ctx:
                c.__exit__(None, None, None)
            return nc
        with ExitStack() as _stD:
            sbD = lambda *a: _stD.enter_context(nc.sbuf_tensor(*a))
            psD = lambda *a: _stD.enter_context(nc.psum_tensor(*a))
            xfT = sbD("xfT", [128, 8, NSLOT * 128], BF16)
            ht_sb = sbD("ht", [128, NHC, NSLOT * 128], BF16)
            wg_sb = sbD("wg_sb", [128, NHC * 8, 128], BF16)
            wu_sb = sbD("wu_sb", [128, NHC * 8, 128], BF16)
            wd_sb = sbD("wd_sb", [128, NHC, 1024], BF16)
            sG0 = sbD("sG0", [128, 512], BF16)
            sG1 = sbD("sG1", [128, 512], BF16)
            xc2 = sbD("xc2", [128, D], F32)
            sq2 = sbD("sq2", [128, D], F32)
            st2 = sbD("st2", [128, 2], F32)
            st3 = sbD("st3", [128, 1], F32)
            rstd2 = sbD("rstd2", [128, 1], F32)
            rln2 = sbD("rln2", [128, 1], F32)
            outst0 = sbD("outst0", [128, D], F32)
            outst1 = sbD("outst1", [128, D], F32)
            pG0 = psD("pG0", [128, 512], F32)
            pG1 = psD("pG1", [128, 512], F32)
            pU0 = psD("pU0", [128, 512], F32)
            pU1 = psD("pU1", [128, 512], F32)
            pD0 = psD("pD0", [128, 1024], F32)
            
            pT3 = psD("pT3", [128, 8, 128], BF16)
            P = Prog(cnt2)
            pG, pU = [pG0, pG1], [pU0, pU1]
            pD = pD0
            sG = [sG0, sG1]
            outst = [outst0, outst1]

            for t in range(NSLOT):
                codd = t % 2 == 1
                if codd:
                    def tpx(te, t=t):
                        ins = None
                        for dc in range(8):
                            ins = te.matmul(pD0[:, dc * 128:(dc + 1) * 128],
                                            acc_t[:, t, dc * 128:(dc + 1) * 128], ident[:],
                                            start=True, stop=True)
                        return ins
                else:
                    def tpx(te, t=t):
                        ins = None
                        for dc in range(8):
                            ins = te.transpose(pT3[:, dc, :], acc_t[:, t, dc * 128:(dc + 1) * 128], ident[:])
                        return ins
                P.emit("tensor", tpx, waits=[("bar2", 5), ("xft", max(0, P.val("xft") - 1))], inc="tpf")
                if codd:
                    def cpx(v, t=t):
                        for dc in range(8):
                            v.tensor_copy(xfT[:, dc, t * 128:(t + 1) * 128], pD0[:, dc * 128:(dc + 1) * 128])
                    P.emit("vector", cpx, waits=[("bar2", 5), ("tpf", P.val("tpf"))])
                else:
                    P.emit("vector", lambda v, t=t: v.tensor_copy(xfT[:, :, t * 128:(t + 1) * 128], pT3[:]),
                           waits=[("bar2", 5), ("tpf", P.val("tpf"))])
                P.emit("vector", lambda v: v.drain(), inc="xft")
            XFT_ALL = P.val("xft")

            for e in range(NE):
                for hc in range(NHC):
                    P.emit("gpsimd", lambda g, e=e, hc=hc: g.dma_start(
                        out=wg_sb[:, hc * 8:(hc + 1) * 8, :],
                        in_=wg_h[e][:, hc * 128:(hc + 1) * 128].rearrange("(dc p) q -> p dc q", p=128)),
                        waits=[("bar2", 5), ("gmm", 44 * (e - 1) + 4 * (hc + 1) if e else 0)],
                        inc="wg", inc_n=16)
                    P.emit("gpsimd", lambda g, e=e, hc=hc: g.dma_start(
                        out=wu_sb[:, hc * 8:(hc + 1) * 8, :],
                        in_=wu_h[e][:, hc * 128:(hc + 1) * 128].rearrange("(dc p) q -> p dc q", p=128)),
                        waits=[("umm", 44 * (e - 1) + 4 * (hc + 1) if e else 0)],
                        inc="wu", inc_n=16)
                P.emit("gpsimd", lambda g, e=e: g.dma_start(
                    out=wd_sb[:], in_=wd_h[e].rearrange("(hc p) n -> p hc n", p=128)),
                    waits=[("dmm", 16 * e)], inc="wd", inc_n=16)

            for e in range(NE):
                for hc in range(NHC):
                    for tc in range(4):
                        gi = (hc * 4 + tc) % 2
                        def gmm(te, hc=hc, tc=tc, gi=gi, e=e):
                            ins = None
                            for dc in range(8):
                                ins = te.matmul(pG[gi][:], wg_sb[:, hc * 8 + dc, :],
                                                xfT[:, dc, tc * 512:(tc + 1) * 512],
                                                start=(dc == 0), stop=(dc == 7))
                            return ins
                        gwaits = [("wg", 16 * NHC * (e + 1)), ("xft", XFT_ALL),
                                  ("sg", P.val("sg") - 1)]
                        if e and hc == 0 and tc == 0:
                            gwaits.append(("acc", 16 * e))
                        P.emit("tensor", gmm, waits=gwaits, inc="gmm")
                        def umm(te, hc=hc, tc=tc, gi=gi, e=e):
                            ins = None
                            for dc in range(8):
                                ins = te.matmul(pU[gi][:], wu_sb[:, hc * 8 + dc, :],
                                                xfT[:, dc, tc * 512:(tc + 1) * 512],
                                                start=(dc == 0), stop=(dc == 7))
                            return ins
                        P.emit("tensor", umm,
                               waits=[("wu", 16 * NHC * (e + 1)), ("hm", P.val("hm") - 1)], inc="umm")
                        P.emit("scalar", lambda s, gi=gi: s.activation(sG[gi][:], pG[gi][:], AF.Silu),
                               waits=[("gmm", P.val("gmm")), ("hm", P.val("hm") - 1)])
                        P.emit("scalar", lambda s: s.drain(), inc="sg")
                        P.emit("vector", lambda v, gi=gi, hc=hc, tc=tc: v.tensor_tensor(
                            ht_sb[:, hc, tc * 512:(tc + 1) * 512], sG[gi][:], pU[gi][:], ALU.mult),
                            waits=[("sg", P.val("sg")), ("umm", P.val("umm"))])
                        P.emit("vector", lambda v: v.drain(), inc="hm")
                HM_E = P.val("hm")
                for t in range(NSLOT):
                    odd = t % 2 == 1
                    if odd:
                        def dmm(te, t=t):
                            ins = None
                            for hc in range(NHC):
                                ins = te.matmul(pG0[:], ht_sb[:, hc, t * 128:(t + 1) * 128],
                                                wd_sb[:, hc, 0:512],
                                                start=(hc == 0), stop=(hc == NHC - 1))
                                ins = te.matmul(pG1[:], ht_sb[:, hc, t * 128:(t + 1) * 128],
                                                wd_sb[:, hc, 512:1024],
                                                start=(hc == 0), stop=(hc == NHC - 1))
                            return ins
                    else:
                        def dmm(te, t=t):
                            ins = None
                            for hc in range(NHC):
                                for nb in range(2):
                                    ins = te.matmul(pD[:, nb * 512:(nb + 1) * 512],
                                                    ht_sb[:, hc, t * 128:(t + 1) * 128],
                                                    wd_sb[:, hc, nb * 512:(nb + 1) * 512],
                                                    start=(hc == 0), stop=(hc == NHC - 1))
                            return ins
                    P.emit("tensor", dmm,
                           waits=[("wd", 16 * (e + 1)), ("hm", HM_E),
                                  ("acc", max(0, P.val("acc") - 1))],
                           inc="dmm")
                    if odd:
                        if e == NE - 1:
                            P.emit("vector", lambda v, e=e, t=t: v.scalar_tensor_tensor(
                                acc_t[:, t, 0:512], pG0[:], comb_t[:, t, e:e + 1], acc_t[:, t, 0:512],
                                ALU.mult, ALU.add, accum_out=st2[:, 0:1]),
                                waits=[("dmm", P.val("dmm"))])
                            P.emit("vector", lambda v, e=e, t=t: v.scalar_tensor_tensor(
                                acc_t[:, t, 512:1024], pG1[:], comb_t[:, t, e:e + 1], acc_t[:, t, 512:1024],
                                ALU.mult, ALU.add, accum_out=st3[:, 0:1]))
                            P.emit("vector", lambda v: v.drain())
                            P.emit("vector", lambda v: v.tensor_add(st2[:, 0:1], st2[:, 0:1], st3[:, 0:1]))
                        else:
                            P.emit("vector", lambda v, e=e, t=t: v.scalar_tensor_tensor(
                                acc_t[:, t, 0:512], pG0[:], comb_t[:, t, e:e + 1], acc_t[:, t, 0:512],
                                ALU.mult, ALU.add),
                                waits=[("dmm", P.val("dmm"))])
                            P.emit("vector", lambda v, e=e, t=t: v.scalar_tensor_tensor(
                                acc_t[:, t, 512:1024], pG1[:], comb_t[:, t, e:e + 1], acc_t[:, t, 512:1024],
                                ALU.mult, ALU.add))
                    elif e == NE - 1:
                        P.emit("vector", lambda v, e=e, t=t: v.scalar_tensor_tensor(
                            acc_t[:, t, :], pD[:], comb_t[:, t, e:e + 1], acc_t[:, t, :],
                            ALU.mult, ALU.add, accum_out=st2[:, 0:1]),
                            waits=[("dmm", P.val("dmm"))])
                    else:
                        P.emit("vector", lambda v, e=e, t=t: v.scalar_tensor_tensor(
                            acc_t[:, t, :], pD[:], comb_t[:, t, e:e + 1], acc_t[:, t, :],
                            ALU.mult, ALU.add),
                            waits=[("dmm", P.val("dmm"))])
                    P.emit("vector", lambda v: v.drain(), inc="acc")
                    if e == NE - 1:
                        ob = t % 2
                        P.emit("vector", lambda v: v.tensor_scalar(st2[:, 0:1], st2[:, 0:1], 1.0 / D, None, ALU.mult),
                               waits=[("acc", P.val("acc"))])
                        P.emit("vector", lambda v: v.drain())
                        P.emit("vector", lambda v, t=t: v.tensor_scalar(xc2[:], acc_t[:, t, :], st2[:, 0:1], None, ALU.subtract))
                        P.emit("vector", lambda v: v.drain())
                        P.emit("vector", lambda v: v.scalar_tensor_tensor(
                            sq2[:], xc2[:], 1.0, xc2[:], ALU.mult, ALU.mult, accum_out=st2[:, 1:2]))
                        P.emit("vector", lambda v: v.drain(), inc="ln2")
                        P.emit("scalar", lambda s: s.activation(rln2[:], st2[:, 1:2], AF.Ln,
                                                                bias=epsb[:, 0:1], scale=1.0 / D),
                               waits=[("ln2", P.val("ln2"))])
                        P.emit("scalar", lambda s: s.drain())
                        P.emit("scalar", lambda s: s.activation(rstd2[:], rln2[:], AF.Exp, bias=0.0, scale=-0.5))
                        P.emit("scalar", lambda s: s.drain(), inc="sd2")
                        P.emit("vector", lambda v, ob=ob: v.tensor_scalar(
                            outst[ob][:], xc2[:], rstd2[:, 0:1], None, ALU.mult),
                            waits=[("sd2", P.val("sd2")), ("out", 16 * t)])
                        P.emit("vector", lambda v: v.drain(), inc="eo")
                        P.emit("sync", lambda sy, t=t, ob=ob: sy.dma_start(
                            out=out_h[t * 128:(t + 1) * 128, :], in_=outst[ob][:]),
                            waits=[("eo", P.val("eo"))], inc="out", inc_n=16)

            P.emit("sync", lambda sy: sy.nop(), waits=[("out", 16 * NSLOT)])
            P.run_block(nc, sems)

        for c in sems_ctx:
            c.__exit__(None, None, None)
    return nc



def _make_masks(tiles):
    m = np.zeros((128, NSLOT, 2, 128), np.float32)
    tri = (np.arange(128)[:, None] <= np.arange(128)[None, :]).astype(np.float32)
    for j, t in enumerate(tiles):
        c = t + 1
        for i in range(2):
            kc = 2 * j + i
            if kc < c - 1:
                m[:, j, i, :] = 1.0
            elif kc == c - 1:
                m[:, j, i, :] = tri
    return np.ascontiguousarray(m.astype(ml_dtypes.bfloat16))


def kernel(L0_attn_x, L0_attn_W1_q, L0_attn_W2_q, L0_attn_W1_k, L0_attn_W2_k,
           L0_attn_W_v, L0_attn_b_v, L0_attn_W_o, L0_attn_b_o,
           L0_ffn_w_router, L0_ffn_w_gate, L0_ffn_w_up, L0_ffn_w_down,
           _debug=0):
    x = np.ascontiguousarray(np.asarray(L0_attn_x, dtype=np.float32))
    arrs = {
        "W1_q": L0_attn_W1_q, "W2_q": L0_attn_W2_q, "W1_k": L0_attn_W1_k,
        "W2_k": L0_attn_W2_k, "W_v": L0_attn_W_v, "W_o": L0_attn_W_o,
        "w_router": L0_ffn_w_router, "w_gate": L0_ffn_w_gate,
        "w_up": L0_ffn_w_up, "w_down": L0_ffn_w_down,
    }
    arrs = {k: np.ascontiguousarray(np.asarray(v, dtype=np.float32)) for k, v in arrs.items()}
    bv = np.asarray(L0_attn_b_v, dtype=np.float32)
    bo = np.asarray(L0_attn_b_o, dtype=np.float32)
    arrs["b_v_b"] = np.ascontiguousarray(np.broadcast_to(bv, (128, D)))
    arrs["b_o_b"] = np.ascontiguousarray(np.broadcast_to(bo, (128, D)))
    arrs["ident_hbm"] = np.eye(128, dtype=np.float32)
    masks = {0: _make_masks(TILES_A), 1: _make_masks(TILES_B)}

    in_maps = []
    for c in range(8):
        b, v_ = c // 2, c % 2
        tiles = TILES_A if v_ == 0 else TILES_B
        rows = np.concatenate([np.arange(t * 128, (t + 1) * 128) for t in tiles])
        xb = x[b]
        prev = rows - 1
        xqp = np.where((prev >= 0)[:, None], xb[np.maximum(prev, 0)], np.float32(0.0))
        m = dict(arrs)
        m["x_kv"] = xb
        m["xq"] = np.ascontiguousarray(xb[rows])
        m["xq_prev"] = np.ascontiguousarray(xqp.astype(np.float32))
        m["masks"] = masks[v_]
        in_maps.append(m)

    nc = build_program(debug=_debug, phases=_PHASES)
    res = run_bass_kernel_spmd(nc, in_maps, core_ids=list(range(8)))

    out = np.zeros((B, T, D), np.float32)
    for c in range(8):
        b, v_ = c // 2, c % 2
        tiles = TILES_A if v_ == 0 else TILES_B
        oq = np.asarray(res.results[c]["out_q"])
        for j, t in enumerate(tiles):
            out[b, t * 128:(t + 1) * 128, :] = oq[j * 128:(j + 1) * 128, :]
    if _debug:
        return out, res
    return out


# version 21 (final)
# speedup vs baseline: 1.1263x; 1.0029x over previous
from contextlib import ExitStack

import numpy as np
import ml_dtypes
import concourse.bass as bass
import concourse.mybir as mybir
from concourse.bass_utils import run_bass_kernel_spmd

F32 = mybir.dt.float32
BF16 = mybir.dt.bfloat16
AF = mybir.ActivationFunctionType
ALU = mybir.AluOpType
AX = mybir.AxisListType

B, T, D, H, DH, C6 = 4, 4096, 1024, 16, 64, 6
NKT = 32
NSLOT = 16
NG = 4
HFF = 1408
NE = 8
NHC = 11
SCALE = float(6.0 ** -0.5)
PAIRS = [(0, 1), (0, 2), (0, 3), (1, 2), (1, 3), (2, 3)]
JSIGN = [1.0, -1.0, 1.0, 1.0, -1.0, 1.0]
TILES_A = [0, 3, 4, 7, 8, 11, 12, 15, 16, 19, 20, 23, 24, 27, 28, 31]
TILES_B = [1, 2, 5, 6, 9, 10, 13, 14, 17, 18, 21, 22, 25, 26, 29, 30]
ENGINES = ("sync", "tensor", "vector", "scalar", "gpsimd")
_PHASES = 3
_SKIP_Q = False


def kmax(g):
    return 8 * g + 8


class Prog:

    def __init__(self, cnt=None):
        self.ops = {e: [] for e in ENGINES}
        self.cnt = dict(cnt) if cnt else {}

    def val(self, sem):
        return self.cnt.get(sem, 0)

    def emit(self, eng, fn, waits=(), inc=None, inc_n=1):
        w = [(s, v) for (s, v) in waits if v > 0]
        self.ops[eng].append((fn, w, inc, inc_n))
        if inc is not None:
            self.cnt[inc] = self.cnt.get(inc, 0) + inc_n

    def replay(self, eng_name, handle, sems):
        for fn, waits, inc, inc_n in self.ops[eng_name]:
            for s, v in waits:
                handle.wait_ge(sems[s], v)
            ins = fn(handle)
            if inc is not None:
                if ins is None:
                    raise RuntimeError(f"op with inc {inc} returned no instruction")
                ins.then_inc(sems[inc], inc_n)

    def run_block(self, nc, sems):
        with nc.Block() as blk:
            blk.sync(lambda e: self.replay("sync", e, sems))
            blk.tensor(lambda e: self.replay("tensor", e, sems))
            blk.vector(lambda e: self.replay("vector", e, sems))
            blk.scalar(lambda e: self.replay("scalar", e, sems))
            blk.gpsimd(lambda e: self.replay("gpsimd", e, sems))


def build_program(debug=0, phases=3):
    nc = bass.Bass("TRN2", target_bir_lowering=False)

    x_kv = nc.dram_tensor("x_kv", [T, D], F32, kind="ExternalInput").ap()
    xq_h = nc.dram_tensor("xq", [NSLOT * 128, D], F32, kind="ExternalInput").ap()
    xqp_h = nc.dram_tensor("xq_prev", [NSLOT * 128, D], F32, kind="ExternalInput").ap()
    masks_h = nc.dram_tensor("masks", [128, NSLOT, 2, 128], BF16, kind="ExternalInput").ap()
    ident_h = nc.dram_tensor("ident_hbm", [128, 128], F32, kind="ExternalInput").ap()
    w1q_h = nc.dram_tensor("W1_q", [2 * D, 64], F32, kind="ExternalInput").ap()
    w2q_h = nc.dram_tensor("W2_q", [2 * D, 64], F32, kind="ExternalInput").ap()
    w1k_h = nc.dram_tensor("W1_k", [D, 64], F32, kind="ExternalInput").ap()
    w2k_h = nc.dram_tensor("W2_k", [D, 64], F32, kind="ExternalInput").ap()
    wv_h = nc.dram_tensor("W_v", [D, D], F32, kind="ExternalInput").ap()
    bvb_h = nc.dram_tensor("b_v_b", [128, D], F32, kind="ExternalInput").ap()
    wo_h = nc.dram_tensor("W_o", [D, D], F32, kind="ExternalInput").ap()
    bob_h = nc.dram_tensor("b_o_b", [128, D], F32, kind="ExternalInput").ap()
    wr_h = nc.dram_tensor("w_router", [D, NE], F32, kind="ExternalInput").ap()
    wg_h = nc.dram_tensor("w_gate", [NE, D, HFF], F32, kind="ExternalInput").ap()
    wu_h = nc.dram_tensor("w_up", [NE, D, HFF], F32, kind="ExternalInput").ap()
    wd_h = nc.dram_tensor("w_down", [NE, HFF, D], F32, kind="ExternalInput").ap()
    out_h = nc.dram_tensor("out_q", [NSLOT * 128, D], F32, kind="ExternalOutput").ap()

    dbg = {}
    if debug >= 1:
        dbg["vs"] = nc.dram_tensor("dbg_vs", [128, NKT, H, 65], F32, kind="ExternalOutput").ap()
        dbg["jlkt"] = nc.dram_tensor("dbg_jlkt", [128, 4, NKT * 128], F32, kind="ExternalOutput").ap()
        dbg["lqt"] = nc.dram_tensor("dbg_lqt", [128, 4, NSLOT * 128], F32, kind="ExternalOutput").ap()
    if debug >= 3:
        dbg["kk"] = nc.dram_tensor("dbg_kk", [128, 128], F32, kind="ExternalOutput").ap()
        dbg["lraw"] = nc.dram_tensor("dbg_lraw", [128, H, C6], F32, kind="ExternalOutput").ap()
        dbg["ssb"] = nc.dram_tensor("dbg_ssb", [128, H], F32, kind="ExternalOutput").ap()
        dbg["rnrm"] = nc.dram_tensor("dbg_rnrm", [128, H], F32, kind="ExternalOutput").ap()
        dbg["lsc"] = nc.dram_tensor("dbg_lsc", [128, H, C6], F32, kind="ExternalOutput").ap()
    if debug >= 2:
        dbg["acc"] = nc.dram_tensor("dbg_acc", [128, NSLOT, D], F32, kind="ExternalOutput").ap()
        dbg["comb"] = nc.dram_tensor("dbg_comb", [128, NSLOT, NE], F32, kind="ExternalOutput").ap()
        dbg["attnT"] = nc.dram_tensor("dbg_attnT", [128, 8, 512], F32, kind="ExternalOutput").ap()

    sem_names = [
        "wA", "z0", "x0", "x1", "cast", "tp", "xt", "mmv", "ss", "rr", "lk", "tk", "jl",
        "bar1", "wB", "mk0", "mk1", "sc", "exp", "msk", "av", "sr", "rbm", "rbf", "rbr",
        "ate", "wo", "xqd", "y1a", "v1", "sd1", "xf", "tpf", "xft", "rt",
        "rmax", "rexp", "cmb", "bar2", "wg", "wu", "wd", "gmm", "umm", "sg",
        "hm", "dmm", "acc", "ln2", "sd2", "eo", "out", "dbgs",
    ]

    with ExitStack() as _st0:
        sb = lambda *a: _st0.enter_context(nc.sbuf_tensor(*a))
        ident = sb("ident", [128, 128], BF16)
        acc_t = sb("acc", [128, NSLOT, D], BF16)
        comb_t = sb("comb", [128, NSLOT, NE], F32)
        rtr_f = sb("rtr_f", [128, 8, NE], F32)
        rtr_hi = sb("rtr_hi", [128, 8, NE], BF16)
        rtr_lo = sb("rtr_lo", [128, 8, NE], BF16)
        epsb = sb("epsb", [128, 1], F32)
        sems_ctx = [nc.semaphore(n) for n in sem_names]
        sems = dict(zip(sem_names, (c.__enter__() for c in sems_ctx)))

        with ExitStack() as _st1:
            sb1 = lambda *a: _st1.enter_context(nc.sbuf_tensor(*a))
            Vs = sb1("Vs", [128, NKT, H, 65], BF16)
            JLKT = sb1("JLKT", [128, 4, NKT * 128], BF16)
            LQT = sb1("LQT", [128, 4, NSLOT * 128], BF16)
            with ExitStack() as _stA:
                sbA = lambda *a: _stA.enter_context(nc.sbuf_tensor(*a))
                psA = lambda *a: _stA.enter_context(nc.psum_tensor(*a))
                wvks = sbA("wvks", [128, 8, 1152], BF16)
                wq_sb = sbA("wq_sb", [128, 16, 128], BF16)
                bvb = sbA("bvb", [128, D], F32)
                xa0 = sbA("xa0", [128, D], F32)
                xa1 = sbA("xa1", [128, D], F32)
                xab0 = sbA("xab0", [128, D], BF16)
                xab1 = sbA("xab1", [128, D], BF16)
                xT0 = sbA("xT0", [128, 8, 128], BF16)
                xT1 = sbA("xT1", [128, 8, 128], BF16)
                xTP0 = sbA("xTP0", [128, 8, 128], BF16)
                xTP1 = sbA("xTP1", [128, 8, 128], BF16)
                Lraw0 = sbA("Lraw0", [128, H, C6], F32)
                Lraw1 = sbA("Lraw1", [128, H, C6], F32)
                Lsq = sbA("Lsq", [128, H, C6], F32)
                prodAll = sbA("prodAll", [128, 12, H], F32)
                sst = sbA("sst", [128, 3, H], F32)
                ssb = sbA("ssb", [128, H], F32)
                rnrm0 = sbA("rnrm0", [128, H], F32)
                rnrm1 = sbA("rnrm1", [128, H], F32)
                nrnrm = sbA("nrnrm", [128, H], F32)
                rln = sbA("rln", [128, H], F32)
                Lsc = sbA("Lsc", [128, H, C6], BF16)
                kk_sb = sbA("kk_sb", [128, 128], F32)
                zb = sbA("zb", [128, 128], BF16)
                pT = psA("pT", [128, 8, 128], BF16)
                pTk = psA("pTk", [128, 4, 128], BF16)
                pV0 = psA("pV0", [128, 1152], F32)
                pV1 = psA("pV1", [128, 1152], F32)
                P = Prog()
                xa, xab, xT, xTP = [xa0, xa1], [xab0, xab1], [xT0, xT1], [xTP0, xTP1]
                pV = [pV0, pV1]

                P.emit("gpsimd", lambda g: g.dma_start(out=ident[:], in_=ident_h), inc="wA", inc_n=16)
                P.emit("gpsimd", lambda g: g.dma_start(
                    out=wvks[:, :, 0:1024], in_=wv_h.rearrange("(dc p) n -> p dc n", p=128)), inc="wA", inc_n=16)
                P.emit("gpsimd", lambda g: g.dma_start(
                    out=wvks[:, :, 1024:1088], in_=w1k_h.rearrange("(dc p) n -> p dc n", p=128)), inc="wA", inc_n=16)
                P.emit("gpsimd", lambda g: g.dma_start(
                    out=wvks[:, :, 1088:1152], in_=w2k_h.rearrange("(dc p) n -> p dc n", p=128)), inc="wA", inc_n=16)
                P.emit("gpsimd", lambda g: g.dma_start(
                    out=wq_sb[:, :, 0:64], in_=w1q_h.rearrange("(kc p) n -> p kc n", p=128)), inc="wA", inc_n=16)
                P.emit("gpsimd", lambda g: g.dma_start(
                    out=wq_sb[:, :, 64:128], in_=w2q_h.rearrange("(kc p) n -> p kc n", p=128)), inc="wA", inc_n=16)
                P.emit("gpsimd", lambda g: g.dma_start(out=bvb[:], in_=bvb_h), inc="wA", inc_n=16)
                P.emit("gpsimd", lambda g: g.dma_start(
                    out=rtr_f[:], in_=wr_h.rearrange("(dc p) n -> p dc n", p=128)), inc="wA", inc_n=16)
                W_SETUP = P.val("wA")

                P.emit("vector", lambda v: v.memset(epsb[:], 1e-5))
                P.emit("vector", lambda v: v.drain())
                P.emit("vector", lambda v: v.memset(zb[:], 0.0))
                P.emit("vector", lambda v: v.drain(), inc="z0")
                def ztps(te):
                    ins = None
                    for m in range(4):
                        ins = te.transpose(pTk[:, m, :], zb[:], ident[:])
                    return ins
                P.emit("tensor", ztps, waits=[("z0", 1), ("wA", W_SETUP)], inc="tk")
                P.emit("vector", lambda v: v.tensor_copy(rtr_hi[:], rtr_f[:]), waits=[("wA", W_SETUP)])
                P.emit("vector", lambda v: v.drain())
                P.emit("vector", lambda v: v.tensor_tensor(rtr_lo[:], rtr_f[:], rtr_hi[:], ALU.subtract))
                P.emit("vector", lambda v: v.drain())


                xt_count = {}

                def tile_prep(idx, src_ap, row0, xT_list, xT_slot, xT_free):
                    b = idx % 2
                    xs = "x%d" % b
                    P.emit("sync", lambda sy, s=src_ap, r=row0, bb=b: sy.dma_start(
                        out=xa[bb][:], in_=s[r:r + 128, :]),
                        waits=[("cast", idx - 1)], inc=xs, inc_n=16)
                    P.emit("vector", lambda v, bb=b: v.tensor_copy(xab[bb][:], xa[bb][:]),
                           waits=[(xs, 16 * (idx // 2 + 1)), ("tp", idx - 1)])
                    P.emit("vector", lambda v: v.drain(), inc="cast")
                    def tps(te, bb=b):
                        ins = None
                        for dc in range(8):
                            ins = te.transpose(pT[:, dc, :], xab[bb][:, dc * 128:(dc + 1) * 128], ident[:])
                        return ins
                    P.emit("tensor", tps, waits=[("cast", idx + 1), ("xt", idx)], inc="tp")
                    P.emit("vector", lambda v, sl=xT_slot: v.tensor_copy(xT_list[sl][:], pT[:]),
                           waits=[("tp", idx + 1)] + list(xT_free))
                    P.emit("vector", lambda v: v.drain(), inc="xt")
                    xt_count[idx] = P.val("xt")

                Lraw = [Lraw0, Lraw1]
                rnrm = [rnrm0, rnrm1]
                ext_state = {}
                ext_serial = [0]

                def ext_front(pv_t, off0, jside, dest, col0):
                    se = ext_serial[0]
                    ext_serial[0] += 1
                    lb = se % 2
                    P.emit("vector", lambda v: v.tensor_copy(kk_sb[:], pv_t[:, off0:off0 + 128]),
                           waits=[("mmv", P.val("mmv"))])
                    P.emit("vector", lambda v: v.drain())
                    o1, o2 = 0, 64
                    for c, (i, j) in enumerate(PAIRS):
                        P.emit("vector", lambda v, i=i, j=j, c=c: v.tensor_mul(
                            prodAll[:, 2 * c, :], kk_sb[:, o1 + i: o1 + 64: 4], kk_sb[:, o2 + j: o2 + 64: 4]))
                        P.emit("vector", lambda v, i=i, j=j, c=c: v.tensor_mul(
                            prodAll[:, 2 * c + 1, :], kk_sb[:, o1 + j: o1 + 64: 4], kk_sb[:, o2 + i: o2 + 64: 4]))
                    P.emit("vector", lambda v: v.drain())
                    for c in range(C6):
                        P.emit("vector", lambda v, c=c, lb=lb: v.tensor_tensor(
                            Lraw[lb][:, :, c], prodAll[:, 2 * c, :], prodAll[:, 2 * c + 1, :], ALU.subtract))
                    P.emit("vector", lambda v: v.drain())
                    P.emit("vector", lambda v, lb=lb: v.tensor_tensor(Lsq[:], Lraw[lb][:], Lraw[lb][:], ALU.mult),
                           waits=[("rr", P.val("rr"))])
                    P.emit("vector", lambda v: v.drain())
                    for k in range(3):
                        P.emit("vector", lambda v, k=k: v.tensor_add(
                            sst[:, k, :], Lsq[:, :, 2 * k], Lsq[:, :, 2 * k + 1]))
                    P.emit("vector", lambda v: v.drain())
                    P.emit("vector", lambda v: v.tensor_add(sst[:, 0, :], sst[:, 0, :], sst[:, 1, :]))
                    P.emit("vector", lambda v: v.drain())
                    P.emit("vector", lambda v: v.tensor_add(sst[:, 1, :], sst[:, 0, :], sst[:, 2, :]))
                    P.emit("vector", lambda v: v.drain())
                    P.emit("vector", lambda v: v.tensor_scalar_max(ssb[:], sst[:, 1, :], 1e-24))
                    P.emit("vector", lambda v: v.drain(), inc="ss")
                    P.emit("scalar", lambda s: s.activation(rln[:], ssb[:], AF.Ln),
                           waits=[("ss", P.val("ss")), ("lk", P.val("lk"))])
                    P.emit("scalar", lambda s: s.drain())
                    P.emit("scalar", lambda s, lb=lb: s.activation(rnrm[lb][:], rln[:], AF.Exp, bias=0.0, scale=-0.5))
                    P.emit("scalar", lambda s: s.drain(), inc="rr")
                    ext_state[se] = (jside, dest, col0, lb)

                def ext_back(se):
                    jside, dest, col0, lb = ext_state.pop(se)
                    P.emit("vector", lambda v, lb=lb: v.tensor_scalar_mul(nrnrm[:], rnrm[lb][:], -1.0),
                           waits=[("rr", se + 1), ("tk", P.val("tk"))])
                    P.emit("vector", lambda v: v.drain())
                    for c in range(C6):
                        if jside:
                            oc = 5 - c
                            m = rnrm[lb] if JSIGN[oc] > 0 else nrnrm
                            P.emit("vector", lambda v, c=c, oc=oc, m=m, lb=lb: v.tensor_tensor(
                                Lsc[:, :, oc], Lraw[lb][:, :, c], m[:], ALU.mult))
                        else:
                            P.emit("vector", lambda v, c=c, lb=lb: v.tensor_tensor(
                                Lsc[:, :, c], Lraw[lb][:, :, c], rnrm[lb][:], ALU.mult))
                    P.emit("vector", lambda v: v.drain(), inc="lk")
                    head_transposes(dest, col0)

                def head_transposes(dest, col0):
                    def tps(te):
                        ins = None
                        for h in range(H):
                            g_, m = h % 4, h // 4
                            ins = te.transpose(
                                pTk[32 * g_:32 * g_ + 6, m, :], Lsc[:, h, :], ident[:],
                                tile_position=(0, 32 * g_))
                        return ins
                    P.emit("tensor", tps, waits=[("lk", P.val("lk")), ("jl", P.val("jl"))], inc="tk")
                    P.emit("vector", lambda v: v.tensor_copy(dest[:, :, col0:col0 + 128], pTk[:]),
                           waits=[("tk", P.val("tk"))])
                    P.emit("vector", lambda v: v.drain(), inc="jl")

                def emit_vmm(kt):
                    b = kt % 2
                    def vmm(te, b=b):
                        ins = None
                        for dc in range(8):
                            for (n0, n1) in ((0, 512), (512, 1024), (1024, 1152)):
                                ins = te.matmul(pV[b][:, n0:n1], xT[b][:, dc, :], wvks[:, dc, n0:n1],
                                                start=(dc == 0), stop=(dc == 7))
                        return ins
                    P.emit("tensor", vmm,
                           waits=[("xt", xt_count[kt]), ("ss", kt - 1), ("wA", W_SETUP)], inc="mmv")
                    P.emit("vector", lambda v, b=b, kt=kt: v.tensor_tensor(
                        Vs[:, kt, :, 0:64], pV[b][:, 0:1024], bvb[:], ALU.add),
                        waits=[("mmv", P.val("mmv"))])
                    P.emit("vector", lambda v: v.drain())

                tile_prep(0, x_kv, 0, xT, 0, [("mmv", -1)])
                for kt in range(NKT):
                    if kt + 1 < NKT:
                        tile_prep(kt + 1, x_kv, (kt + 1) * 128, xT, (kt + 1) % 2,
                                  [("mmv", kt)])
                    emit_vmm(kt)
                    ext_front(pV[kt % 2], 1024, True, JLKT, kt * 128)
                    if kt >= 1:
                        ext_back(kt - 1)
                P.emit("vector", lambda v: v.memset(Vs[:, :, :, 64:65], 1.0))
                P.emit("vector", lambda v: v.drain(), inc="jl")

                def emit_qmm(s_):
                    sb_ = s_ % 2
                    pvb = pV[s_ % 2]
                    def qmm(te, sb_=sb_, pvb=pvb):
                        ins = None
                        for kc in range(16):
                            lhs = xT[sb_][:, kc, :] if kc < 8 else xTP[sb_][:, kc - 8, :]
                            ins = te.matmul(pvb[:, 0:128], lhs, wq_sb[:, kc, :],
                                            start=(kc == 0), stop=(kc == 15))
                        return ins
                    P.emit("tensor", qmm,
                           waits=[("xt", xt_count[NKT + 2 * s_ + 1]), ("ss", NKT + s_ - 1), ("wA", W_SETUP)], inc="mmv")

                tile_prep(NKT, xq_h, 0, xT, 0, [("mmv", NKT - 1)])
                tile_prep(NKT + 1, xqp_h, 0, xTP, 0, [("mmv", NKT - 1)])
                for s_ in range(NSLOT):
                    if s_ + 1 < NSLOT:
                        i0 = NKT + 2 * (s_ + 1)
                        tile_prep(i0, xq_h, (s_ + 1) * 128, xT, (s_ + 1) % 2, [("mmv", NKT + s_)])
                        tile_prep(i0 + 1, xqp_h, (s_ + 1) * 128, xTP, (s_ + 1) % 2, [("mmv", NKT + s_)])
                    emit_qmm(s_)
                    ext_front(pV[s_ % 2], 0, False, LQT, s_ * 128)
                    ext_back(NKT + s_ - 1)
                ext_back(NKT + NSLOT - 1)

                if debug >= 3:
                    JL_D3 = P.val("jl")
                    P.emit("gpsimd", lambda g: g.dma_start(out=dbg["kk"], in_=kk_sb[:]),
                           waits=[("jl", JL_D3)], inc="dbgs", inc_n=16)
                    P.emit("gpsimd", lambda g: g.dma_start(out=dbg["lraw"], in_=Lraw[:]), inc="dbgs", inc_n=16)
                    P.emit("gpsimd", lambda g: g.dma_start(out=dbg["ssb"], in_=ssb[:]), inc="dbgs", inc_n=16)
                    P.emit("gpsimd", lambda g: g.dma_start(out=dbg["rnrm"], in_=rnrm[:]), inc="dbgs", inc_n=16)
                    P.emit("gpsimd", lambda g: g.dma_start(out=dbg["lsc"], in_=Lsc[:]), inc="dbgs", inc_n=16)
                if debug >= 1:
                    JL_D = P.val("jl")
                    P.emit("gpsimd", lambda g: g.dma_start(out=dbg["vs"], in_=Vs[:]),
                           waits=[("jl", JL_D)], inc="dbgs", inc_n=16)
                    P.emit("gpsimd", lambda g: g.dma_start(out=dbg["jlkt"], in_=JLKT[:]), inc="dbgs", inc_n=16)
                    P.emit("gpsimd", lambda g: g.dma_start(out=dbg["lqt"], in_=LQT[:]), inc="dbgs", inc_n=16)

                JL_ALL = P.val("jl")
                P.emit("sync", lambda sy: sy.sem_inc(sems["bar1"], 1), waits=[("x0", P.val("x0")), ("x1", P.val("x1"))])
                P.emit("tensor", lambda te: te.sem_inc(sems["bar1"], 1))
                P.emit("vector", lambda v: v.sem_inc(sems["bar1"], 1), waits=[("jl", JL_ALL)])
                P.emit("scalar", lambda s: s.sem_inc(sems["bar1"], 1), waits=[("rr", P.val("rr"))])
                P.emit("gpsimd", lambda g: g.sem_inc(sems["bar1"], 1),
                       waits=[("wA", W_SETUP), ("dbgs", P.val("dbgs"))])
                P.cnt["bar1"] = 5
                P.run_block(nc, sems)
                cnt1 = P.cnt

            if phases < 2:
                for c in sems_ctx:
                    c.__exit__(None, None, None)
                return nc
            with ExitStack() as _stB:
                sbB = lambda *a: _stB.enter_context(nc.sbuf_tensor(*a))
                psB = lambda *a: _stB.enter_context(nc.psum_tensor(*a))
                masks_g0 = sbB("masks_g0", [128, 4, 2, 128], BF16)
                attnT = sbB("attnT", [128, 8, 512], BF16)
                PT0 = sbB("PT0", [128, 512], BF16)
                PT1 = sbB("PT1", [128, 512], BF16)
                PT2 = sbB("PT2", [128, 512], BF16)
                PT3 = sbB("PT3", [128, 512], BF16)
                
                wo_sb = sbB("wo_sb", [128, 8, 1024], BF16)
                bob = sbB("bob", [128, D], F32)
                srow_b = sbB("srow_b", [1, 512], BF16)
                ones1b = sbB("ones1b", [1, 64], BF16)
                
                rbl_t = sbB("rbl", [64, 512], F32)
                rbr0 = sbB("rbr0", [64, 512], F32)
                rbr1 = sbB("rbr1", [64, 512], F32)
                xqr_t = sbB("xqr", [128, D], F32)
                y1a_t = sbB("y1a", [128, D], F32)
                y1b_t = sbB("y1b", [128, D], F32)
                xlo_t = sbB("xlo", [128, D], BF16)
                xfT_t = sbB("xfT_t", [128, 8, 128], BF16)
                xfTlo_t = sbB("xfTlo_t", [128, 8, 128], BF16)
                stq = sbB("stq", [128, 4], F32)
                rstd1 = sbB("rstd1", [128, 1], F32)
                rln1 = sbB("rln1", [128, 1], F32)
                e8_t = sbB("e8", [128, NE], F32)
                nmx_t = sbB("nmx", [128, 1], F32)
                s8_t = sbB("s8", [128, 1], F32)
                scr8_t = sbB("scr8", [128, NE], F32)
                st8_t = sbB("st8", [128, 8], F32)
                s2t_t = sbB("s2t", [128, NE], F32)
                mask8_t = sbB("mask8", [128, NE], F32)
                
                pS0 = psB("pS0", [128, 512], F32)
                pS1 = psB("pS1", [128, 512], F32)
                pS2 = psB("pS2", [128, 512], F32)
                pAV0 = psB("pAV0", [65, 512], F32)
                pAV1 = psB("pAV1", [65, 512], F32)
                
                pWO0 = psB("pWO0", [128, 1024], F32)
                
                pT2 = psB("pT2", [128, 8, 128], BF16)
                P = Prog(cnt1)
                PT = [PT0, PT1, PT2, PT3]
                pS, pAV = [pS0, pS1, pS2], [pAV0, pAV1]
                pWO = pWO0
                rbr = [rbr0, rbr1]
                masks_g = masks_g0

                P.emit("gpsimd", lambda g: g.dma_start(
                    out=wo_sb[:], in_=wo_h.rearrange("(dc p) n -> p dc n", p=128)), inc="wB", inc_n=16)
                P.emit("gpsimd", lambda g: g.dma_start(out=bob[:], in_=bob_h), inc="wB", inc_n=16)
                W_B = P.val("wB")
                P.emit("vector", lambda v: v.memset(ones1b[:], 1.0), waits=[("bar1", 5)])
                P.emit("vector", lambda v: v.drain(), inc="xt")
                ONES_RDY = P.val("xt")

                sbank_free = [None, None, None]
                ptslot_free = [None, None, None, None]
                avbank_free = [None, None]
                exp_serial = [0]
                wo_prev_g = 0

                def emit_score(g, h, kc):
                    gr, m = h % 4, h // 4
                    jlo = max(4 * g, kc // 2)
                    off = (jlo - 4 * g) * 128
                    sb = exp_serial[0] % 3
                    pslot = exp_serial[0] % 4
                    exp_serial[0] += 1
                    waits = []
                    if sbank_free[sb]:
                        waits.append(sbank_free[sb])
                    if h <= 1 and g == 0 and kc < 2:
                        waits.append(("jl", JL_ALL))
                    def scmm(te, sb=sb, off=off, gr=gr, m=m, kc=kc, g=g):
                        return te.matmul(
                            pS[sb][:, off:512],
                            JLKT[32 * gr:32 * gr + 6, m, kc * 128:(kc + 1) * 128],
                            LQT[32 * gr:32 * gr + 6, m, g * 512 + off:(g + 1) * 512],
                            start=True, stop=True, tile_position=(32 * gr, 0))
                    P.emit("tensor", scmm, waits=waits, inc="sc")
                    ncols = 512 - off
                    ew = [("sc", P.val("sc"))]
                    if ptslot_free[pslot]:
                        ew.append(ptslot_free[pslot])
                    P.emit("scalar", lambda s, sb=sb, off=off, pslot=pslot, ncols=ncols: s.activation(
                        PT[pslot][:, 0:ncols], pS[sb][:, off:512], AF.Exp, bias=0.0, scale=SCALE),
                        waits=ew, inc="exp")
                    sbank_free[sb] = ("exp", P.val("exp"))
                    masked = kc >= 8 * g
                    if masked:
                        jj, ii = kc // 2 - 4 * g, kc % 2
                        P.emit("vector", lambda v, pslot=pslot, jj=jj, ii=ii, g=g: v.tensor_mul(
                            PT[pslot][:, 0:128], PT[pslot][:, 0:128], masks_g[:, jj, ii, :]),
                            waits=[("exp", P.val("exp")), MSK_RDY[0]])
                        P.emit("vector", lambda v: v.drain(), inc="msk")
                    return (pslot, off, ncols, masked, P.val("exp"), P.val("msk"))

                def emit_av(g, h, kc, sinfo):
                    pslot, off, ncols, masked, expv, mskv = sinfo
                    hb = h % 2
                    K = kmax(g)
                    avw = [("msk", mskv)] if masked else [("exp", expv)]
                    if kc == 0 and avbank_free[hb]:
                        avw.append(avbank_free[hb])
                    def avmm(te, hb=hb, off=off, kc=kc, h=h, ncols=ncols, K=K, pslot=pslot):
                        return te.matmul(
                            pAV[hb][:, off:512], Vs[:, kc, h, :], PT[pslot][:, 0:ncols],
                            start=(kc == 0), stop=(kc == K - 1))
                    P.emit("tensor", avmm, waits=avw, inc="av")
                    ptslot_free[pslot] = ("av", P.val("av"))

                for g in range(NG):
                    av_counts, sr_vals, rbm_vals, rbr_vals = {}, {}, {}, {}
                    last_rbr = P.val("rbr")
                    P.emit("gpsimd", lambda gq, g=g: gq.dma_start(
                        out=masks_g[:], in_=masks_h[:, 4 * g:4 * g + 4, :, :]),
                        waits=[("bar1", 5), ("msk", P.cnt.get("_mskuse_%d" % (g - 1), 0))],
                        inc="mk0", inc_n=16)
                    MSK_RDY = [("mk0", P.val("mk0"))]
                    K = kmax(g)

                    def emit_chain_pre(hh):
                        hbb = hh % 2
                        srw = [("av", av_counts[hh])]
                        if rbm_vals:
                            srw.append(("rbm", max(rbm_vals.values())))
                        P.emit("vector", lambda v, hbb=hbb: v.tensor_copy(
                            srow_b[0:1, :], pAV[hbb][64:65, :]), waits=srw)
                        P.emit("vector", lambda v: v.drain(), inc="sr")
                        sr_vals[hh] = P.val("sr")
                        rw = [("sr", sr_vals[hh]), ("xt", ONES_RDY)]
                        prev_rbr = rbr_vals[hh - 1] if (hh - 1) in rbr_vals else last_rbr
                        if prev_rbr:
                            rw.append(("rbr", prev_rbr))
                        rw.append(("y1a", P.val("y1a")))
                        def rbmm(te):
                            return te.matmul(pWO[0:64, 512:1024], ones1b[:], srow_b[0:1, :], start=True, stop=True)
                        P.emit("tensor", rbmm, waits=rw, inc="rbm")
                        rbm_vals[hh] = P.val("rbm")

                    def emit_chain_act(hh):
                        hbb = hh % 2
                        P.emit("scalar", lambda s: s.activation(rbl_t[:], pWO[0:64, 512:1024], AF.Ln),
                               waits=[("rbm", rbm_vals[hh])])
                        P.emit("scalar", lambda s: s.drain())
                        P.emit("scalar", lambda s, hbb=hbb: s.activation(
                            rbr[hbb][:], rbl_t[:], AF.Exp, bias=0.0, scale=-1.0))
                        P.emit("scalar", lambda s: s.drain(), inc="rbr")
                        rbr_vals[hh] = P.val("rbr")

                    def emit_chain_post(hh):
                        hbb = hh % 2
                        cp, half = hh // 2, (hh % 2) * 64
                        P.emit("vector", lambda v, hbb=hbb, cp=cp, half=half: v.tensor_tensor(
                            attnT[half:half + 64, cp, :], pAV[hbb][0:64, :], rbr[hbb][:], ALU.mult),
                            waits=[("rbr", rbr_vals[hh]), ("wo", wo_prev_g)])
                        P.emit("vector", lambda v: v.drain(), inc="ate")
                        avbank_free[hbb] = ("ate", P.val("ate"))

                    def emit_chain(hh):
                        emit_chain_pre(hh)
                        emit_chain_act(hh)
                        emit_chain_post(hh)

                    for p in range(8):
                        a, b = 2 * p, 2 * p + 1
                        prev_a = prev_b = None
                        for kc in range(K):
                            sinfo_a = emit_score(g, a, kc)
                            sinfo_b = emit_score(g, b, kc)
                            if p >= 1:
                                if kc == 0:
                                    emit_chain_pre(2 * p - 2)
                                    emit_chain_act(2 * p - 2)
                                    emit_chain_pre(2 * p - 1)
                                    emit_chain_act(2 * p - 1)
                                elif kc == 1:
                                    emit_chain_post(2 * p - 2)
                                    emit_chain_post(2 * p - 1)
                            if kc >= 1:
                                emit_av(g, a, kc - 1, prev_a)
                                emit_av(g, b, kc - 1, prev_b)
                            prev_a, prev_b = sinfo_a, sinfo_b
                        emit_av(g, a, K - 1, prev_a)
                        emit_av(g, b, K - 1, prev_b)
                        av_counts[a] = P.val("av") - 1
                        av_counts[b] = P.val("av")
                    emit_chain(14)
                    emit_chain(14)
                    emit_chain(15)

                    ATE_G = P.val("ate")
                    for qq in range(4):
                        qt = 4 * g + qq
                        qb = qt % 2
                        P.emit("sync", lambda sy, qt=qt: sy.dma_start(
                            out=xqr_t[:], in_=xq_h[qt * 128:(qt + 1) * 128, :]),
                            waits=[("bar1", 5), ("y1a", P.val("y1a"))], inc="xqd", inc_n=16)
                        def womm(te, qq=qq):
                            ins = None
                            for cp in range(8):
                                for nb in range(2):
                                    ins = te.matmul(
                                        pWO[:, nb * 512:(nb + 1) * 512],
                                        attnT[:, cp, qq * 128:(qq + 1) * 128],
                                        wo_sb[:, cp, nb * 512:(nb + 1) * 512],
                                        start=(cp == 0), stop=(cp == 7))
                            return ins
                        wow = [("ate", ATE_G), ("wB", W_B), ("y1a", P.val("y1a"))]
                        P.emit("tensor", womm, waits=wow, inc="wo")
                        WO_T = P.val("wo")
                        P.emit("vector", lambda v: v.tensor_tensor(y1a_t[:], pWO[:], bob[:], ALU.add),
                               waits=[("wo", WO_T), ("xqd", P.val("xqd"))])
                        P.emit("vector", lambda v: v.drain())
                        P.emit("vector", lambda v: v.scalar_tensor_tensor(
                            y1b_t[:], y1a_t[:], 1.0, xqr_t[:], ALU.mult, ALU.add, accum_out=stq[:, 0:1]))
                        P.emit("vector", lambda v: v.drain(), inc="y1a")
                        Y1A = P.val("y1a")
                        P.emit("vector", lambda v: v.tensor_scalar(stq[:, 0:1], stq[:, 0:1], 1.0 / D, None, ALU.mult))
                        P.emit("vector", lambda v: v.drain())
                        P.emit("vector", lambda v: v.tensor_scalar(y1a_t[:], y1b_t[:], stq[:, 0:1], None, ALU.subtract))
                        P.emit("vector", lambda v: v.drain())
                        P.emit("vector", lambda v: v.scalar_tensor_tensor(
                            y1b_t[:], y1a_t[:], 1.0, y1a_t[:], ALU.mult, ALU.mult, accum_out=stq[:, 1:2]))
                        P.emit("vector", lambda v: v.drain(), inc="v1")
                        P.emit("scalar", lambda s: s.activation(rln1[:], stq[:, 1:2], AF.Ln,
                                                                bias=epsb[:, 0:1], scale=1.0 / D),
                               waits=[("v1", P.val("v1"))])
                        P.emit("scalar", lambda s: s.drain())
                        P.emit("scalar", lambda s: s.activation(rstd1[:], rln1[:], AF.Exp, bias=0.0, scale=-0.5))
                        P.emit("scalar", lambda s: s.drain(), inc="sd1")
                        P.emit("vector", lambda v: v.tensor_scalar(y1b_t[:], y1a_t[:], rstd1[:, 0:1], None, ALU.mult),
                               waits=[("sd1", P.val("sd1"))])
                        P.emit("vector", lambda v: v.drain())
                        P.emit("vector", lambda v, qt=qt: v.tensor_copy(acc_t[:, qt, :], y1b_t[:]))
                        P.emit("vector", lambda v: v.drain())
                        P.emit("vector", lambda v, qt=qt: v.tensor_tensor(xlo_t[:], y1b_t[:], acc_t[:, qt, :], ALU.subtract))
                        P.emit("vector", lambda v: v.drain(), inc="xf")
                        XF = P.val("xf")
                        def tph(te, qt=qt):
                            ins = None
                            for dc in range(8):
                                ins = te.transpose(pT2[:, dc, :], acc_t[:, qt, dc * 128:(dc + 1) * 128], ident[:])
                            return ins
                        P.emit("tensor", tph, waits=[("xf", XF), ("xft", P.val("xft"))], inc="tpf")
                        P.emit("vector", lambda v: v.tensor_copy(xfT_t[:], pT2[:]),
                               waits=[("tpf", P.val("tpf"))])
                        P.emit("vector", lambda v: v.drain(), inc="xft")
                        def tpl(te):
                            ins = None
                            for dc in range(8):
                                ins = te.transpose(pT2[:, dc, :], xlo_t[:, dc * 128:(dc + 1) * 128], ident[:])
                            return ins
                        P.emit("tensor", tpl, waits=[("xft", P.val("xft"))], inc="tpf")
                        P.emit("vector", lambda v: v.tensor_copy(xfTlo_t[:], pT2[:]),
                               waits=[("tpf", P.val("tpf"))])
                        P.emit("vector", lambda v: v.drain(), inc="xft")
                        XFT = P.val("xft")
                        rtw = [("xft", XFT)]
                        if sbank_free[qb]:
                            rtw.append(sbank_free[qb])
                        def rtmm(te, qb=qb):
                            ins = None
                            srcs = ((xfT_t, rtr_hi), (xfT_t, rtr_lo), (xfTlo_t, rtr_hi))
                            n = 0
                            for src, w in srcs:
                                for dc in range(8):
                                    ins = te.matmul(pS[qb][:, 0:8], src[:, dc, :], w[:, dc, :],
                                                    start=(n == 0), stop=(n == 23))
                                    n += 1
                            return ins
                        P.emit("tensor", rtmm, waits=rtw, inc="rt")
                        P.emit("vector", lambda v, qb=qb: v.reduce_max(
                            nmx_t[:], pS[qb][:, 0:8], axis=AX.X, negate=True),
                            waits=[("rt", P.val("rt"))])
                        P.emit("vector", lambda v: v.drain(), inc="rmax")
                        P.emit("scalar", lambda s, qb=qb: s.activation(
                            e8_t[:], pS[qb][:, 0:8], AF.Exp, bias=nmx_t[:, 0:1], scale=1.0),
                            waits=[("rmax", P.val("rmax"))])
                        P.emit("scalar", lambda s: s.drain(), inc="rexp")
                        sbank_free[qb] = ("rexp", P.val("rexp"))
                        P.emit("vector", lambda v: v.reduce_sum(s8_t[:], e8_t[:], axis=AX.X),
                               waits=[("rexp", P.val("rexp"))])
                        P.emit("vector", lambda v: v.drain())
                        P.emit("vector", lambda v: v.reciprocal(nmx_t[:], s8_t[:]))
                        P.emit("vector", lambda v: v.drain())
                        P.emit("vector", lambda v: v.tensor_scalar(scr8_t[:], e8_t[:], nmx_t[:, 0:1], None, ALU.mult))
                        P.emit("vector", lambda v: v.drain())
                        P.emit("vector", lambda v: v.max(st8_t[:], scr8_t[:]))
                        P.emit("vector", lambda v: v.drain())
                        P.emit("vector", lambda v: v.memset(st8_t[:, 2:8], -1.0e30))
                        P.emit("vector", lambda v: v.drain())
                        P.emit("vector", lambda v: v.match_replace(s2t_t[:], st8_t[:], scr8_t[:], -2.0e30))
                        P.emit("vector", lambda v: v.drain())
                        P.emit("vector", lambda v: v.tensor_single_scalar(mask8_t[:], s2t_t[:], -1.0e29, ALU.is_lt))
                        P.emit("vector", lambda v: v.drain())
                        P.emit("vector", lambda v: v.tensor_mul(mask8_t[:], mask8_t[:], scr8_t[:]))
                        P.emit("vector", lambda v: v.tensor_add(stq[:, 2:3], st8_t[:, 0:1], st8_t[:, 1:2]))
                        P.emit("vector", lambda v: v.drain())
                        P.emit("vector", lambda v: v.tensor_scalar(s2t_t[:, 0:1], stq[:, 2:3], 1.0, 1e-20, ALU.mult, ALU.add))
                        P.emit("vector", lambda v: v.drain())
                        P.emit("vector", lambda v: v.reciprocal(s8_t[:], s2t_t[:, 0:1]))
                        P.emit("vector", lambda v: v.drain())
                        P.emit("vector", lambda v, qt=qt: v.tensor_scalar(
                            comb_t[:, qt, :], mask8_t[:], s8_t[:, 0:1], None, ALU.mult))
                        P.emit("vector", lambda v: v.drain(), inc="cmb")

                    wo_prev_g = P.val("wo")
                    P.cnt["_mskuse_%d" % g] = P.val("msk")

                if debug >= 2:
                    P.emit("gpsimd", lambda g: g.dma_start(out=dbg["acc"], in_=acc_t[:]),
                           waits=[("xf", P.val("xf"))], inc="dbgs", inc_n=16)
                    P.emit("gpsimd", lambda g: g.dma_start(out=dbg["comb"], in_=comb_t[:]),
                           waits=[("cmb", P.val("cmb"))], inc="dbgs", inc_n=16)
                    P.emit("gpsimd", lambda g: g.dma_start(out=dbg["attnT"], in_=attnT[:]),
                           waits=[("ate", P.val("ate"))], inc="dbgs", inc_n=16)

                P.emit("sync", lambda sy: sy.sem_inc(sems["bar2"], 1), waits=[("xqd", P.val("xqd"))])
                P.emit("tensor", lambda te: te.sem_inc(sems["bar2"], 1))
                P.emit("vector", lambda v: v.sem_inc(sems["bar2"], 1), waits=[("cmb", P.val("cmb"))])
                P.emit("scalar", lambda s: s.sem_inc(sems["bar2"], 1))
                P.emit("gpsimd", lambda g: g.sem_inc(sems["bar2"], 1),
                       waits=[("wB", W_B), ("dbgs", P.val("dbgs"))])
                P.cnt["bar2"] = 5
                P.run_block(nc, sems)
                cnt2 = P.cnt

        if phases < 3:
            for c in sems_ctx:
                c.__exit__(None, None, None)
            return nc
        with ExitStack() as _stD:
            sbD = lambda *a: _stD.enter_context(nc.sbuf_tensor(*a))
            psD = lambda *a: _stD.enter_context(nc.psum_tensor(*a))
            xfT = sbD("xfT", [128, 8, NSLOT * 128], BF16)
            ht_sb = sbD("ht", [128, NHC, NSLOT * 128], BF16)
            wg_sb = sbD("wg_sb", [128, NHC * 8, 128], BF16)
            wu_sb = sbD("wu_sb", [128, NHC * 8, 128], BF16)
            wd_sb = sbD("wd_sb", [128, NHC, 1024], BF16)
            sG0 = sbD("sG0", [128, 512], BF16)
            sG1 = sbD("sG1", [128, 512], BF16)
            xc2 = sbD("xc2", [128, D], F32)
            sq2 = sbD("sq2", [128, D], F32)
            st2 = sbD("st2", [128, 2], F32)
            st3 = sbD("st3", [128, 1], F32)
            rstd2 = sbD("rstd2", [128, 1], F32)
            rln2 = sbD("rln2", [128, 1], F32)
            outst0 = sbD("outst0", [128, D], F32)
            outst1 = sbD("outst1", [128, D], F32)
            pG0 = psD("pG0", [128, 512], F32)
            pG1 = psD("pG1", [128, 512], F32)
            pU0 = psD("pU0", [128, 512], F32)
            pU1 = psD("pU1", [128, 512], F32)
            pD0 = psD("pD0", [128, 1024], F32)
            
            pT3 = psD("pT3", [128, 8, 128], BF16)
            P = Prog(cnt2)
            pG, pU = [pG0, pG1], [pU0, pU1]
            pD = pD0
            sG = [sG0, sG1]
            outst = [outst0, outst1]

            for t in range(NSLOT):
                codd = t % 2 == 1
                if codd:
                    def tpx(te, t=t):
                        ins = None
                        for dc in range(8):
                            ins = te.matmul(pD0[:, dc * 128:(dc + 1) * 128],
                                            acc_t[:, t, dc * 128:(dc + 1) * 128], ident[:],
                                            start=True, stop=True)
                        return ins
                else:
                    def tpx(te, t=t):
                        ins = None
                        for dc in range(8):
                            ins = te.transpose(pT3[:, dc, :], acc_t[:, t, dc * 128:(dc + 1) * 128], ident[:])
                        return ins
                P.emit("tensor", tpx, waits=[("bar2", 5), ("xft", max(0, P.val("xft") - 1))], inc="tpf")
                if codd:
                    def cpx(v, t=t):
                        for dc in range(8):
                            v.tensor_copy(xfT[:, dc, t * 128:(t + 1) * 128], pD0[:, dc * 128:(dc + 1) * 128])
                    P.emit("vector", cpx, waits=[("bar2", 5), ("tpf", P.val("tpf"))])
                else:
                    P.emit("vector", lambda v, t=t: v.tensor_copy(xfT[:, :, t * 128:(t + 1) * 128], pT3[:]),
                           waits=[("bar2", 5), ("tpf", P.val("tpf"))])
                P.emit("vector", lambda v: v.drain(), inc="xft")
            XFT_ALL = P.val("xft")

            for e in range(NE):
                for hc in range(NHC):
                    P.emit("gpsimd", lambda g, e=e, hc=hc: g.dma_start(
                        out=wg_sb[:, hc * 8:(hc + 1) * 8, :],
                        in_=wg_h[e][:, hc * 128:(hc + 1) * 128].rearrange("(dc p) q -> p dc q", p=128)),
                        waits=[("bar2", 5), ("gmm", 44 * (e - 1) + 4 * (hc + 1) if e else 0)],
                        inc="wg", inc_n=16)
                    P.emit("gpsimd", lambda g, e=e, hc=hc: g.dma_start(
                        out=wu_sb[:, hc * 8:(hc + 1) * 8, :],
                        in_=wu_h[e][:, hc * 128:(hc + 1) * 128].rearrange("(dc p) q -> p dc q", p=128)),
                        waits=[("umm", 44 * (e - 1) + 4 * (hc + 1) if e else 0)],
                        inc="wu", inc_n=16)
                P.emit("gpsimd", lambda g, e=e: g.dma_start(
                    out=wd_sb[:], in_=wd_h[e].rearrange("(hc p) n -> p hc n", p=128)),
                    waits=[("dmm", 16 * e)], inc="wd", inc_n=16)

            for e in range(NE):
                for hc in range(NHC):
                    for tc in range(4):
                        gi = (hc * 4 + tc) % 2
                        def gmm(te, hc=hc, tc=tc, gi=gi, e=e):
                            ins = None
                            for dc in range(8):
                                ins = te.matmul(pG[gi][:], wg_sb[:, hc * 8 + dc, :],
                                                xfT[:, dc, tc * 512:(tc + 1) * 512],
                                                start=(dc == 0), stop=(dc == 7))
                            return ins
                        gwaits = [("wg", 16 * NHC * (e + 1)), ("xft", XFT_ALL),
                                  ("sg", P.val("sg") - 1)]
                        if e and hc == 0 and tc == 0:
                            gwaits.append(("acc", 16 * e - 1))
                        P.emit("tensor", gmm, waits=gwaits, inc="gmm")
                        def umm(te, hc=hc, tc=tc, gi=gi, e=e):
                            ins = None
                            for dc in range(8):
                                ins = te.matmul(pU[gi][:], wu_sb[:, hc * 8 + dc, :],
                                                xfT[:, dc, tc * 512:(tc + 1) * 512],
                                                start=(dc == 0), stop=(dc == 7))
                            return ins
                        P.emit("tensor", umm,
                               waits=[("wu", 16 * NHC * (e + 1)), ("hm", P.val("hm") - 1)], inc="umm")
                        P.emit("scalar", lambda s, gi=gi: s.activation(sG[gi][:], pG[gi][:], AF.Silu),
                               waits=[("gmm", P.val("gmm")), ("hm", P.val("hm") - 1)])
                        P.emit("scalar", lambda s: s.drain(), inc="sg")
                        P.emit("vector", lambda v, gi=gi, hc=hc, tc=tc: v.tensor_tensor(
                            ht_sb[:, hc, tc * 512:(tc + 1) * 512], sG[gi][:], pU[gi][:], ALU.mult),
                            waits=[("sg", P.val("sg")), ("umm", P.val("umm"))])
                        P.emit("vector", lambda v: v.drain(), inc="hm")
                HM_E = P.val("hm")
                for t in range(NSLOT):
                    odd = t % 2 == 0
                    if odd:
                        def dmm(te, t=t):
                            ins = None
                            for hc in range(NHC):
                                ins = te.matmul(pG0[:], ht_sb[:, hc, t * 128:(t + 1) * 128],
                                                wd_sb[:, hc, 0:512],
                                                start=(hc == 0), stop=(hc == NHC - 1))
                                ins = te.matmul(pG1[:], ht_sb[:, hc, t * 128:(t + 1) * 128],
                                                wd_sb[:, hc, 512:1024],
                                                start=(hc == 0), stop=(hc == NHC - 1))
                            return ins
                    else:
                        def dmm(te, t=t):
                            ins = None
                            for hc in range(NHC):
                                for nb in range(2):
                                    ins = te.matmul(pD[:, nb * 512:(nb + 1) * 512],
                                                    ht_sb[:, hc, t * 128:(t + 1) * 128],
                                                    wd_sb[:, hc, nb * 512:(nb + 1) * 512],
                                                    start=(hc == 0), stop=(hc == NHC - 1))
                            return ins
                    P.emit("tensor", dmm,
                           waits=[("wd", 16 * (e + 1)), ("hm", HM_E),
                                  ("acc", max(0, P.val("acc") - 1))],
                           inc="dmm")
                    if odd:
                        if e == NE - 1:
                            P.emit("vector", lambda v, e=e, t=t: v.scalar_tensor_tensor(
                                acc_t[:, t, 0:512], pG0[:], comb_t[:, t, e:e + 1], acc_t[:, t, 0:512],
                                ALU.mult, ALU.add, accum_out=st2[:, 0:1]),
                                waits=[("dmm", P.val("dmm"))])
                            P.emit("vector", lambda v, e=e, t=t: v.scalar_tensor_tensor(
                                acc_t[:, t, 512:1024], pG1[:], comb_t[:, t, e:e + 1], acc_t[:, t, 512:1024],
                                ALU.mult, ALU.add, accum_out=st3[:, 0:1]))
                            P.emit("vector", lambda v: v.drain())
                            P.emit("vector", lambda v: v.tensor_add(st2[:, 0:1], st2[:, 0:1], st3[:, 0:1]))
                        else:
                            P.emit("vector", lambda v, e=e, t=t: v.scalar_tensor_tensor(
                                acc_t[:, t, 0:512], pG0[:], comb_t[:, t, e:e + 1], acc_t[:, t, 0:512],
                                ALU.mult, ALU.add),
                                waits=[("dmm", P.val("dmm"))])
                            P.emit("vector", lambda v, e=e, t=t: v.scalar_tensor_tensor(
                                acc_t[:, t, 512:1024], pG1[:], comb_t[:, t, e:e + 1], acc_t[:, t, 512:1024],
                                ALU.mult, ALU.add))
                    elif e == NE - 1:
                        P.emit("vector", lambda v, e=e, t=t: v.scalar_tensor_tensor(
                            acc_t[:, t, :], pD[:], comb_t[:, t, e:e + 1], acc_t[:, t, :],
                            ALU.mult, ALU.add, accum_out=st2[:, 0:1]),
                            waits=[("dmm", P.val("dmm"))])
                    else:
                        P.emit("vector", lambda v, e=e, t=t: v.scalar_tensor_tensor(
                            acc_t[:, t, :], pD[:], comb_t[:, t, e:e + 1], acc_t[:, t, :],
                            ALU.mult, ALU.add),
                            waits=[("dmm", P.val("dmm"))])
                    P.emit("vector", lambda v: v.drain(), inc="acc")
                    if e == NE - 1:
                        ob = t % 2
                        P.emit("vector", lambda v: v.tensor_scalar(st2[:, 0:1], st2[:, 0:1], 1.0 / D, None, ALU.mult),
                               waits=[("acc", P.val("acc"))])
                        P.emit("vector", lambda v: v.drain())
                        P.emit("vector", lambda v, t=t: v.tensor_scalar(xc2[:], acc_t[:, t, :], st2[:, 0:1], None, ALU.subtract))
                        P.emit("vector", lambda v: v.drain())
                        P.emit("vector", lambda v: v.scalar_tensor_tensor(
                            sq2[:], xc2[:], 1.0, xc2[:], ALU.mult, ALU.mult, accum_out=st2[:, 1:2]))
                        P.emit("vector", lambda v: v.drain(), inc="ln2")
                        P.emit("scalar", lambda s: s.activation(rln2[:], st2[:, 1:2], AF.Ln,
                                                                bias=epsb[:, 0:1], scale=1.0 / D),
                               waits=[("ln2", P.val("ln2"))])
                        P.emit("scalar", lambda s: s.drain())
                        P.emit("scalar", lambda s: s.activation(rstd2[:], rln2[:], AF.Exp, bias=0.0, scale=-0.5))
                        P.emit("scalar", lambda s: s.drain(), inc="sd2")
                        P.emit("vector", lambda v, ob=ob: v.tensor_scalar(
                            outst[ob][:], xc2[:], rstd2[:, 0:1], None, ALU.mult),
                            waits=[("sd2", P.val("sd2")), ("out", 16 * t)])
                        P.emit("vector", lambda v: v.drain(), inc="eo")
                        P.emit("sync", lambda sy, t=t, ob=ob: sy.dma_start(
                            out=out_h[t * 128:(t + 1) * 128, :], in_=outst[ob][:]),
                            waits=[("eo", P.val("eo"))], inc="out", inc_n=16)

            P.emit("sync", lambda sy: sy.nop(), waits=[("out", 16 * NSLOT)])
            P.run_block(nc, sems)

        for c in sems_ctx:
            c.__exit__(None, None, None)
    return nc



def _make_masks(tiles):
    m = np.zeros((128, NSLOT, 2, 128), np.float32)
    tri = (np.arange(128)[:, None] <= np.arange(128)[None, :]).astype(np.float32)
    for j, t in enumerate(tiles):
        c = t + 1
        for i in range(2):
            kc = 2 * j + i
            if kc < c - 1:
                m[:, j, i, :] = 1.0
            elif kc == c - 1:
                m[:, j, i, :] = tri
    return np.ascontiguousarray(m.astype(ml_dtypes.bfloat16))


def kernel(L0_attn_x, L0_attn_W1_q, L0_attn_W2_q, L0_attn_W1_k, L0_attn_W2_k,
           L0_attn_W_v, L0_attn_b_v, L0_attn_W_o, L0_attn_b_o,
           L0_ffn_w_router, L0_ffn_w_gate, L0_ffn_w_up, L0_ffn_w_down,
           _debug=0):
    x = np.ascontiguousarray(np.asarray(L0_attn_x, dtype=np.float32))
    arrs = {
        "W1_q": L0_attn_W1_q, "W2_q": L0_attn_W2_q, "W1_k": L0_attn_W1_k,
        "W2_k": L0_attn_W2_k, "W_v": L0_attn_W_v, "W_o": L0_attn_W_o,
        "w_router": L0_ffn_w_router, "w_gate": L0_ffn_w_gate,
        "w_up": L0_ffn_w_up, "w_down": L0_ffn_w_down,
    }
    arrs = {k: np.ascontiguousarray(np.asarray(v, dtype=np.float32)) for k, v in arrs.items()}
    bv = np.asarray(L0_attn_b_v, dtype=np.float32)
    bo = np.asarray(L0_attn_b_o, dtype=np.float32)
    arrs["b_v_b"] = np.ascontiguousarray(np.broadcast_to(bv, (128, D)))
    arrs["b_o_b"] = np.ascontiguousarray(np.broadcast_to(bo, (128, D)))
    arrs["ident_hbm"] = np.eye(128, dtype=np.float32)
    masks = {0: _make_masks(TILES_A), 1: _make_masks(TILES_B)}

    in_maps = []
    for c in range(8):
        b, v_ = c // 2, c % 2
        tiles = TILES_A if v_ == 0 else TILES_B
        rows = np.concatenate([np.arange(t * 128, (t + 1) * 128) for t in tiles])
        xb = x[b]
        prev = rows - 1
        xqp = np.where((prev >= 0)[:, None], xb[np.maximum(prev, 0)], np.float32(0.0))
        m = dict(arrs)
        m["x_kv"] = xb
        m["xq"] = np.ascontiguousarray(xb[rows])
        m["xq_prev"] = np.ascontiguousarray(xqp.astype(np.float32))
        m["masks"] = masks[v_]
        in_maps.append(m)

    nc = build_program(debug=_debug, phases=_PHASES)
    res = run_bass_kernel_spmd(nc, in_maps, core_ids=list(range(8)))

    out = np.zeros((B, T, D), np.float32)
    for c in range(8):
        b, v_ = c // 2, c % 2
        tiles = TILES_A if v_ == 0 else TILES_B
        oq = np.asarray(res.results[c]["out_q"])
        for j, t in enumerate(tiles):
            out[b, t * 128:(t + 1) * 128, :] = oq[j * 128:(j + 1) * 128, :]
    if _debug:
        return out, res
    return out
